# version 04
# baseline (speedup 1.0000x reference)
import numpy as np
import ml_dtypes
import concourse.bass as bass
import concourse.mybir as mybir
from concourse.bass_utils import run_bass_kernel_spmd

F32 = mybir.dt.float32
U32 = mybir.dt.uint32
BF16 = mybir.dt.bfloat16
AF = mybir.ActivationFunctionType
ALU = mybir.AluOpType
AX = mybir.AxisListType.X

T = 2048
C = 8
D = 1024
TT = 512
NTT = T // TT
NEG = -1e30
EPS_LN = 1e-5
N_CORES = 8
PAIRS = [[0, 1], [2, 3], [4, 5], [6, 7]]

ENGINES = ("tensor", "vector", "scalar", "gpsimd", "sync")
RACE_DETECT = True


def _mk_nc():
    return bass.Bass("TRN2", target_bir_lowering=False,
                     detect_race_conditions=RACE_DETECT)


class B:

    def __init__(self, nc):
        self.nc = nc
        self.ops = {e: [] for e in ENGINES}
        self.marks = {}
        self.sems = {}
        self._nded = 0

    def _semkey(self, key):
        if key not in self.sems:
            self.sems[key] = self.nc.alloc_semaphore(f"pg_{key[0]}_{key[1]}")
            self.marks[key] = 0
        return key

    def op(self, eng, fn, waits=(), mark=False, dma=False, coll=False):
        waits = [w for w in waits if w is not None]
        rec = dict(fn=fn, waits=waits, markval=None, inc=0, key=None)
        self.ops[eng].append(rec)
        if coll:
            key = self._semkey(("x", self._nded))
            self._nded += 1
            rec["inc"] = 1
            self.marks[key] += 1
            rec["markval"] = self.marks[key]
            rec["key"] = key
            return (key, self.marks[key])
        if mark or dma:
            key = self._semkey((eng, "d" if dma else "c"))
            rec["inc"] = 16 if dma else 1
            self.marks[key] += rec["inc"]
            rec["markval"] = self.marks[key]
            rec["key"] = key
            return (key, self.marks[key])
        return None

    def barrier_token(self, eng, kind="c"):
        key = (eng, kind)
        if key not in self.marks or self.marks[key] == 0:
            return None
        return (key, self.marks[key])

    def wait(self, eng, tokens):
        toks = [t for t in tokens if t is not None]
        if toks:
            self.ops[eng].append(dict(fn=None, waits=toks, markval=None, inc=0, key=None))

    def replay(self):
        with self.nc.Block() as blk:
            for ename in ENGINES:
                if not self.ops[ename]:
                    continue

                def make(ename):
                    def body(eng):
                        for rec in self.ops[ename]:
                            for (wk, wv) in rec["waits"]:
                                if rec["fn"] is not None and wk == (ename, "c"):
                                    continue
                                eng.wait_ge(self.sems[wk], wv)
                            if rec["fn"] is None:
                                continue
                            ins = rec["fn"](eng)
                            if rec["markval"] is not None:
                                ins.then_inc(self.sems[rec["key"]], rec["inc"])
                    return body

                getattr(blk, ename)(make(ename))


class Ctx:
    def __init__(self, nc):
        self.nc = nc
        self.b = B(nc)
        self.pb = [nc.alloc_psum_tensor(f"bank{i}", [128, 512], F32).ap() for i in range(8)]
        self._consts_done = None

    def sb(self, name, shape, dtype):
        if getattr(self, "view_map", None) is not None and name in self.view_map:
            return self.view_map[name]
        if getattr(self, "overlay", None) is not None:
            nbytes = 1
            for d_ in shape[1:]:
                nbytes *= d_
            nbytes *= (4 if dtype == F32 else (2 if dtype == BF16 else 4))
            off = (self.overlay_cur + 63) // 64 * 64
            assert off + nbytes <= self.overlay_end, f"overlay overflow for {name}"
            self.overlay_cur = off + nbytes
            npart = shape[0]
            view = self.overlay[0:npart, off // 2:(off + nbytes) // 2]
            if dtype == F32:
                view = view.bitcast(F32)
            elif dtype != BF16:
                view = view.bitcast(dtype)
            if len(shape) > 2:
                raise RuntimeError("overlay only supports 2-D")
            return view
        return self.nc.alloc_sbuf_tensor(name, shape, dtype).ap()

    def begin_overlay(self, arena_ap, view_map=None, end_bytes=None):
        self.overlay = arena_ap
        self.overlay_cur = 0
        self.overlay_end = end_bytes if end_bytes is not None else arena_ap.shape[1] * 2
        self.view_map = view_map or {}

    def end_overlay(self):
        self.overlay = None
        self.view_map = None

    def emit_consts(self, alloc_lnf32=True, alloc_ln=True):
        b = self.b
        self.ones128_32 = self.sb("ones128_32", [128, 1], F32)
        self.ones128_16 = self.sb("ones128_16", [128, 1], BF16)
        self.ones1_32 = self.sb("ones1_32", [1, 128], F32)
        self.ones1_16 = self.sb("ones1_16", [1, 128], BF16)
        self.ident32 = self.sb("ident32", [128, 128], F32)
        self.ident16 = self.sb("ident16", [128, 128], BF16)
        self.tri16 = self.sb("tri16", [32, 32], BF16)
        self.tri32 = self.sb("tri32", [32, 32], F32)
        self.indsel16 = self.sb("indsel16", [128, 2], BF16)
        b.op("gpsimd", lambda e: e.memset(self.ones128_32, 1.0))
        b.op("gpsimd", lambda e: e.memset(self.ones128_16, 1.0))
        b.op("gpsimd", lambda e: e.memset(self.ones1_32, 1.0))
        b.op("gpsimd", lambda e: e.memset(self.ones1_16, 1.0))
        b.op("gpsimd", lambda e: e.memset(self.ident32, 1.0))
        b.op("gpsimd", lambda e: e.affine_select(self.ident32, self.ident32, [[1, 128]],
                                                 ALU.is_equal, 0.0, base=0, channel_multiplier=-1))
        b.op("gpsimd", lambda e: e.memset(self.ident16, 1.0))
        b.op("gpsimd", lambda e: e.affine_select(self.ident16, self.ident16, [[1, 128]],
                                                 ALU.is_equal, 0.0, base=0, channel_multiplier=-1))
        b.op("gpsimd", lambda e: e.memset(self.tri16, 1.0))
        b.op("gpsimd", lambda e: e.affine_select(self.tri16, self.tri16, [[1, 32]],
                                                 ALU.is_gt, 0.0, base=0, channel_multiplier=-1))
        b.op("gpsimd", lambda e: e.memset(self.tri32, 1.0))
        b.op("gpsimd", lambda e: e.affine_select(self.tri32, self.tri32, [[1, 32]],
                                                 ALU.is_gt, 0.0, base=0, channel_multiplier=-1))
        b.op("gpsimd", lambda e: e.memset(self.indsel16, 1.0))
        b.op("gpsimd", lambda e: e.affine_select(self.indsel16, self.indsel16, [[-64, 2]],
                                                 ALU.is_ge, 0.0, base=0, channel_multiplier=1))
        t = b.op("gpsimd", lambda e: e.affine_select(self.indsel16, self.indsel16, [[64, 2]],
                                                     ALU.is_ge, 0.0, base=63,
                                                     channel_multiplier=-1), mark=True)
        if alloc_ln:
            self.ln_b16 = self.sb("ln_b16", [128, 2 * 512], BF16)
            self.ln_sq16 = self.sb("ln_sq16", [128, 2 * 512], BF16)
            self.ln_small = self.sb("ln_small", [1, 512 * 2], F32)
            if alloc_lnf32:
                self.ln_f32 = self.sb("ln_f32", [128, 512], F32)
        self._consts_done = t
        return t


def mm_group(b, out_ap, pairs, waits=(), mark=True):
    n = len(pairs)
    tok = None
    for i, (lh, rh) in enumerate(pairs):
        def f(e, lh=lh, rh=rh, s=(i == 0), p=(i == n - 1), o=out_ap):
            return e.matmul(o, lh, rh, start=s, stop=p)
        tok = b.op("tensor", f, waits=(waits if i == 0 else ()), mark=(mark and i == n - 1))
    return tok


def emit_inv(ctx, out_ap, in_ap, waits=()):
    b = ctx.b
    t1 = b.op("scalar", lambda e, o=out_ap, i=in_ap: e.activation(o, i, AF.Ln),
              waits=list(waits), mark=True)
    t2 = b.op("vector", lambda e, o=out_ap: e.tensor_scalar(o, o, -1.0, None, ALU.mult),
              waits=[t1], mark=True)
    t3 = b.op("scalar", lambda e, o=out_ap: e.activation(o, o, AF.Exp),
              waits=[t2], mark=True)
    return t3


def emit_ln(ctx, x_c, Ttot, y32_c=None, y16_c=None, g_col=None, b_col=None,
            extra_wait=()):
    b = ctx.b
    nt = (Ttot + TT - 1) // TT
    S = ctx.pb[6][0:1, :]
    Q = ctx.pb[7][0:1, :]
    sm = ctx.ln_small
    y32_tok = y16_tok = None
    prev_small_tok = None
    prev_apply_tok = None
    prev_stats_tok = None
    for it in range(nt):
        t0 = it * TT
        t1 = min(Ttot, t0 + TT)
        w = t1 - t0
        qtoks = {}
        stats_tok = None
        for c in range(C):
            xa = x_c(c, t0, t1)
            sl = c % 2
            dst16 = ctx.ln_b16[:, sl * 512:sl * 512 + w]
            dsq = ctx.ln_sq16[:, sl * 512:sl * 512 + w]
            wts = list(extra_wait) if (c == 0 and it == 0) else []
            wts += [qtoks.get(c - 2)] if c >= 2 else ([prev_stats_tok] if prev_stats_tok else [])
            tg = b.op("gpsimd", lambda e, o=dst16, i=xa: e.tensor_copy(o, i),
                      waits=wts, mark=True)
            ts = b.op("scalar", lambda e, o=dsq, i=xa: e.activation(o, i, AF.Square),
                      waits=wts, mark=True)
            wmm = [tg, ts] + ([prev_small_tok] if (c == 0 and prev_small_tok) else [])
            b.op("tensor", lambda e, o=S[:, 0:w], l=ctx.ones128_16, r=dst16,
                 s=(c == 0), p=(c == C - 1): e.matmul(o, l, r, start=s, stop=p), waits=wmm)
            stats_tok = b.op("tensor", lambda e, o=Q[:, 0:w], l=ctx.ones128_16, r=dsq,
                             s=(c == 0), p=(c == C - 1): e.matmul(o, l, r, start=s, stop=p),
                             mark=True)
            qtoks[c] = stats_tok
        prev_stats_tok = stats_tok
        arow, brow = sm[:, 0:w], sm[:, 512:512 + w]
        mrow = S[:, 0:w]
        tvm = b.op("vector", lambda e, o=mrow: e.tensor_scalar(o, o, 1.0 / 1024, None, ALU.mult),
                   waits=[stats_tok], mark=True)
        tsq = b.op("scalar", lambda e, o=arow, i=mrow: e.activation(o, i, AF.Square),
                   waits=[tvm], mark=True)
        b.op("vector", lambda e, o=Q[:, 0:w], j=arow:
             e.scalar_tensor_tensor(o, o, 1.0 / 1024, j, ALU.mult, ALU.subtract),
             waits=[tsq])
        tv = b.op("vector", lambda e, o=Q[:, 0:w]:
                  e.tensor_scalar(o, o, EPS_LN, None, ALU.add), mark=True)
        tln = b.op("scalar", lambda e, o=arow, i=Q[:, 0:w]:
                   e.activation(o, i, AF.Ln), waits=[tv], mark=True)
        tv2 = b.op("vector", lambda e, o=arow, i=arow:
                   e.tensor_scalar(o, i, -0.5, None, ALU.mult), waits=[tln], mark=True)
        texp = b.op("scalar", lambda e, o=arow, i=arow:
                    e.activation(o, i, AF.Exp), waits=[tv2], mark=True)
        small_tok = b.op("vector", lambda e, o=brow, i=mrow, j=arow:
                         e.scalar_tensor_tensor(o, i, -1.0, j, ALU.mult, ALU.mult),
                         waits=[texp], mark=True)
        prev_small_tok = small_tok
        A_b = ctx.pb[4][:, 0:w]
        B_b = ctx.pb[5][:, 0:w]
        wb = [small_tok] + ([prev_apply_tok] if prev_apply_tok else [])
        b.op("tensor", lambda e, o=A_b, r=arow: e.matmul(o, ctx.ones1_32, r, start=True, stop=True),
             waits=wb)
        bcast_tok = b.op("tensor", lambda e, o=B_b, r=brow:
                         e.matmul(o, ctx.ones1_32, r, start=True, stop=True), mark=True)
        prev_cast16 = None
        for c in range(C):
            xa = x_c(c, t0, t1)
            scratch = y32_c is None
            dst = ctx.ln_f32[:, 0:w] if scratch else y32_c(c, t0, t1)
            wl = ([bcast_tok] if c == 0 else []) + ([prev_cast16] if (scratch and prev_cast16) else [])
            b.op("vector", lambda e, o=dst, i=xa, a=A_b: e.tensor_tensor(o, i, a, ALU.mult),
                 waits=wl)
            t32 = b.op("vector", lambda e, o=dst, bb=B_b: e.tensor_tensor(o, o, bb, ALU.add),
                       mark=True)
            if y16_c is not None:
                d16 = y16_c(c, t0, t1)
                if g_col is not None:
                    y16_tok = b.op("vector", lambda e, o=d16, i=dst, g=g_col(c), bc=b_col(c):
                                   e.tensor_scalar(o, i, g, bc, ALU.mult, ALU.add),
                                   mark=True)
                    t32 = y16_tok
                    prev_cast16 = y16_tok
                else:
                    y16_tok = b.op("gpsimd", lambda e, o=d16, i=dst: e.tensor_copy(o, i),
                                   waits=[t32], mark=True)
                    prev_cast16 = y16_tok
        prev_apply_tok = t32
        y32_tok = t32
    return y32_tok, y16_tok


def emit_tm_to_fm(ctx, src_tm_dram, dst_fm_c, Ttot, banks=(0, 1, 2, 3)):
    b = ctx.b
    if not hasattr(ctx, "tm_stage"):
        ctx.tm_stage = ctx.sb("tm_stage", [128, 2 * 1024], F32)
    ntile = Ttot // 128
    evac_toks = {}
    tr_toks = {}
    last = None
    for i in range(ntile):
        slot = i % 2
        stg = ctx.tm_stage[:, slot * 1024:(slot + 1) * 1024]
        wl = [tr_toks.get(slot), ctx._consts_done] if i < 2 else [tr_toks.get(slot)]
        dtok = b.op("sync", lambda e, o=stg, s=src_tm_dram[i * 128:(i + 1) * 128, :]:
                    e.dma_start(out=o, in_=s), waits=wl, dma=True)
        for c in range(C):
            bk = banks[(i * C + c) % len(banks)]
            pt = ctx.pb[bk][:, 0:128]
            wl2 = [dtok] + ([evac_toks.get(bk)] if evac_toks.get(bk) else [])
            ttok = b.op("tensor", lambda e, o=pt, s=stg[:, c * 128:(c + 1) * 128]:
                        e.transpose(o, s, ctx.ident32), waits=wl2, mark=True)
            last = b.op("vector", lambda e, o=dst_fm_c(c, i * 128, (i + 1) * 128), s=pt:
                        e.tensor_copy(o, s), waits=[ttok], mark=True)
            evac_toks[bk] = last
            tr_toks[slot] = ttok
    return last


def emit_fm_to_tm_store(ctx, src_fm_c, dst_tm_dram, Ttot, waits=(), banks=(0, 1, 2, 3)):
    b = ctx.b
    if not hasattr(ctx, "tm_stage"):
        ctx.tm_stage = ctx.sb("tm_stage", [128, 2 * 1024], F32)
    ntile = Ttot // 128
    evac_toks = {}
    dma_toks = {}
    last = None
    for i in range(ntile):
        slot = i % 2
        stg = ctx.tm_stage[:, slot * 1024:(slot + 1) * 1024]
        for c in range(C):
            bk = banks[(i * C + c) % len(banks)]
            pt = ctx.pb[bk][:, 0:128]
            wl = list(waits) if (i == 0 and c == 0) else []
            if evac_toks.get(bk):
                wl.append(evac_toks[bk])
            ttok = b.op("tensor", lambda e, o=pt, s=src_fm_c(c, i * 128, (i + 1) * 128):
                        e.transpose(o, s, ctx.ident32), waits=wl, mark=True)
            wl2 = [ttok] + ([dma_toks.get(slot)] if dma_toks.get(slot) else [])
            etok = b.op("vector", lambda e, o=stg[:, c * 128:(c + 1) * 128], s=pt:
                        e.tensor_copy(o, s), waits=wl2, mark=True)
            evac_toks[bk] = etok
        last = b.op("sync", lambda e, o=dst_tm_dram[i * 128:(i + 1) * 128, :], s=stg:
                    e.dma_start(out=o, in_=s), waits=[etok], dma=True)
        dma_toks[slot] = last
    return last


def fm_relay_in(ctx, buf_flat, relay_dram, dtype_note="", waits=(), eng="sync", Tr=T):
    b = ctx.b
    o = buf_flat.rearrange("p (c t) -> p c t", c=C)
    i = relay_dram.ap().rearrange("c p t -> p c t")
    return b.op(eng, lambda e: e.dma_start(out=o, in_=i), waits=waits, dma=True)


def fm_relay_out(ctx, buf_flat, relay_dram, waits=(), eng="sync", Tr=T):
    b = ctx.b
    i = buf_flat.rearrange("p (c t) -> p c t", c=C)
    o = relay_dram.ap().rearrange("c p t -> p c t")
    return b.op(eng, lambda e: e.dma_start(out=o, in_=i), waits=waits, dma=True)


def fm_view(buf, Tbuf=T):
    def v(c, t0, t1):
        return buf[:, c * Tbuf + t0:c * Tbuf + t1]
    return v


SC_LOC = float(128.0 ** -0.5)


class NS:
    pass


def emit_inner_attn_common(ctx, P, phase1_only=False, stop_after=None):
    b = ctx.b
    nc = ctx.nc
    lv = P.lv
    bo_c = lambda c: lv[:, c:c + 1]
    l1g_c = lambda c: lv[:, 8 + c:9 + c]
    l1b_c = lambda c: lv[:, 16 + c:17 + c]
    b1_c = lambda j: lv[:, 24 + j:25 + j]
    b2_c = lambda c: lv[:, 56 + c:57 + c]
    l2g_c = lambda c: lv[:, 64 + c:65 + c]
    l2b_c = lambda c: lv[:, 72 + c:73 + c]
    zv = fm_view(P.z)
    xnv = fm_view(P.xn)

    _, xn_tok = emit_ln(ctx, zv, T, y16_c=xnv, g_col=l1g_c, b_col=l1b_c,
                        extra_wait=[P.z_wait, P.lv_tok])
    if stop_after == "ln1":
        return xn_tok

    hxn_tok = None
    if not phase1_only:
        hv = fm_view(P.halo_sb, Tbuf=128)
        hxv = fm_view(P.haloxn, Tbuf=128)
        _, hxn_tok = emit_ln(ctx, hv, 128, y16_c=hxv, g_col=l1g_c, b_col=l1b_c,
                             extra_wait=[P.halo_tok, P.lv_tok, xn_tok])

    wp = getattr(P, "wslot_prev", None)
    wslot_tok = [wp, wp]
    wload_tok = [None, None]
    P.wload = wload_tok

    def load_head_w(h, slot, extra=()):
        wl = [wslot_tok[slot]] + list(extra)
        t1 = b.op("gpsimd", lambda e, o=P.wq_sb[slot], i=P.wq.ap()[h]:
                  e.dma_start(out=o, in_=i), waits=wl, dma=True)
        t2 = b.op("gpsimd", lambda e, o=P.wk_sb[slot], i=P.wk.ap()[h]:
                  e.dma_start(out=o, in_=i), dma=True)
        t3 = b.op("gpsimd", lambda e, o=P.wv_sb[slot], i=P.wv.ap()[h]:
                  e.dma_start(out=o, in_=i), dma=True)
        wload_tok[slot] = t3
        return t3

    load_head_w(4, 0)
    ktm = P.lin_ktm
    vtm = P.lin_vtm
    cg_sb = P.cg_sb
    ksum_sb = P.ksum_sb
    ts_sb = P.totsend_sb
    sc1 = P.sc1
    ktot_toks = []
    for hh in range(4):
        h = 4 + hh
        slot = hh % 2
        if hh < 3:
            load_head_w(h + 1, (hh + 1) % 2)
        wsb_k, wsb_v = P.wk_sb[slot], P.wv_sb[slot]
        wtok = wload_tok[slot]
        last_elu = None
        last_v = None
        bank_tok = {}
        for g in range(16):
            bk = (g % 2)
            pk = ctx.pb[4 + bk]
            pvv = ctx.pb[6 + bk]
            wl = [xn_tok, wtok, bank_tok.get(4 + bk)] + \
                 ([hxn_tok] if (hh == 0 and g == 0) else [])
            mm_group(b, pk[:, 0:128],
                     [(xnv(kc, g * 128, (g + 1) * 128), wsb_k[:, kc * 128:(kc + 1) * 128])
                      for kc in range(8)], waits=wl, mark=True)
            wl2 = [bank_tok.get(6 + bk)]
            mm_group(b, pvv[:, 0:128],
                     [(xnv(kc, g * 128, (g + 1) * 128), wsb_v[:, kc * 128:(kc + 1) * 128])
                      for kc in range(8)], waits=wl2, mark=True)
            tk_mm = b.barrier_token("tensor")
            tvmin = b.op("vector", lambda e, o=sc1[:, 0:128], i=pk[:, 0:128]:
                         e.tensor_scalar(o, i, 0.0, None, ALU.min), waits=[tk_mm], mark=True)
            texp = b.op("scalar", lambda e, o=sc1[:, 256:384], i=sc1[:, 0:128]:
                        e.activation(o, i, AF.Exp), waits=[tvmin], mark=True)
            last_elu = b.op("vector", lambda e, o=ktm[:, hh * 2048 + g * 128: hh * 2048 + (g + 1) * 128],
                            i=pk[:, 0:128], x2=sc1[:, 256:384]:
                            e.scalar_tensor_tensor(o, i, 0.0, x2, ALU.max, ALU.add),
                            waits=[texp], mark=True)
            bank_tok[4 + bk] = last_elu
            last_v = b.op("scalar", lambda e, o=vtm[:, hh * 2048 + g * 128: hh * 2048 + (g + 1) * 128],
                          i=pvv[:, 0:128]: e.copy(o, i), waits=[tk_mm], mark=True)
            bank_tok[6 + bk] = last_v
        cg_ps = ctx.pb[0]
        ks_ps = ctx.pb[1]
        cg_done = []
        for g in range(4):
            mm_group(b, cg_ps[:, g * 128:(g + 1) * 128],
                     [(ktm[:, hh * 2048 + (g * 4 + j) * 128: hh * 2048 + (g * 4 + j + 1) * 128],
                       vtm[:, hh * 2048 + (g * 4 + j) * 128: hh * 2048 + (g * 4 + j + 1) * 128])
                      for j in range(4)],
                     waits=[last_elu, last_v, P.cg_free], mark=(g == 3))
        for g in range(16):
            tsk = b.op("tensor", lambda e, o=ks_ps[:, 2 * g:2 * g + 2],
                       l=ktm[:, hh * 2048 + g * 128: hh * 2048 + (g + 1) * 128]:
                       e.matmul(o, l, ctx.indsel16, start=True, stop=True),
                       waits=([P.ks_free] if g == 0 else ()), mark=(g == 15))
        tcg = b.barrier_token("tensor")
        ev1 = b.op("vector", lambda e, o=cg_sb[:, hh * 512:(hh + 1) * 512], i=cg_ps[:, 0:512]:
                   e.tensor_copy(o, i), waits=[tcg], mark=True)
        b.op("vector", lambda e, o=ts_sb[:, hh * 129 + 128: hh * 129 + 129], i=ks_ps[:, 0:32]:
             e.reduce_sum(o, i, axis=AX))
        ev2 = b.op("vector", lambda e, o=ksum_sb[:, hh * 32:(hh + 1) * 32], i=ks_ps[:, 0:32]:
                   e.tensor_copy(o, i), mark=True)
        b.op("vector", lambda e, o=ts_sb[:, hh * 129: hh * 129 + 128],
             a=cg_sb[:, hh * 512: hh * 512 + 128], bb=cg_sb[:, hh * 512 + 128: hh * 512 + 256]:
             e.tensor_tensor(o, a, bb, ALU.add))
        b.op("vector", lambda e, o=ts_sb[:, hh * 129: hh * 129 + 128],
             a=cg_sb[:, hh * 512 + 256: hh * 512 + 384]:
             e.tensor_tensor(o, o, a, ALU.add))
        tt = b.op("vector", lambda e, o=ts_sb[:, hh * 129: hh * 129 + 128],
                  a=cg_sb[:, hh * 512 + 384: hh * 512 + 512]:
                  e.tensor_tensor(o, o, a, ALU.add), mark=True)
        ktot_toks.append(tt)
        P.cg_free = ev1
        P.ks_free = ev2
        wslot_tok[slot] = b.barrier_token("tensor")
    P.ksum_tok = P.cgsb_tok = b.barrier_token("vector")
    send_tok = P.send_cb(ktot_toks[-1])
    if phase1_only:
        return send_tok
    if stop_after == "p1":
        return b.barrier_token("vector")
    return emit_inner_attn_phase2(ctx, P, xn_tok, hxn_tok, wslot_tok, load_head_w,
                                  stop_after=stop_after)


def emit_inner_attn_phase2(ctx, P, xn_tok, hxn_tok, wslot_tok, load_head_w, stop_after=None):
    b = ctx.b
    xnv = fm_view(P.xn)
    zv = fm_view(P.z)
    hxv = fm_view(P.haloxn, Tbuf=128)
    lv = P.lv
    bo_c = lambda c: lv[:, c:c + 1]
    b1_c = lambda j: lv[:, 24 + j:25 + j]
    b2_c = lambda c: lv[:, 56 + c:57 + c]
    l2g_c = lambda c: lv[:, 64 + c:65 + c]
    l2b_c = lambda c: lv[:, 72 + c:73 + c]
    bt = {}

    def mm(bank_region, pairs, waits=(), mark=True):
        return mm_group(b, bank_region, pairs, waits=waits, mark=mark)

    ph1_done = [b.barrier_token("vector"), b.barrier_token("scalar")]
    load_head_w(0, 0)
    for h in range(4):
        slot = h % 2
        if h < 3:
            load_head_w(h + 1, (h + 1) % 2)
        wtok = P.wload[slot]
        qb, kb, vb = P.loc_q, P.loc_k, P.loc_v
        for tt in range(4):
            pq = ctx.pb[0 if tt % 2 == 0 else 1]
            wl = [xn_tok, wtok, bt.get(0 if tt % 2 == 0 else 1)] + \
                 (ph1_done if (h == 0 and tt == 0) else [])
            mm(pq[:, 0:512], [(P.wq_sb[slot][:, kc * 128:(kc + 1) * 128],
                               xnv(kc, tt * 512, (tt + 1) * 512)) for kc in range(8)], waits=wl)
            tq = b.op("scalar", lambda e, o=qb[:, tt * 512:(tt + 1) * 512], i=pq[:, 0:512]:
                      e.copy(o, i), waits=[b.barrier_token("tensor")], mark=True)
            bt[0 if tt % 2 == 0 else 1] = tq
        for tt in range(4):
            pk = ctx.pb[0 if tt % 2 == 0 else 1]
            wl = [bt.get(0 if tt % 2 == 0 else 1)]
            mm(pk[:, 0:512], [(P.wk_sb[slot][:, kc * 128:(kc + 1) * 128],
                               xnv(kc, tt * 512, (tt + 1) * 512)) for kc in range(8)], waits=wl)
            tk = b.op("scalar", lambda e, o=kb[:, tt * 512:(tt + 1) * 512], i=pk[:, 0:512]:
                      e.copy(o, i), waits=[b.barrier_token("tensor")], mark=True)
            bt[0 if tt % 2 == 0 else 1] = tk
        ph = ctx.pb[0]
        mm(ph[:, 0:128], [(P.wk_sb[slot][:, kc * 128:(kc + 1) * 128],
                           hxv(kc, 0, 128)) for kc in range(8)],
           waits=[hxn_tok, bt.get(0)])
        thk = b.op("scalar", lambda e, o=P.hk, i=ph[:, 0:128]: e.copy(o, i),
                   waits=[b.barrier_token("tensor")], mark=True)
        mm(ph[:, 128:256], [(hxv(kc, 0, 128),
                             P.wv_sb[slot][:, kc * 128:(kc + 1) * 128]) for kc in range(8)],
           waits=[thk])
        thv = b.op("scalar", lambda e, o=P.hv, i=ph[:, 128:256]: e.copy(o, i),
                   waits=[b.barrier_token("tensor")], mark=True)
        bt[0] = thv
        for g in range(16):
            pvv = ctx.pb[2 + g % 2]
            wl = [bt.get(2 + g % 2)]
            mm(pvv[:, 0:128], [(xnv(kc, g * 128, (g + 1) * 128),
                                P.wv_sb[slot][:, kc * 128:(kc + 1) * 128]) for kc in range(8)],
               waits=wl)
            tv = b.op("scalar", lambda e, o=vb[:, g * 128:(g + 1) * 128], i=pvv[:, 0:128]:
                      e.copy(o, i), waits=[b.barrier_token("tensor")], mark=True)
            bt[2 + g % 2] = tv
        qdone = b.barrier_token("scalar")
        pav = ctx.pb[6]
        psc = ctx.pb[4]
        ptT = ctx.pb[5].bitcast(BF16)
        prev_texp = None
        prev_ttr = None
        for w in range(16):
            prevk = P.hk if w == 0 else kb[:, (w - 1) * 128: w * 128]
            prevv = P.hv if w == 0 else vb[:, (w - 1) * 128: w * 128]
            maskap = P.maskw0 if w == 0 else P.maskstd
            wl = [qdone, thk, thv, bt.get(4)]
            b.op("tensor", lambda e, o=psc[:, 0:128], l=qb[:, w * 128:(w + 1) * 128], r=prevk:
                 e.matmul(o, l, r, start=True, stop=True), waits=wl)
            tsc = b.op("tensor", lambda e, o=psc[:, 128:256], l=qb[:, w * 128:(w + 1) * 128],
                       r=kb[:, w * 128:(w + 1) * 128]:
                       e.matmul(o, l, r, start=True, stop=True), mark=True)
            b.op("vector", lambda e, o=P.psc, i=psc[:, 0:256], m=maskap:
                 e.scalar_tensor_tensor(o, i, SC_LOC, m, ALU.mult, ALU.add),
                 waits=[tsc, prev_texp] + list(P.mask_toks))
            tmax = b.op("vector", lambda e, o=P.scmisc[:, 0:1], i=P.psc:
                        e.reduce_max(o, i, axis=AX, negate=True), mark=True)
            texp = b.op("scalar", lambda e, o=P.p16, i=P.psc, bi=P.scmisc[:, 0:1]:
                        e.activation(o, i, AF.Exp, bias=bi),
                        waits=[tmax, prev_ttr], mark=True)
            prev_texp = texp
            trs = b.op("vector", lambda e, o=P.scmisc[:, 1:2], i=P.p16:
                       e.reduce_sum(o, i, axis=AX), waits=[texp], mark=True)
            tiv = emit_inv(ctx, P.scmisc[:, 2:3], P.scmisc[:, 1:2], waits=[trs])
            tnrm = b.op("vector", lambda e, o=P.p16, r=P.scmisc[:, 2:3]:
                        e.tensor_scalar(o, o, r, None, ALU.mult), waits=[tiv], mark=True)
            bt[4] = tnrm
            wl2 = [tnrm, bt.get(5)]
            b.op("tensor", lambda e, o=ptT[:, 0:128], i=P.p16[:, 0:128]:
                 e.transpose(o, i, ctx.ident16), waits=wl2)
            ttr = b.op("tensor", lambda e, o=ptT[:, 128:256], i=P.p16[:, 128:256]:
                       e.transpose(o, i, ctx.ident16), mark=True)
            prev_ttr = ttr
            tev = b.op("vector", lambda e, o=P.pT16, i=ptT[:, 0:256]: e.tensor_copy(o, i),
                       waits=[ttr], mark=True)
            bt[5] = tev
            reg = pav[:, (w % 4) * 128:(w % 4 + 1) * 128]
            wl3 = [tev] + ([bt.get(6)] if (w % 4 == 0) else [])
            b.op("tensor", lambda e, o=reg, l=prevv, r=P.pT16[:, 0:128]:
                 e.matmul(o, l, r, start=True, stop=False), waits=wl3)
            tav = b.op("tensor", lambda e, o=reg, l=vb[:, w * 128:(w + 1) * 128], r=P.pT16[:, 128:256]:
                       e.matmul(o, l, r, start=False, stop=True), mark=True)
            if w % 4 == 3:
                teo = b.op("scalar", lambda e, o=P.att_out[:, h * 2048 + (w // 4) * 512: h * 2048 + (w // 4 + 1) * 512],
                           i=pav[:, 0:512]: e.copy(o, i), waits=[tav], mark=True)
                bt[6] = teo
        wslot_tok[slot] = b.barrier_token("tensor")
    att_loc_done = b.barrier_token("scalar")
    if stop_after == "loc":
        return att_loc_done

    def emit_wo_pass(heads, first_pass, done_tok):
        zlast = None
        for pi in range(0, len(heads), 2):
            pair = heads[pi:pi + 2]
            ltoks = []
            for k, h in enumerate(pair):
                ltoks.append(b.op("gpsimd", lambda e, o=P.wo_sb[k],
                                  i=P.wo.ap()[h * 128:(h + 1) * 128, :]:
                                  e.dma_start(out=o, in_=i),
                                  waits=[P.wo_slot_free[k]], dma=True))
            for dc in range(C):
                for tt in range(4):
                    bkid = 7 if (dc * 4 + tt) % 2 == 0 else 0
                    pw = ctx.pb[bkid]
                    wl = [done_tok, bt.get(bkid)] + ltoks
                    mm(pw[:, 0:512],
                       [(P.wo_sb[k][:, dc * 128:(dc + 1) * 128],
                         P.att_out[:, (h % 4) * 2048 + tt * 512:(h % 4) * 2048 + (tt + 1) * 512])
                        for k, h in enumerate(pair)], waits=wl)
                    if first_pass and pi == 0:
                        zlast = b.op("vector", lambda e, o=zv(dc, tt * 512, (tt + 1) * 512),
                                     i=pw[:, 0:512], bc=bo_c(dc):
                                     e.scalar_tensor_tensor(o, i, bc, o, ALU.add, ALU.add),
                                     waits=[b.barrier_token("tensor")], mark=True)
                    else:
                        zlast = b.op("vector", lambda e, o=zv(dc, tt * 512, (tt + 1) * 512),
                                     i=pw[:, 0:512]: e.tensor_tensor(o, o, i, ALU.add),
                                     waits=[b.barrier_token("tensor")], mark=True)
                    bt[bkid] = zlast
            P.wo_slot_free[0] = P.wo_slot_free[1] = b.barrier_token("tensor")
        return zlast

    P.wo_slot_free = [None, None]
    wo1_tok = emit_wo_pass([0, 1, 2, 3], True, att_loc_done)
    if stop_after == "wo1":
        return wo1_tok
    wo_free = b.barrier_token("tensor")

    muse = P.muse
    qb = P.loc_q
    qfm = P.loc_v
    for hh in range(4):
        h = 4 + hh
        slot = hh % 2
        load_head_w(h, slot, extra=[wo_free])
        wtok = P.wload[slot]
        tce = b.op("vector", lambda e, o=P.ceff32, i=P.tot_sb[:, hh * 129: hh * 129 + 128], m=muse:
                   e.tensor_scalar(o, i, m, None, ALU.mult),
                   waits=[P.tot_tok, P.muse_tok], mark=True)
        tce16 = b.op("gpsimd", lambda e, o=P.ceff16, i=P.ceff32: e.tensor_copy(o, i),
                     waits=[tce], mark=True)
        tkb = b.op("vector", lambda e, o=P.kb_sb[:, hh:hh + 1],
                   i=P.tot_sb[:, hh * 129 + 128: hh * 129 + 129], m=muse:
                   e.tensor_tensor(o, i, m, ALU.mult), mark=True)
        for g in range(16):
            pq = ctx.pb[g % 2]
            wl = [xn_tok, wtok, bt.get(g % 2)]
            mm(pq[:, 0:128], [(xnv(kc, g * 128, (g + 1) * 128),
                               P.wq_sb[slot][:, kc * 128:(kc + 1) * 128]) for kc in range(8)],
               waits=wl)
            tq = b.op("scalar", lambda e, o=qb[:, g * 128:(g + 1) * 128], i=pq[:, 0:128]:
                      e.copy(o, i), waits=[b.barrier_token("tensor")], mark=True)
            bt[g % 2] = tq
        prev_texp2 = None
        for g in range(16):
            sl = qb[:, g * 128:(g + 1) * 128]
            tmax = b.op("vector", lambda e, o=P.scmisc[:, 0:1], i=sl:
                        e.reduce_max(o, i, axis=AX, negate=True),
                        waits=[bt[1] if g == 0 else None, prev_texp2], mark=True)
            texp = b.op("scalar", lambda e, o=sl, i=sl, bi=P.scmisc[:, 0:1]:
                        e.activation(o, i, AF.Exp, bias=bi), waits=[tmax], mark=True)
            prev_texp2 = texp
            trs = b.op("vector", lambda e, o=P.scmisc[:, 1:2], i=sl:
                       e.reduce_sum(o, i, axis=AX), waits=[texp], mark=True)
            tiv = emit_inv(ctx, P.scmisc[:, 2:3], P.scmisc[:, 1:2], waits=[trs])
            tn = b.op("vector", lambda e, o=sl, r=P.scmisc[:, 2:3]:
                      e.tensor_scalar(o, o, r, None, ALU.mult), waits=[tiv], mark=True)
            pq2 = ctx.pb[2].bitcast(BF16)
            ttr = b.op("tensor", lambda e, o=pq2[:, 0:128], i=sl:
                       e.transpose(o, i, ctx.ident16), waits=[tn, bt.get(2)], mark=True)
            tev = b.op("vector", lambda e, o=qfm[:, g * 128:(g + 1) * 128], i=pq2[:, 0:128]:
                       e.tensor_copy(o, i), waits=[ttr], mark=True)
            bt[2] = tev
        pk3 = ctx.pb[3]
        t1 = b.op("tensor", lambda e, o=pk3[0:32, 0:128], i=P.ksum_sb[:, hh * 32:(hh + 1) * 32]:
                  e.transpose(o, i, ctx.ident32), waits=[bt.get(3), P.ksum_tok], mark=True)
        t2 = b.op("vector", lambda e, o=P.ksum_um, i=pk3[0:32, 0:128]: e.tensor_copy(o, i),
                  waits=[t1], mark=True)
        t3 = b.op("tensor", lambda e, o=pk3[:, 128:160], l=P.ksum_um, r=ctx.tri32:
                  e.matmul(o, l, r, start=True, stop=True), waits=[t2], mark=True)
        t4 = b.op("vector", lambda e, o=P.r_sb, i=pk3[:, 128:160], kb2=P.kb_sb[:, hh:hh + 1]:
                  e.tensor_scalar(o, i, kb2, 1e-6, ALU.add, ALU.add), waits=[t3, tkb], mark=True)
        t5 = emit_inv(ctx, P.r_sb, P.r_sb, waits=[t4])
        bt[3] = t5
        for u in range(32):
            tqs = b.op("vector", lambda e, o=qfm[:, u * 64:(u + 1) * 64], r=P.r_sb[:, u:u + 1]:
                       e.tensor_scalar(o, o, r, None, ALU.mult),
                       waits=[t5] if u == 0 else (), mark=(u == 31))
        qs_done = tqs
        crun_tok = tce16
        for g in range(4):
            if g > 0:
                if g == 1:
                    tcr = b.op("vector", lambda e, o=P.crun32, a=P.ceff32,
                               bb=cg_view(P, hh, 0): e.tensor_tensor(o, a, bb, ALU.add),
                               waits=[P.cgsb_tok], mark=True)
                else:
                    tcr = b.op("vector", lambda e, o=P.crun32, bb=cg_view(P, hh, g - 1):
                               e.tensor_tensor(o, o, bb, ALU.add), mark=True)
                crun_tok = b.op("gpsimd", lambda e, o=P.crun16, i=P.crun32: e.tensor_copy(o, i),
                                waits=[tcr], mark=True)
            lhs0 = P.ceff16 if g == 0 else P.crun16
            pt5 = ctx.pb[5].bitcast(BF16)
            tbk = None
            for j in range(4):
                tbk = b.op("tensor", lambda e, o=pt5[:, j * 128:(j + 1) * 128],
                           i=P.lin_ktm[:, hh * 2048 + (g * 4 + j) * 128: hh * 2048 + (g * 4 + j + 1) * 128]:
                           e.transpose(o, i, ctx.ident16),
                           waits=[bt.get(5)] if j == 0 else (), mark=(j == 3))
            tbe = b.op("vector", lambda e, o=P.bk16, i=pt5[:, 0:512]: e.tensor_copy(o, i),
                       waits=[tbk], mark=True)
            bt[5] = tbe
            pO = ctx.pb[6]
            tO0 = b.op("tensor", lambda e, o=pO[:, 0:512], l=lhs0, r=qfm[:, g * 512:(g + 1) * 512]:
                       e.matmul(o, l, r, start=True, stop=False),
                       waits=[qs_done, crun_tok, bt.get(6)], mark=True)
            for j in range(4):
                pA = ctx.pb[4]
                tA = b.op("tensor", lambda e, o=pA[:, 0:512], l=P.bk16[:, j * 128:(j + 1) * 128],
                          r=qfm[:, g * 512:(g + 1) * 512]:
                          e.matmul(o, l, r, start=True, stop=True),
                          waits=[tbe, bt.get(4)], mark=True)
                tm0 = b.op("gpsimd", lambda e, o=P.A16: e.memset(o, 0.0),
                           waits=[P.A16_free], mark=True)
                c0 = (2 * j + 1) * 64
                tc1 = b.op("scalar", lambda e, o=P.A16[0:64, c0:512], i=pA[0:64, c0:512]:
                           e.copy(o, i), waits=[tA, tm0], mark=True)
                tc2 = tc1
                if j < 3:
                    c1 = (2 * j + 2) * 64
                    tc2 = b.op("scalar", lambda e, o=P.A16[64:128, c1:512], i=pA[64:128, c1:512]:
                               e.copy(o, i), mark=True)
                bt[4] = tc2
                tOj = b.op("tensor", lambda e, o=pO[:, 0:512],
                           l=P.lin_vtm[:, hh * 2048 + (g * 4 + j) * 128: hh * 2048 + (g * 4 + j + 1) * 128],
                           r=P.A16, p=(j == 3):
                           e.matmul(o, l, r, start=False, stop=p),
                           waits=[tc2], mark=True)
                P.A16_free = tOj
            teo = b.op("scalar", lambda e, o=P.att_out[:, hh * 2048 + g * 512: hh * 2048 + (g + 1) * 512],
                       i=pO[:, 0:512]: e.copy(o, i), waits=[tOj], mark=True)
            bt[6] = teo
        wslot_tok[slot] = b.barrier_token("tensor")
    att_lin_done = b.barrier_token("scalar")
    if stop_after == "lin":
        return att_lin_done
    zadd2 = emit_wo_pass([4, 5, 6, 7], False, att_lin_done)
    lin_q_done = b.barrier_token("tensor")
    if stop_after == "wo2":
        return zadd2

    _, xn2_tok = emit_ln(ctx, zv, T, y16_c=xnv, g_col=l2g_c, b_col=l2b_c,
                         extra_wait=[zadd2, lin_q_done])
    ffl_tok = [None, None]

    def load_ffn_half(half, slot, extra=()):
        wl = [P.ff_free[slot]] + list(extra)
        b.op("gpsimd", lambda e, o=P.w1h_sb[slot].rearrange("p (kc h2) -> p kc h2", kc=8),
             i=P.w1p.ap()[:, :, half * 2048:(half + 1) * 2048].rearrange("kc kp h -> kp kc h"):
             e.dma_start(out=o, in_=i), waits=wl, dma=True)
        t = b.op("gpsimd", lambda e, o=P.w2h_sb[slot].rearrange("p (kc d2) -> p kc d2", kc=16),
                 i=P.w2p.ap()[:, half * 16:(half + 1) * 16, :]:
                 e.dma_start(out=o, in_=i), dma=True)
        ffl_tok[slot] = t
        return t

    load_ffn_half(0, 0, extra=[zadd2, lin_q_done])
    zffn = None
    for half in range(2):
        slot = half
        wtok = ffl_tok[slot]
        for tt in range(4):
            for j in range(16):
                pj = ctx.pb[j % 2]
                wl = [xn2_tok, wtok, bt.get(j % 2)]
                mm(pj[:, 0:512], [(P.w1h_sb[slot][:, kc * 2048 + j * 128: kc * 2048 + (j + 1) * 128],
                                   xnv(kc, tt * 512, (tt + 1) * 512)) for kc in range(8)],
                   waits=wl)
                tg = b.op("scalar", lambda e, o=P.h1_sb[:, j * 512:(j + 1) * 512], i=pj[:, 0:512],
                          bi=b1_c(half * 16 + j): e.activation(o, i, AF.Gelu, bias=bi),
                          waits=[b.barrier_token("tensor")], mark=True)
                bt[j % 2] = tg
            h1_done = b.barrier_token("scalar")
            for dcq in range(2):
                for dc4 in range(4):
                    dc = dcq * 4 + dc4
                    pw = ctx.pb[2 + dc4]
                    wl = [h1_done, bt.get(2 + dc4)]
                    mm(pw[:, 0:512],
                       [(P.w2h_sb[slot][:, j * 1024 + dc * 128: j * 1024 + (dc + 1) * 128],
                         P.h1_sb[:, j * 512:(j + 1) * 512]) for j in range(16)],
                       waits=wl)
                    if half == 0:
                        zffn = b.op("vector", lambda e, o=zv(dc, tt * 512, (tt + 1) * 512),
                                    i=pw[:, 0:512], bc=b2_c(dc):
                                    e.scalar_tensor_tensor(o, i, bc, o, ALU.add, ALU.add),
                                    waits=[b.barrier_token("tensor")], mark=True)
                    else:
                        zffn = b.op("vector", lambda e, o=zv(dc, tt * 512, (tt + 1) * 512),
                                    i=pw[:, 0:512]: e.tensor_tensor(o, o, i, ALU.add),
                                    waits=[b.barrier_token("tensor")], mark=True)
                    bt[2 + dc4] = zffn
        P.ff_free[slot] = b.barrier_token("tensor")
        if half == 0:
            load_ffn_half(1, 1, extra=[P.ff_free[0], b.barrier_token("scalar")])
    P.wslot_prev = b.barrier_token("tensor")
    return zffn


def cg_view(P, hh, g):
    return P.cg_sb[:, hh * 512 + g * 128: hh * 512 + (g + 1) * 128]


ARENA_COLS = 50880


def alloc_inner_P(arena):
    def bv(off, cols, dt):
        if dt == BF16:
            return arena[:, off // 2: off // 2 + cols]
        return arena[:, off // 2: off // 2 + 2 * cols].bitcast(F32)
    P = NS()
    P.lin_ktm = bv(0, 8192, BF16)
    P.lin_vtm = bv(16384, 8192, BF16)
    P.att_out = bv(32768, 8192, BF16)
    P.cg_sb = bv(49152, 2048, F32)
    P.loc_q = bv(57344, 2048, BF16)
    P.loc_k = bv(61440, 2048, BF16)
    P.loc_v = bv(65536, 2048, BF16)
    P.wq_sb = [bv(69632, 1024, BF16), bv(71680, 1024, BF16)]
    P.wk_sb = [bv(73728, 1024, BF16), bv(75776, 1024, BF16)]
    P.wv_sb = [bv(77824, 1024, BF16), bv(79872, 1024, BF16)]
    P.wo_sb = [bv(81920, 1024, BF16), bv(83968, 1024, BF16)]
    P.halo_sb = bv(86016, 1024, F32)
    P.haloxn = bv(90112, 1024, BF16)
    P.tot_sb = bv(92160, 516, F32)
    P.totsend_sb = bv(94272, 516, F32)
    P.ksum_sb = bv(96384, 128, F32)
    SCR = 96896
    P.sc1 = bv(SCR, 512, F32)
    P.psc = bv(SCR, 256, F32)
    P.p16 = bv(SCR + 1024, 256, BF16)
    P.pT16 = bv(SCR + 1536, 256, BF16)
    P.hk = bv(SCR + 2048, 128, BF16)
    P.hv = bv(SCR + 2304, 128, BF16)
    P.bk16 = bv(SCR, 512, BF16)
    P.A16 = bv(SCR + 1024, 512, BF16)
    base2 = SCR + 2560
    P.scmisc = bv(base2, 16, F32)
    P.ceff32 = bv(base2 + 64, 128, F32)
    P.crun32 = bv(base2 + 576, 128, F32)
    P.ceff16 = bv(base2 + 1088, 128, BF16)
    P.crun16 = bv(base2 + 1344, 128, BF16)
    P.kb_sb = bv(base2 + 1600, 4, F32)
    P.ksum_um = bv(base2 + 1664, 128, F32)[0:32, :]
    P.r_sb = bv(base2 + 2176, 32, F32)
    P.w1h_sb = [bv(0, 16384, BF16)] * 2
    P.w2h_sb = [bv(32768, 16384, BF16)] * 2
    P.h1_sb = bv(65536, 8192, BF16)
    P.A16_free = None
    P.ff_free = [None, None]
    P.cg_free = None
    P.ks_free = None
    return P


def inner_dram_tensors(nc, with_exchange_in=True, lite=False):
    d = NS()
    d.wq = nc.dram_tensor("wq", [8, 128, 1024], F32, kind="ExternalInput")
    d.wk = nc.dram_tensor("wk", [8, 128, 1024], F32, kind="ExternalInput")
    d.wv = nc.dram_tensor("wv", [8, 128, 1024], F32, kind="ExternalInput")
    if not lite:
        d.wo = nc.dram_tensor("wo", [1024, 1024], F32, kind="ExternalInput")
        d.w1p = nc.dram_tensor("w1p", [8, 128, 4096], F32, kind="ExternalInput")
        d.w2p = nc.dram_tensor("w2p", [128, 32, 1024], F32, kind="ExternalInput")
    else:
        d.wo = d.w1p = d.w2p = None
    d.lvd = nc.dram_tensor("lvd", [128, 80], F32, kind="ExternalInput")
    if with_exchange_in:
        d.halo_in = nc.dram_tensor("halo_in", [8, 128, 128], F32, kind="ExternalInput")
        d.tot_in = nc.dram_tensor("tot_in", [4, 128, 129], F32, kind="ExternalInput")
        d.muse_in = nc.dram_tensor("muse_in", [128, 1], F32, kind="ExternalInput")
        d.maskw0_in = nc.dram_tensor("maskw0_in", [128, 256], F32, kind="ExternalInput")
    return d


def emit_inner_common_setup(ctx, P, d):
    b = ctx.b
    P.lv = ctx.sb("lv", [128, 80], F32)
    P.lv_tok = b.op("sync", lambda e: e.dma_start(out=P.lv, in_=d.lvd.ap()), dma=True)
    P.wq, P.wk, P.wv, P.wo, P.w1p, P.w2p = d.wq, d.wk, d.wv, d.wo, d.w1p, d.w2p
    P.maskstd = ctx.sb("mask_std", [128, 256], F32)
    b.op("gpsimd", lambda e: e.memset(P.maskstd, 0.0))
    t1 = b.op("gpsimd", lambda e: e.affine_select(P.maskstd, P.maskstd, [[-1, 256]],
                                                  ALU.is_ge, NEG, base=128,
                                                  channel_multiplier=1), mark=True)
    P.maskw0 = ctx.sb("maskw0", [128, 256], F32)
    t2 = b.op("sync", lambda e: e.dma_start(out=P.maskw0, in_=d.maskw0_in.ap()), dma=True)
    P.mask_toks = [t1, t2]
    P.muse = ctx.sb("muse", [128, 1], F32)
    P.muse_tok = b.op("sync", lambda e: e.dma_start(out=P.muse, in_=d.muse_in.ap()), dma=True)
    P.halo_tok = b.op("sync", lambda e, o=P.halo_sb.rearrange("p (c t) -> p c t", c=8),
                      i=d.halo_in.ap().rearrange("c p t -> p c t"):
                      e.dma_start(out=o, in_=i), dma=True)
    P.tot_tok = b.op("sync", lambda e, o=P.tot_sb.rearrange("p (h r) -> p h r", h=4),
                     i=d.tot_in.ap().rearrange("h d r -> d h r"):
                     e.dma_start(out=o, in_=i), dma=True)


def build_pre():
    nc = _mk_nc()
    xs = nc.dram_tensor("xs", [T, D], F32, kind="ExternalInput")
    z_rel = nc.dram_tensor("z", [C, 128, T], F32, kind="ExternalOutput")
    x0_rel = nc.dram_tensor("x0", [C, 128, T], F32, kind="ExternalOutput")
    ctx = Ctx(nc)
    ctx.emit_consts()
    x0 = ctx.sb("x0f", [128, C * T], F32)
    z = ctx.sb("zf", [128, C * T], F32)
    tload = emit_tm_to_fm(ctx, xs.ap(), fm_view(x0), T)
    t32, _ = emit_ln(ctx, fm_view(x0), T, y32_c=fm_view(z), extra_wait=[tload])
    d1 = fm_relay_out(ctx, x0, x0_rel, waits=[tload])
    d2 = fm_relay_out(ctx, z, z_rel, waits=[t32])
    ctx.b.wait("sync", [d1, d2])
    ctx.b.replay()
    return nc


def build_minner(stop_after=None, dumps=()):
    nc = _mk_nc()
    z_in = nc.dram_tensor("z_in", [C, 128, T], F32, kind="ExternalInput")
    z_out = nc.dram_tensor("z_out", [C, 128, T], F32, kind="ExternalOutput")
    d = inner_dram_tensors(nc)
    ctx = Ctx(nc)
    ctx.emit_consts(alloc_lnf32=False)
    z = ctx.sb("z", [128, C * T], F32)
    xn = ctx.sb("xn", [128, C * T], BF16)
    arena = ctx.sb("arena", [128, ARENA_COLS], BF16)
    P = alloc_inner_P(arena)
    ctx.ln_f32 = P.sc1
    P.z, P.xn = z, xn
    emit_inner_common_setup(ctx, P, d)
    P.z_wait = fm_relay_in(ctx, z, z_in, waits=[ctx._consts_done])
    P.send_cb = lambda tok: None
    zffn = emit_inner_attn_common(ctx, P, stop_after=stop_after)
    dd = fm_relay_out(ctx, z, z_out, waits=[zffn])
    waitlist = [dd]
    for (nm, src_attr, c0, cols) in dumps:
        dt_out = nc.dram_tensor(nm, [128, cols], F32, kind="ExternalOutput")
        if src_attr == "arena32":
            src = arena[:, c0:c0 + 2 * cols].bitcast(F32)
        else:
            src = dict(z=z, xn=xn, arena16=arena)[src_attr][:, c0:c0 + cols]
        if src.dtype == F32:
            dtk = ctx.b.op("sync", lambda e, o=dt_out.ap(), i=src: e.dma_start(out=o, in_=i),
                           waits=[zffn], dma=True)
        else:
            dtk = ctx.b.op("gpsimd", lambda e, o=dt_out.ap(), i=src: e.dma_start(out=o, in_=i),
                           waits=[zffn], dma=True)
        waitlist.append(dtk)
        ctx.b.wait("gpsimd" if src.dtype != F32 else "sync", [dtk])
    ctx.b.wait("sync", [dd])
    ctx.b.replay()
    return nc


def build_finner():
    nc = _mk_nc()
    z_in = nc.dram_tensor("z_in", [C, 128, T], F32, kind="ExternalInput")
    tot_out = nc.dram_tensor("tot_out", [4, 128, 129], F32, kind="ExternalOutput")
    d = inner_dram_tensors(nc, with_exchange_in=False, lite=True)
    ctx = Ctx(nc)
    ctx.emit_consts(alloc_lnf32=False)
    z = ctx.sb("z", [128, C * T], F32)
    xn = ctx.sb("xn", [128, C * T], BF16)
    arena = ctx.sb("arena", [128, ARENA_COLS], BF16)
    P = alloc_inner_P(arena)
    ctx.ln_f32 = P.sc1
    P.z, P.xn = z, xn
    b = ctx.b
    P.lv = ctx.sb("lv", [128, 80], F32)
    P.lv_tok = b.op("sync", lambda e: e.dma_start(out=P.lv, in_=d.lvd.ap()), dma=True)
    P.wq, P.wk, P.wv, P.wo, P.w1p, P.w2p = d.wq, d.wk, d.wv, d.wo, d.w1p, d.w2p
    P.z_wait = fm_relay_in(ctx, z, z_in, waits=[ctx._consts_done])

    def send_cb(tok):
        dd = b.op("sync", lambda e, o=tot_out.ap().rearrange("h d r -> d h r"),
                  i=P.totsend_sb.rearrange("p (h r) -> p h r", h=4):
                  e.dma_start(out=o, in_=i), waits=[tok], dma=True)
        b.wait("sync", [dd])
        return dd
    P.send_cb = send_cb
    emit_inner_attn_common(ctx, P, phase1_only=True)
    ctx.b.replay()
    return nc
    nc = _mk_nc()
    xs = nc.dram_tensor("xs", [T, D], F32, kind="ExternalInput")
    z_rel = nc.dram_tensor("z", [C, 128, T], F32, kind="ExternalOutput")
    x0_rel = nc.dram_tensor("x0", [C, 128, T], F32, kind="ExternalOutput")
    ctx = Ctx(nc)
    ctx.emit_consts()
    x0 = ctx.sb("x0f", [128, C * T], F32)
    z = ctx.sb("zf", [128, C * T], F32)
    tload = emit_tm_to_fm(ctx, xs.ap(), fm_view(x0), T)
    t32, _ = emit_ln(ctx, fm_view(x0), T, y32_c=fm_view(z), extra_wait=[tload])
    d1 = fm_relay_out(ctx, x0, x0_rel, waits=[tload])
    d2 = fm_relay_out(ctx, z, z_rel, waits=[t32])
    ctx.b.wait("sync", [d1, d2])
    ctx.b.replay()
    return nc





MOE_SUPERS = [4, 4, 4, 4, 4, 2]


def build_moe(stop_after=None, dbg=False):
    nc = _mk_nc()
    z0_rel = nc.dram_tensor("z0_rel", [C, 128, T], F32, kind="ExternalInput")
    x0_rel = nc.dram_tensor("x0_rel", [C, 128, T], F32, kind="ExternalInput")
    z4_rel = nc.dram_tensor("z4_rel", [C, 128, T], F32, kind="ExternalInput")
    w1t = nc.dram_tensor("w1t", [9, 128, 22, 1024], F32, kind="ExternalInput")
    w3t = nc.dram_tensor("w3t", [9, 128, 22, 1024], F32, kind="ExternalInput")
    w2t = nc.dram_tensor("w2t", [9, 128, 22, 1024], F32, kind="ExternalInput")
    gt = nc.dram_tensor("gt", [128, 72], F32, kind="ExternalInput")
    gb = nc.dram_tensor("gb", [9, 1], F32, kind="ExternalInput")
    x1_rel = nc.dram_tensor("x1_rel", [C, 128, T], F32, kind="ExternalOutput")
    ctx = Ctx(nc)
    ctx.emit_consts(alloc_ln=False)
    b = ctx.b
    base = ctx.sb("base", [128, C * T], F32)
    xn0b = ctx.sb("xn0b", [128, C * T], BF16)
    dbg_out = None
    if dbg:
        dbg_out = (nc.dram_tensor("d_scores", [9, T], F32, kind="ExternalOutput"),
                   nc.dram_tensor("d_cw", [9, T], F32, kind="ExternalOutput"),
                   nc.dram_tensor("d_stm", [128, 9], F32, kind="ExternalOutput"),
                   nc.dram_tensor("d_mx8", [128, 8], F32, kind="ExternalOutput"),
                   nc.dram_tensor("d_idxf", [128, 2], F32, kind="ExternalOutput"),
                   nc.dram_tensor("d_mis2", [128, 8], F32, kind="ExternalOutput"),
                   nc.dram_tensor("d_e1b", [128, 9], F32, kind="ExternalOutput"),
                   nc.dram_tensor("d_cwt", [128, 9], F32, kind="ExternalOutput"))
    emit_moe(ctx, base, xn0b, z0_rel, x0_rel, z4_rel, w1t, w3t, w2t, gt, gb, x1_rel,
             stop_after=stop_after, dbg_out=dbg_out)
    ctx.b.replay()
    return nc


def emit_moe(ctx, base, xn0b, z0_rel, x0_rel, z4_rel, w1t, w3t, w2t, gt, gb, x1_rel,
             fused_in=None, stop_after=None, dbg_out=None, base_tok=None, skip_out=False):
    nc = ctx.nc
    b = ctx.b
    bt = {}
    bv_ = fm_view(base)
    xv = fm_view(xn0b)
    if base_tok is None:
        t_base = fm_relay_in(ctx, base, z4_rel, waits=[ctx._consts_done])
    else:
        t_base = base_tok
    t_xn = b.op("gpsimd", lambda e, o=xn0b.rearrange("p (c t) -> p c t", c=C),
                i=z0_rel.ap().rearrange("c p t -> p c t"): e.dma_start(out=o, in_=i),
                waits=[ctx._consts_done, base_tok], dma=True)
    x0tmp = ctx.sb("x0tmp", [128, T], F32)
    add_tok = None
    for c in range(C):
        td = b.op("sync", lambda e, o=x0tmp, i=x0_rel.ap()[c]: e.dma_start(out=o, in_=i),
                  waits=[t_base, add_tok], dma=True)
        add_tok = b.op("vector", lambda e, o=bv_(c, 0, T), i=x0tmp:
                       e.tensor_tensor(o, o, i, ALU.add), waits=[td], mark=True)
    gtsb = ctx.sb("gtsb", [128, 72], F32)
    gbsb = ctx.sb("gbsb", [9, 1], F32)
    tg1 = b.op("sync", lambda e, o=gtsb, i=gt.ap():
               e.dma_start(out=o, in_=i), dma=True)
    tg2 = b.op("sync", lambda e, o=gbsb, i=gb.ap(): e.dma_start(out=o, in_=i), dma=True)
    gf32 = ctx.sb("gf32", [128, 2 * 512], F32)
    scores = ctx.sb("scores", [9, T], F32)
    gld = {}
    for tt in range(4):
        for kc in range(C):
            sl_ = (tt * C + kc) % 2
            gsl = gf32[:, sl_ * 512:(sl_ + 1) * 512]
            tld = b.op("sync", lambda e, o=gsl, i=z0_rel.ap()[kc][:, tt * 512:(tt + 1) * 512]:
                       e.dma_start(out=o, in_=i), waits=[gld.get(sl_)], dma=True)
            tmm = b.op("tensor", lambda e, o=ctx.pb[6][0:9, 0:512], l=gtsb[:, kc * 9:(kc + 1) * 9],
                       r=gsl, st=(kc == 0), sp=(kc == C - 1):
                       e.matmul(o, l, r, start=st, stop=sp),
                       waits=[tld, tg1] + ([bt.get(6)] if kc == 0 else []), mark=True)
            gld[sl_] = tmm
        tsg = b.op("scalar", lambda e, o=scores[:, tt * 512:(tt + 1) * 512],
                   i=ctx.pb[6][0:9, 0:512], bi=gbsb:
                   e.activation(o, i, AF.Sigmoid, bias=bi), waits=[tmm, tg2], mark=True)
        bt[6] = tsg
    if stop_after == "gate":
        dd = fm_relay_out(ctx, base, x1_rel, waits=[b.barrier_token("scalar")])
        b.wait("sync", [dd])
        return dd
    cw16 = ctx.sb("cw16", [9, T], BF16)
    stm = ctx.sb("stm", [128, 9], F32)
    e1b = ctx.sb("e1b", [128, 9], F32)
    s2b = ctx.sb("s2b", [128, 9], F32)
    cwt = ctx.sb("cwt", [128, 9], F32)
    mis2 = ctx.sb("mis2", [128, 8], F32)
    mx8 = ctx.sb("mx8", [128, 8], F32)
    idx2 = ctx.sb("idx2", [128, 8], U32)
    ctx.iota9 = ctx.sb("iota9", [128, 9], F32)
    idxf = ctx.sb("idxf", [128, 2], F32)
    b.op("gpsimd", lambda e: e.iota(ctx.iota9, [[1, 9]], channel_multiplier=0, allow_small_or_imprecise_dtypes=True))
    prev_cw_tr = None
    prev_tiv = None
    for t16 in range(16):
        ttr = b.op("tensor", lambda e, o=ctx.pb[6][:, 0:9],
                   i=scores[:, t16 * 128:(t16 + 1) * 128], idn=ctx.ident32[0:9, 0:9]:
                   e.transpose(o, i, idn), waits=[bt.get(6)], mark=True)
        tev = b.op("scalar", lambda e, o=stm, i=ctx.pb[6][:, 0:9]: e.copy(o, i),
                   waits=[ttr, prev_cw_tr], mark=True)
        bt[6] = tev
        trm = b.op("vector", lambda e, o=mis2[:, 0:1], i=stm: e.reduce_max(o, i, axis=AX),
                   waits=[tev], mark=True)
        te1 = b.op("gpsimd", lambda e, o=e1b, i=stm, m=mis2[:, 0:1]:
                   e.tensor_scalar(o, i, m, None, ALU.is_equal), waits=[trm], mark=True)
        ts2 = b.op("vector", lambda e, o=s2b, i=e1b, j=stm:
                   e.scalar_tensor_tensor(o, i, NEG, j, ALU.mult, ALU.add),
                   waits=[te1], mark=True)
        tbb = b.op("scalar", lambda e: e.nop(), waits=[ts2], mark=True)
        trm2 = b.op("vector", lambda e, o=mis2[:, 1:2], i=s2b: e.reduce_max(o, i, axis=AX),
                    waits=[tbb], mark=True)
        te2 = b.op("gpsimd", lambda e, o=s2b, i=s2b, m=mis2[:, 1:2]:
                   e.tensor_scalar(o, i, m, None, ALU.is_equal), waits=[trm2], mark=True)
        tdn = b.op("gpsimd", lambda e, o=mis2[:, 2:3], a=mis2[:, 0:1], c2=mis2[:, 1:2]:
                   e.tensor_tensor(o, a, c2, ALU.add), waits=[prev_tiv], mark=True)
        trd = b.op("gpsimd", lambda e, o=mis2[:, 2:3]:
                   e.tensor_scalar(o, o, 1e-20, None, ALU.add), mark=True)
        tiv = emit_inv(ctx, mis2[:, 3:4], mis2[:, 2:3], waits=[trd])
        prev_tiv = tiv
        tw1 = b.op("vector", lambda e, o=mis2[:, 4:5], a=mis2[:, 0:1], r=mis2[:, 3:4]:
                   e.tensor_tensor(o, a, r, ALU.mult), waits=[tiv], mark=True)
        tw2 = b.op("vector", lambda e, o=mis2[:, 5:6], a=mis2[:, 1:2], r=mis2[:, 3:4]:
                   e.tensor_tensor(o, a, r, ALU.mult), mark=True)
        tc1 = b.op("gpsimd", lambda e, o=cwt, i=e1b, w1c=mis2[:, 4:5]:
                   e.tensor_scalar(o, i, w1c, None, ALU.mult), waits=[tw2, te2], mark=True)
        tcw = b.op("vector", lambda e, o=cwt, i=s2b, w2c=mis2[:, 5:6]:
                   e.scalar_tensor_tensor(o, i, w2c, o, ALU.mult, ALU.add),
                   waits=[tc1], mark=True)
        ttr2 = b.op("tensor", lambda e, o=ctx.pb[7][0:9, 0:128],
                    i=cwt, idn=ctx.ident32:
                    e.transpose(o, i, idn), waits=[tcw, bt.get(7)], mark=True)
        prev_cw_tr = ttr2
        tcv = b.op("vector", lambda e, o=cw16[0:9, t16 * 128:(t16 + 1) * 128],
                   i=ctx.pb[7][0:9, 0:128]: e.tensor_copy(o, i), waits=[ttr2], mark=True)
        bt[7] = tcv
    cw_done = b.barrier_token("vector")
    if dbg_out is not None:
        d1 = b.op("sync", lambda e, o=dbg_out[0].ap(), i=scores: e.dma_start(out=o, in_=i),
                  waits=[cw_done], dma=True)
        d2 = b.op("gpsimd", lambda e, o=dbg_out[1].ap(), i=cw16: e.dma_start(out=o, in_=i),
                  waits=[cw_done], dma=True)
        for dd_, src_ in [(2, stm), (3, mx8), (5, mis2), (6, e1b), (7, cwt)]:
            d1 = b.op("sync", lambda e, o=dbg_out[dd_].ap(), i=src_: e.dma_start(out=o, in_=i),
                      waits=[cw_done], dma=True)
        d3 = b.op("sync", lambda e, o=dbg_out[4].ap(), i=idxf: e.dma_start(out=o, in_=i),
                  waits=[cw_done], dma=True)
        b.wait("sync", [d1, d2, d3])
    HCOF = [0, 4, 8, 12, 16, 20]
    w1sb = [ctx.sb(f"w1sb{s}", [128, 4 * 1024], BF16) for s in range(2)]
    w3sb = [ctx.sb(f"w3sb{s}", [128, 4 * 1024], BF16) for s in range(2)]
    w2sb = [ctx.sb(f"w2sb{s}", [128, 4 * 1024], BF16) for s in range(2)]
    CWb = ctx.sb("CWb", [128, T], BF16)
    cwrow = ctx.sb("cwrow", [1, T], BF16)
    tb1 = ctx.sb("tb1", [128, 2 * 512], BF16)
    tb2 = ctx.sb("tb2", [128, 4 * 512], BF16)
    slot_free = [None, None]
    wload = [None, None]

    def load_super(e_, s_, slot):
        hc0 = HCOF[s_]
        hcn = MOE_SUPERS[s_]
        wl = [slot_free[slot]]
        b.op("gpsimd", lambda e, o=w1sb[slot][:, 0:hcn * 1024].rearrange(
            "p (hj x) -> p hj x", hj=hcn),
            i=w1t.ap()[e_][:, hc0:hc0 + hcn, :]:
            e.dma_start(out=o, in_=i), waits=wl, dma=True)
        b.op("gpsimd", lambda e, o=w3sb[slot][:, 0:hcn * 1024].rearrange(
            "p (hj x) -> p hj x", hj=hcn),
            i=w3t.ap()[e_][:, hc0:hc0 + hcn, :]:
            e.dma_start(out=o, in_=i), dma=True)
        t = b.op("gpsimd", lambda e, o=w2sb[slot][:, 0:hcn * 1024].rearrange(
            "p (hj x) -> p hj x", hj=hcn),
            i=w2t.ap()[e_][:, hc0:hc0 + hcn, :]:
            e.dma_start(out=o, in_=i), dma=True)
        wload[slot] = t
        return t

    NS_ = len(MOE_SUPERS)
    load_super(0, 0, 0)
    prev_cwb = None
    prev_phaseB = None
    sidx = 0
    prev_bc = None
    for e_ in range(9):
        trow = b.op("gpsimd", lambda e, o=cwrow, i=cw16[e_:e_ + 1, :]:
                    e.dma_start(out=o, in_=i), waits=[cw_done, prev_bc], dma=True)
        cwb_toks = []
        for tt4 in range(4):
            tbc = b.op("tensor", lambda e, o=ctx.pb[7][:, 0:512], l=ctx.ones1_16,
                       r=cwrow[0:1, tt4 * 512:(tt4 + 1) * 512]:
                       e.matmul(o, l, r, start=True, stop=True),
                       waits=[trow, bt.get(7)], mark=True)
            tce = b.op("scalar", lambda e, o=CWb[:, tt4 * 512:(tt4 + 1) * 512],
                       i=ctx.pb[7][:, 0:512]: e.copy(o, i), waits=[tbc, prev_cwb], mark=True)
            bt[7] = tce
            cwb_toks.append(tce)
        cwb_tok = cwb_toks[-1]
        prev_bc = cwb_tok
        for s_ in range(NS_):
            slot = sidx % 2
            if sidx + 1 < 9 * NS_:
                ne, ns2 = (e_, s_ + 1) if s_ + 1 < NS_ else (e_ + 1, 0)
                load_super(ne, ns2, (sidx + 1) % 2)
            wtok = wload[slot]
            hcn = MOE_SUPERS[s_]
            for tt in range(4):
                hstore_tok = None
                for hj in range(hcn):
                    bka = 4 + hj % 2
                    bkb = 6 + hj % 2
                    ph1 = ctx.pb[bka]
                    ph3 = ctx.pb[bkb]
                    mm_group(b, ph1[:, 0:512],
                             [(w1sb[slot][:, (hj * 8 + kc) * 128:(hj * 8 + kc + 1) * 128],
                               xv(kc, tt * 512, (tt + 1) * 512)) for kc in range(C)],
                             waits=[t_xn, wtok, bt.get(bka)])
                    mm_group(b, ph3[:, 0:512],
                             [(w3sb[slot][:, (hj * 8 + kc) * 128:(hj * 8 + kc + 1) * 128],
                               xv(kc, tt * 512, (tt + 1) * 512)) for kc in range(C)],
                             waits=[bt.get(bkb)])
                    tmm = b.barrier_token("tensor")
                    tbs = tb1[:, (hj % 2) * 512:(hj % 2 + 1) * 512]
                    ts1 = b.op("scalar", lambda e, o=tbs, i=ph1[:, 0:512]:
                               e.activation(o, i, AF.Silu),
                               waits=[tmm, bt.get(("tb1", hj % 2))], mark=True)
                    bt[bka] = ts1
                    th = b.op("vector", lambda e, o=tb2[:, hj * 512:(hj + 1) * 512],
                              i=ph3[:, 0:512], j=tbs:
                              e.scalar_tensor_tensor(o, i, 1.0, j, ALU.mult, ALU.mult),
                              waits=[ts1] + ([prev_phaseB] if hj == 0 else []), mark=True)
                    bt[bkb] = th
                    bt[("tb1", hj % 2)] = th
                    hstore_tok = b.op("vector", lambda e, o=tb2[:, hj * 512:(hj + 1) * 512],
                                      c2=CWb[:, tt * 512:(tt + 1) * 512]:
                                      e.tensor_tensor(o, o, c2, ALU.mult),
                                      waits=[cwb_tok], mark=True)
                for dc in range(C):
                    bk_o = dc % 4
                    reg = ctx.pb[bk_o][:, 0:512]
                    tout = None
                    for hj in range(hcn):
                        tout = b.op("tensor", lambda e, o=reg,
                                    l=w2sb[slot][:, hj * 1024 + dc * 128: hj * 1024 + (dc + 1) * 128],
                                    r=tb2[:, hj * 512:(hj + 1) * 512],
                                    st=(hj == 0), sp=(hj == hcn - 1):
                                    e.matmul(o, l, r, start=st, stop=sp),
                                    waits=([hstore_tok, bt.get(bk_o)] if hj == 0 else []),
                                    mark=(hj == hcn - 1))
                    tza = b.op("vector", lambda e, o=bv_(dc, tt * 512, (tt + 1) * 512),
                               i=reg: e.tensor_tensor(o, o, i, ALU.add),
                               waits=[tout], mark=True)
                    bt[bk_o] = tza
                prev_phaseB = b.barrier_token("tensor")
            slot_free[slot] = b.barrier_token("tensor")
            sidx += 1
        prev_cwb = b.barrier_token("tensor")
    if skip_out:
        return b.barrier_token("vector")
    dd = fm_relay_out(ctx, base, x1_rel, waits=[b.barrier_token("vector")])
    b.wait("sync", [dd])
    return dd


def build_l1attn():
    nc = _mk_nc()
    x1_rel = nc.dram_tensor("x1_rel", [C, 128, T], F32, kind="ExternalInput")
    x1b_rel = nc.dram_tensor("x1b_rel", [C, 128, T], F32, kind="ExternalOutput")
    xn1_rel = nc.dram_tensor("xn1_rel", [C, 128, T], BF16, kind="ExternalOutput")
    ctx = Ctx(nc)
    ctx.emit_consts()
    b = ctx.b
    xb = ctx.sb("xb", [128, C * T], F32)
    xn1 = ctx.sb("xn1", [128, C * T], BF16)
    t_x = fm_relay_in(ctx, xb, x1_rel, waits=[ctx._consts_done])
    _, t_ln = emit_ln(ctx, fm_view(xb), T, y16_c=fm_view(xn1), extra_wait=[t_x])
    d_xn = b.op("gpsimd", lambda e, o=xn1_rel.ap().rearrange("c p t -> p c t"),
                i=xn1.rearrange("p (c t) -> p c t", c=C): e.dma_start(out=o, in_=i),
                waits=[t_ln], dma=True)
    zlast = emit_l1attn(ctx, xb, xn1, t_ln)
    dd = fm_relay_out(ctx, xb, x1b_rel, waits=[zlast])
    b.wait("sync", [dd])
    b.wait("gpsimd", [d_xn])
    ctx.b.replay()
    return nc


def emit_l1attn(ctx, xb, xn1, t_ln):
    b = ctx.b
    bt = {}
    xnv = fm_view(xn1)
    Xr = ctx.sb("Xr", [128, C * 512], BF16)
    Xtm = ctx.sb("Xtm", [128, 4 * 1024], BF16)
    probs = ctx.sb("probs", [128, 4 * 512], BF16)
    pT = ctx.sb("pT", [128, 4 * 512], BF16)
    psc2 = ctx.sb("psc2", [128, 512], F32)
    scm = ctx.sb("scm", [128, 4], F32)
    zlast = None
    prev_gather = None
    for r in range(4):
        tg = None
        for c in range(C):
            tg = b.op("vector", lambda e, o=Xr[:, c * 512:(c + 1) * 512],
                      i=xn1[:, c * T + r: c * T + T: 4]: e.tensor_copy(o, i),
                      waits=([t_ln, prev_gather] if c == 0 else []), mark=(c == C - 1))
        prev_texp = None
        for qt in range(4):
            pq = ctx.pb[0 if qt % 2 == 0 else 1]
            bk = 0 if qt % 2 == 0 else 1
            mm_group(b, pq[:, 0:512],
                     [(Xr[:, kc * 512 + qt * 128: kc * 512 + (qt + 1) * 128],
                       Xr[:, kc * 512:(kc + 1) * 512]) for kc in range(C)],
                     waits=[tg, bt.get(bk)])
            tmm = b.barrier_token("tensor")
            tst = b.op("vector", lambda e, o=psc2, i=pq[:, 0:512]:
                       e.tensor_scalar(o, i, 0.03125, None, ALU.mult),
                       waits=[tmm, prev_texp], mark=True)
            bt[bk] = tst
            tmx = b.op("vector", lambda e, o=scm[:, 0:1], i=psc2:
                       e.reduce_max(o, i, axis=AX, negate=True), mark=True)
            texp = b.op("scalar", lambda e, o=probs[:, qt * 512:(qt + 1) * 512], i=psc2,
                        bi=scm[:, 0:1]: e.activation(o, i, AF.Exp, bias=bi),
                        waits=[tmx], mark=True)
            trs = b.op("vector", lambda e, o=scm[:, 1:2], i=probs[:, qt * 512:(qt + 1) * 512]:
                       e.reduce_sum(o, i, axis=AX), waits=[texp], mark=True)
            tiv = emit_inv(ctx, scm[:, 2:3], scm[:, 1:2], waits=[trs])
            prev_texp = b.op("vector", lambda e, o=probs[:, qt * 512:(qt + 1) * 512],
                             rc=scm[:, 2:3]: e.tensor_scalar(o, o, rc, None, ALU.mult),
                             waits=[tiv], mark=True)
        pnorm_done = prev_texp
        pt2 = ctx.pb[2].bitcast(BF16)
        for qt in range(4):
            for kt in range(4):
                ttr = b.op("tensor", lambda e, o=pt2[:, kt * 128:(kt + 1) * 128],
                           i=probs[:, qt * 512 + kt * 128: qt * 512 + (kt + 1) * 128]:
                           e.transpose(o, i, ctx.ident16),
                           waits=[pnorm_done, bt.get(2)] if kt == 0 else (), mark=(kt == 3))
            tpv = b.op("scalar", lambda e, o=pT[:, qt * 512:(qt + 1) * 512], i=pt2[:, 0:512]:
                       e.copy(o, i), waits=[ttr], mark=True)
            bt[2] = tpv
        pT_done = tpv
        pt3 = ctx.pb[3].bitcast(BF16)
        for kt in range(4):
            for c in range(C):
                ttx = b.op("tensor", lambda e, o=pt3[:, c * 128:(c + 1) * 128],
                           i=Xr[:, c * 512 + kt * 128: c * 512 + (kt + 1) * 128]:
                           e.transpose(o, i, ctx.ident16),
                           waits=[bt.get(3)] if c == 0 else (), mark=(c == C - 1))
            txv = b.op("scalar", lambda e, o=Xtm[:, kt * 1024:(kt + 1) * 1024], i=pt3[:, 0:1024]:
                       e.copy(o, i), waits=[ttx], mark=True)
            bt[3] = txv
        Xtm_done = txv
        for dc in range(C):
            bko = 4 + dc % 2
            pO = ctx.pb[bko]
            tmo = None
            for qt in range(4):
                for kt in range(4):
                    tmo = b.op("tensor", lambda e, o=pO[:, qt * 128:(qt + 1) * 128],
                               l=Xtm[:, kt * 1024 + dc * 128: kt * 1024 + (dc + 1) * 128],
                               r=pT[:, qt * 512 + kt * 128: qt * 512 + (kt + 1) * 128],
                               st=(kt == 0), sp=(kt == 3): e.matmul(o, l, r, start=st, stop=sp),
                               waits=([pT_done, Xtm_done, bt.get(bko)] if (qt == 0 and kt == 0) else ()),
                               mark=(qt == 3 and kt == 3))
            zlast = b.op("vector", lambda e, o=xb[:, dc * T + r: dc * T + T: 4], i=pO[:, 0:512]:
                         e.scalar_tensor_tensor(o, i, 0.25, o, ALU.mult, ALU.add),
                         waits=[tmo], mark=True)
            bt[bko] = zlast
        prev_gather = zlast
    return zlast


def build_l1ffn():
    nc = _mk_nc()
    x1b_rel = nc.dram_tensor("x1b_rel", [C, 128, T], F32, kind="ExternalInput")
    xn1_rel = nc.dram_tensor("xn1_rel", [C, 128, T], BF16, kind="ExternalInput")
    w1tp = nc.dram_tensor("w1tp", [32, 128, 1024], F32, kind="ExternalInput")
    w2tp = nc.dram_tensor("w2tp", [32, 128, 1024], F32, kind="ExternalInput")
    h1tp = nc.dram_tensor("h1tp", [32, 128, 4096], F32, kind="ExternalInput")
    h2tp = nc.dram_tensor("h2tp", [32, 128, 4096], F32, kind="ExternalInput")
    w3tp = nc.dram_tensor("w3tp", [8, 128, 4096], F32, kind="ExternalInput")
    l1s = nc.dram_tensor("l1s", [128, 136], F32, kind="ExternalInput")
    out = nc.dram_tensor("out", [T, D], F32, kind="ExternalOutput")
    ctx = Ctx(nc)
    ctx.emit_consts(alloc_ln=False)
    b = ctx.b
    emit_l1ffn(ctx, x1b_rel, xn1_rel, w1tp, w2tp, h1tp, h2tp, w3tp, l1s, out)
    ctx.b.replay()
    return nc


def emit_l1ffn(ctx, x1b_rel, xn1_rel, w1tp, w2tp, h1tp, h2tp, w3tp, l1s, out,
               start_wait=None):
    b = ctx.b
    bt = {}
    h_a = ctx.sb("h_a", [128, 32 * 1024], BF16)
    h_b = ctx.sb("h_b", [128, 32 * 1024], BF16)
    xnh = ctx.sb("xnh", [128, C * 1024], BF16)
    lsb = ctx.sb("lsb", [128, 136], F32)
    wsm = [ctx.sb(f"wsm{i}", [128, 8 * 128], BF16) for i in range(4)]
    wbg = [ctx.sb(f"wbg{i}", [128, 32 * 128], BF16) for i in range(2)]
    res_fm = ctx.sb("res_fm", [128, C * 512], F32)
    x1t = ctx.sb("x1t", [128, 2 * 512], F32)
    stg = ctx.sb("stg", [128, 1024], F32)
    tb1 = ctx.sb("l1tb1", [128, 512], BF16)
    tl = b.op("sync", lambda e: e.dma_start(out=lsb, in_=l1s.ap()), waits=[start_wait], dma=True)
    s1c = lambda j: lsb[:, j:j + 1]
    s2c = lambda j: lsb[:, 32 + j:33 + j]
    sh1c = lambda j: lsb[:, 64 + j:65 + j]
    sh2c = lambda j: lsb[:, 96 + j:97 + j]
    s3c = lambda dc: lsb[:, 128 + dc:129 + dc]
    wsm_free = [start_wait] * 4
    wbg_free = [start_wait] * 2
    wload = {}

    def load_small(which, j, slot):
        src = w1tp if which == 0 else w2tp
        t = b.op("gpsimd", lambda e, o=wsm[slot], i=src.ap()[j]:
                 e.dma_start(out=o, in_=i), waits=[wsm_free[slot]], dma=True)
        wload[("s", slot)] = t

    def load_big(src, j, slot):
        t = b.op("gpsimd", lambda e, o=wbg[slot], i=src.ap()[j]:
                 e.dma_start(out=o, in_=i), waits=[wbg_free[slot]], dma=True)
        wload[("b", slot)] = t

    last_store = None
    for th in range(2):
        txn = b.op("gpsimd", lambda e, o=xnh.rearrange("p (c t) -> p c t", c=C),
                   i=xn1_rel.ap()[:, :, th * 1024:(th + 1) * 1024].rearrange("c p t -> p c t"):
                   e.dma_start(out=o, in_=i), waits=[bt.get("xnh"), start_wait], dma=True)
        load_small(0, 0, 0)
        load_small(1, 0, 1)
        prev_silu = None
        tha = None
        for j in range(32):
            sl1 = 0 if j % 2 == 0 else 2
            sl2 = 1 if j % 2 == 0 else 3
            if j < 31:
                load_small(0, j + 1, 2 if j % 2 == 0 else 0)
                load_small(1, j + 1, 3 if j % 2 == 0 else 1)
            t1tok = wload[("s", sl1)]
            t2tok = wload[("s", sl2)]
            for tt2 in range(2):
                p0 = ctx.pb[tt2]
                p1 = ctx.pb[2 + tt2]
                mm_group(b, p0[:, 0:512],
                         [(wsm[sl1][:, kc * 128:(kc + 1) * 128],
                           xnh[:, kc * 1024 + tt2 * 512: kc * 1024 + (tt2 + 1) * 512])
                          for kc in range(C)],
                         waits=[txn, t1tok, tl, bt.get(tt2)])
                mm_group(b, p1[:, 0:512],
                         [(wsm[sl2][:, kc * 128:(kc + 1) * 128],
                           xnh[:, kc * 1024 + tt2 * 512: kc * 1024 + (tt2 + 1) * 512])
                          for kc in range(C)],
                         waits=[t2tok, bt.get(2 + tt2)])
                tmm = b.barrier_token("tensor")
                tsl = b.op("scalar", lambda e, o=tb1, i=p0[:, 0:512], sc=s1c(j):
                           e.activation(o, i, AF.Silu, scale=sc),
                           waits=[tmm, prev_silu], mark=True)
                bt[tt2] = tsl
                tha = b.op("vector", lambda e,
                           o=h_a[:, j * 1024 + tt2 * 512: j * 1024 + (tt2 + 1) * 512],
                           i=p1[:, 0:512], sc=s2c(j), j2=tb1:
                           e.scalar_tensor_tensor(o, i, sc, j2, ALU.mult, ALU.mult),
                           waits=[tsl], mark=True)
                bt[2 + tt2] = tha
                prev_silu = tha
            wsm_free[sl1] = b.barrier_token("tensor")
            wsm_free[sl2] = wsm_free[sl1]
        bt["xnh"] = b.barrier_token("tensor")
        s1_done = tha
        for stage in range(2):
            src = h1tp if stage == 0 else h2tp
            scf = sh1c if stage == 0 else sh2c
            hin = h_a if stage == 0 else h_b
            hout = h_b if stage == 0 else h_a
            sc_ctl = s1_done if stage == 0 else None
            load_big(src, 0, 0)
            prev_ev = None
            for j in range(32):
                slot = j % 2
                if j < 31:
                    load_big(src, j + 1, (j + 1) % 2)
                wtok = wload[("b", slot)]
                for tt2 in range(2):
                    pj = ctx.pb[4 + tt2]
                    mm_group(b, pj[:, 0:512],
                             [(wbg[slot][:, kc * 128:(kc + 1) * 128],
                               hin[:, kc * 1024 + tt2 * 512: kc * 1024 + (tt2 + 1) * 512])
                              for kc in range(32)],
                             waits=[wtok, sc_ctl, bt.get(4 + tt2)])
                    tmm = b.barrier_token("tensor")
                    prev_ev = b.op("scalar", lambda e,
                                   o=hout[:, j * 1024 + tt2 * 512: j * 1024 + (tt2 + 1) * 512],
                                   i=pj[:, 0:512], sc=scf(j):
                                   e.activation(o, i, AF.Gelu, scale=sc),
                                   waits=[tmm], mark=True)
                    bt[4 + tt2] = prev_ev
                wbg_free[slot] = b.barrier_token("tensor")
        s3_done = prev_ev
        for tt2 in range(2):
            tres = None
            for dc in range(C):
                slot = dc % 2
                load_big(w3tp, dc, slot)
                wtok = wload[("b", slot)]
                tx1 = b.op("sync", lambda e, o=x1t[:, slot * 512:(slot + 1) * 512],
                           i=x1b_rel.ap()[dc][:, th * 1024 + tt2 * 512: th * 1024 + (tt2 + 1) * 512]:
                           e.dma_start(out=o, in_=i), waits=[bt.get(("x1", slot))], dma=True)
                pj = ctx.pb[6 + dc % 2]
                mm_group(b, pj[:, 0:512],
                         [(wbg[slot][:, kc * 128:(kc + 1) * 128],
                           h_a[:, kc * 1024 + tt2 * 512: kc * 1024 + (tt2 + 1) * 512])
                          for kc in range(32)],
                         waits=[wtok, s3_done, bt.get(6 + dc % 2)])
                tmm = b.barrier_token("tensor")
                tres = b.op("vector", lambda e, o=res_fm[:, dc * 512:(dc + 1) * 512],
                            i=pj[:, 0:512], sc=s3c(dc), x1a=x1t[:, slot * 512:(slot + 1) * 512]:
                            e.scalar_tensor_tensor(o, i, sc, x1a, ALU.mult, ALU.add),
                            waits=[tmm, tx1], mark=True)
                bt[6 + dc % 2] = tres
                bt[("x1", slot)] = tres
                wbg_free[slot] = b.barrier_token("tensor")
            res_done = tres
            for it in range(4):
                tok0 = th * 1024 + tt2 * 512 + it * 128
                tev = None
                for c in range(C):
                    bk = c % 2
                    ttr = b.op("tensor", lambda e, o=ctx.pb[bk][:, 0:128],
                               i=res_fm[:, c * 512 + it * 128: c * 512 + (it + 1) * 128]:
                               e.transpose(o, i, ctx.ident32),
                               waits=[res_done, bt.get(("tr", bk))], mark=True)
                    tev = b.op("scalar", lambda e, o=stg[:, c * 128:(c + 1) * 128],
                               i=ctx.pb[bk][:, 0:128]: e.copy(o, i),
                               waits=[ttr, last_store if c < 2 else None], mark=True)
                    bt[("tr", bk)] = tev
                last_store = b.op("sync", lambda e, o=out.ap()[tok0:tok0 + 128, :], i=stg:
                                  e.dma_start(out=o, in_=i), waits=[tev], dma=True)
    b.wait("sync", [last_store])
    return last_store


def _vcol(v, n):
    return np.ascontiguousarray(np.asarray(v, np.float32).reshape(n, 128).T)


def _prep_host(inputs):
    H = {}
    x = np.ascontiguousarray(np.asarray(inputs["L0_attn_x"], np.float32))
    H["xs"] = x.reshape(8, 2048, 1024)
    def hkp(w):
        return np.ascontiguousarray(
            np.asarray(w, np.float32).reshape(8, 128, 8, 128)
            .transpose(2, 1, 0, 3).reshape(8, 128, 1024))

    for i in range(4):
        H[f"wq{i}"] = hkp(inputs["L0_attn_Wq"][i])
        H[f"wk{i}"] = hkp(inputs["L0_attn_Wk"][i])
        H[f"wv{i}"] = hkp(inputs["L0_attn_Wv"][i])
        H[f"wo{i}"] = np.ascontiguousarray(inputs["L0_attn_Wo"][i])
        H[f"w1p{i}"] = np.ascontiguousarray(inputs["L0_attn_W1"][i].reshape(8, 128, 4096))
        H[f"w2p{i}"] = np.ascontiguousarray(
            np.asarray(inputs["L0_attn_W2"][i], np.float32)
            .reshape(32, 128, 1024).transpose(1, 0, 2))
        H[f"lv{i}"] = np.concatenate(
            [_vcol(inputs["L0_attn_bo"][i], 8), _vcol(inputs["L0_attn_ln1_g"][i], 8),
             _vcol(inputs["L0_attn_ln1_b"][i], 8), _vcol(inputs["L0_attn_b1"][i], 32),
             _vcol(inputs["L0_attn_b2"][i], 8), _vcol(inputs["L0_attn_ln2_g"][i], 8),
             _vcol(inputs["L0_attn_ln2_b"][i], 8)], axis=1).astype(np.float32)
    ew1 = np.concatenate([inputs["L0_ffn_w1"], inputs["L0_ffn_new_w1"]], 0)
    ew2 = np.concatenate([inputs["L0_ffn_w2"], inputs["L0_ffn_new_w2"]], 0)
    ew3 = np.concatenate([inputs["L0_ffn_w3"], inputs["L0_ffn_new_w3"]], 0)
    H["w1t"] = np.ascontiguousarray(
        np.asarray(ew1, np.float32).transpose(0, 2, 1).reshape(9, 8, 128, 22, 128)
        .transpose(0, 2, 3, 1, 4).reshape(9, 128, 22, 1024))
    H["w3t"] = np.ascontiguousarray(
        np.asarray(ew3, np.float32).transpose(0, 2, 1).reshape(9, 8, 128, 22, 128)
        .transpose(0, 2, 3, 1, 4).reshape(9, 128, 22, 1024))
    H["w2t"] = np.ascontiguousarray(
        np.asarray(ew2, np.float32).transpose(0, 2, 1).reshape(9, 22, 128, 1024)
        .transpose(0, 2, 1, 3))
    gate = np.concatenate([inputs["L0_ffn_gate_weight"], inputs["L0_ffn_new_gate_weight"]], 0)
    H["gt"] = np.ascontiguousarray(
        np.asarray(gate, np.float32).T.reshape(8, 128, 9).transpose(1, 0, 2).reshape(128, 72))
    gb = np.zeros((9, 1), np.float32)
    gb[8, 0] = np.asarray(inputs["L0_ffn_gate_bias"], np.float32)[0]
    H["gb"] = gb

    def tp(w, nj):
        wt = np.asarray(w, np.float32).T
        nin = wt.shape[0]
        return np.ascontiguousarray(
            wt.reshape(nin // 128, 128, nj, 128).transpose(2, 1, 0, 3)
            .reshape(nj, 128, nin))

    H["w1tp"] = tp(inputs["L1_ffn_w1_w"], 32)
    H["w2tp"] = tp(inputs["L1_ffn_w2_w"], 32)
    H["h1tp"] = tp(inputs["L1_ffn_h1_w"], 32)
    H["h2tp"] = tp(inputs["L1_ffn_h2_w"], 32)
    H["w3tp"] = tp(inputs["L1_ffn_w3_w"], 8)
    H["l1s"] = np.concatenate(
        [_vcol(inputs["L1_ffn_w1_s"], 32), _vcol(inputs["L1_ffn_w2_s"], 32),
         _vcol(inputs["L1_ffn_h1_s"], 32), _vcol(inputs["L1_ffn_h2_s"], 32),
         _vcol(inputs["L1_ffn_w3_s"], 8)], axis=1).astype(np.float32)
    ii = np.arange(128)[:, None]
    jj = np.arange(256)[None, :]
    mask_std = np.where(jj - 128 <= ii, 0.0, NEG).astype(np.float32)
    mask_first = mask_std.copy()
    mask_first[:, 0:128] = NEG
    H["mask_std"] = mask_std
    H["mask_first"] = mask_first
    return H


def kernel_multilaunch(**inputs):
    H = _prep_host(inputs)
    cores = list(range(N_CORES))

    ncP = build_pre()
    res = run_bass_kernel_spmd(ncP, [dict(xs=H["xs"][c]) for c in cores], core_ids=cores)
    z = [res.results[c]["z"] for c in cores]
    x0 = [res.results[c]["x0"] for c in cores]
    z0 = z

    ncF = build_finner()
    ncM = build_minner()
    for i in range(4):
        baseF = dict(wq=H[f"wq{i}"], wk=H[f"wk{i}"], wv=H[f"wv{i}"], lvd=H[f"lv{i}"])
        resF = run_bass_kernel_spmd(
            ncF, [dict(z_in=z[c], **baseF) for c in cores], core_ids=cores)
        tots = [resF.results[c]["tot_out"] for c in cores]
        baseM = dict(wq=H[f"wq{i}"], wk=H[f"wk{i}"], wv=H[f"wv{i}"], wo=H[f"wo{i}"],
                     w1p=H[f"w1p{i}"], w2p=H[f"w2p{i}"], lvd=H[f"lv{i}"])
        maps = []
        for c in cores:
            partner = c - 1 if c % 2 == 1 else c
            maps.append(dict(
                z_in=z[c],
                halo_in=np.ascontiguousarray(z[partner][:, :, 1920:2048]),
                tot_in=tots[partner],
                muse_in=np.full((128, 1), 1.0 if c % 2 else 0.0, np.float32),
                maskw0_in=H["mask_first"] if c % 2 == 0 else H["mask_std"],
                **baseM))
        resM = run_bass_kernel_spmd(ncM, maps, core_ids=cores)
        z = [resM.results[c]["z_out"] for c in cores]

    ncMOE = build_moe()
    maps = [dict(z0_rel=z0[c], x0_rel=x0[c], z4_rel=z[c], w1t=H["w1t"], w3t=H["w3t"],
                 w2t=H["w2t"], gt=H["gt"], gb=H["gb"]) for c in cores]
    resMOE = run_bass_kernel_spmd(ncMOE, maps, core_ids=cores)
    x1 = [resMOE.results[c]["x1_rel"] for c in cores]

    ncA = build_l1attn()
    resA = run_bass_kernel_spmd(ncA, [dict(x1_rel=x1[c]) for c in cores], core_ids=cores)
    x1b = [resA.results[c]["x1b_rel"] for c in cores]
    xn1 = [resA.results[c]["xn1_rel"] for c in cores]

    ncFF = build_l1ffn()
    maps = [dict(x1b_rel=x1b[c], xn1_rel=xn1[c], w1tp=H["w1tp"], w2tp=H["w2tp"],
                 h1tp=H["h1tp"], h2tp=H["h2tp"], w3tp=H["w3tp"], l1s=H["l1s"])
            for c in cores]
    resFF = run_bass_kernel_spmd(ncFF, maps, core_ids=cores)
    out = np.stack([resFF.results[c]["out"] for c in cores], 0)
    return np.ascontiguousarray(out.reshape(4, 4096, 1024))


PAIR_GROUPS = [[0, 1], [2, 3], [4, 5], [6, 7]]


def build_fused():
    nc = _mk_nc()
    xs = nc.dram_tensor("xs", [T, D], F32, kind="ExternalInput")
    layers = []
    for i in range(4):
        d = NS()
        d.wq = nc.dram_tensor(f"wq{i}", [8, 128, 1024], F32, kind="ExternalInput")
        d.wk = nc.dram_tensor(f"wk{i}", [8, 128, 1024], F32, kind="ExternalInput")
        d.wv = nc.dram_tensor(f"wv{i}", [8, 128, 1024], F32, kind="ExternalInput")
        d.wo = nc.dram_tensor(f"wo{i}", [1024, 1024], F32, kind="ExternalInput")
        d.w1p = nc.dram_tensor(f"w1p{i}", [8, 128, 4096], F32, kind="ExternalInput")
        d.w2p = nc.dram_tensor(f"w2p{i}", [128, 32, 1024], F32, kind="ExternalInput")
        d.lvd = nc.dram_tensor(f"lv{i}", [128, 80], F32, kind="ExternalInput")
        layers.append(d)
    w1t = nc.dram_tensor("w1t", [9, 128, 22, 1024], F32, kind="ExternalInput")
    w3t = nc.dram_tensor("w3t", [9, 128, 22, 1024], F32, kind="ExternalInput")
    w2t = nc.dram_tensor("w2t", [9, 128, 22, 1024], F32, kind="ExternalInput")
    gt = nc.dram_tensor("gt", [128, 72], F32, kind="ExternalInput")
    gb = nc.dram_tensor("gb", [9, 1], F32, kind="ExternalInput")
    w1tp = nc.dram_tensor("w1tp", [32, 128, 1024], F32, kind="ExternalInput")
    w2tp = nc.dram_tensor("w2tp", [32, 128, 1024], F32, kind="ExternalInput")
    h1tp = nc.dram_tensor("h1tp", [32, 128, 4096], F32, kind="ExternalInput")
    h2tp = nc.dram_tensor("h2tp", [32, 128, 4096], F32, kind="ExternalInput")
    w3tp = nc.dram_tensor("w3tp", [8, 128, 4096], F32, kind="ExternalInput")
    l1s = nc.dram_tensor("l1s", [128, 136], F32, kind="ExternalInput")
    muse_in = nc.dram_tensor("muse_in", [128, 1], F32, kind="ExternalInput")
    maskw0_in = nc.dram_tensor("maskw0_in", [128, 256], F32, kind="ExternalInput")
    out = nc.dram_tensor("out", [T, D], F32, kind="ExternalOutput")
    x0_sp = nc.dram_tensor("x0_sp", [C, 128, T], F32, kind="Internal")
    z0_sp = nc.dram_tensor("z0_sp", [C, 128, T], F32, kind="Internal")
    x1_sp = nc.dram_tensor("x1_sp", [C, 128, T], F32, kind="Internal")
    xn1_sp = nc.dram_tensor("xn1_sp", [C, 128, T], BF16, kind="Internal")
    halo_snd = nc.dram_tensor("halo_snd", [128, 1024], F32, kind="Internal")
    halo_rcv = nc.dram_tensor("halo_rcv", [256, 1024], F32, kind="Internal")
    tot_snd = nc.dram_tensor("tot_snd", [128, 516], F32, kind="Internal")
    tot_rcv = nc.dram_tensor("tot_rcv", [256, 516], F32, kind="Internal")

    ctx = Ctx(nc)
    ctx.emit_consts(alloc_lnf32=False)
    b = ctx.b
    z = ctx.sb("z", [128, C * T], F32)
    xn = ctx.sb("xn", [128, C * T], BF16)
    arena = ctx.sb("arena", [128, ARENA_COLS], BF16)
    P = alloc_inner_P(arena)
    P.z, P.xn = z, xn
    ctx.ln_f32 = P.sc1

    P.maskstd = ctx.sb("mask_std", [128, 256], F32)
    b.op("gpsimd", lambda e: e.memset(P.maskstd, 0.0))
    t1 = b.op("gpsimd", lambda e: e.affine_select(P.maskstd, P.maskstd, [[-1, 256]],
                                                  ALU.is_ge, NEG, base=128,
                                                  channel_multiplier=1), mark=True)
    P.maskw0 = ctx.sb("maskw0", [128, 256], F32)
    t2 = b.op("sync", lambda e: e.dma_start(out=P.maskw0, in_=maskw0_in.ap()), dma=True)
    P.mask_toks = [t1, t2]
    P.muse = ctx.sb("muse", [128, 1], F32)
    P.muse_tok = b.op("sync", lambda e: e.dma_start(out=P.muse, in_=muse_in.ap()), dma=True)
    P.lv = ctx.sb("lv", [128, 80], F32)

    ctx.begin_overlay(arena)
    x0buf = ctx.sb("x0f", [128, C * T], F32)
    ctx.tm_stage = ctx.sb("tm_stage", [128, 2048], F32)
    ctx.end_overlay()
    tload = emit_tm_to_fm(ctx, xs.ap(), fm_view(x0buf), T)
    t32, _ = emit_ln(ctx, fm_view(x0buf), T, y32_c=fm_view(z), extra_wait=[tload])
    d_x0 = fm_relay_out(ctx, x0buf, x0_sp, waits=[tload])
    d_z0 = fm_relay_out(ctx, z, z0_sp, waits=[t32])

    prev_z_tok = t32
    prev_collA = None
    prev_hsb = None
    lv_free = None
    for i in range(4):
        d = layers[i]
        P.wq, P.wk, P.wv, P.wo, P.w1p, P.w2p = d.wq, d.wk, d.wv, d.wo, d.w1p, d.w2p
        P.lv_tok = b.op("sync", lambda e, dd=d: e.dma_start(out=P.lv, in_=dd.lvd.ap()),
                        waits=[lv_free], dma=True)
        extra_a = [d_x0, d_z0] if i == 0 else []
        tsnd = b.op("sync", lambda e,
                    o=halo_snd.ap().rearrange("p (c t) -> p c t", c=C),
                    i=z.rearrange("p (c t) -> p c t", c=C)[:, :, 1920:2048]:
                    e.dma_start(out=o, in_=i),
                    waits=[prev_z_tok, prev_collA] + extra_a, dma=True)
        tca = b.op("gpsimd", lambda e: e.collective_compute(
            "AllGather", ALU.bypass, replica_groups=PAIR_GROUPS,
            ins=[halo_snd.ap().opt()], outs=[halo_rcv.ap().opt()]),
            waits=[tsnd, prev_hsb], coll=True)
        prev_collA = tca
        thsb = b.op("sync", lambda e, o=P.halo_sb, i=halo_rcv.ap()[0:128, :]:
                    e.dma_start(out=o, in_=i), waits=[tca], dma=True)
        prev_hsb = thsb
        P.halo_tok = thsb
        P.z_wait = prev_z_tok

        cb_state = {}

        def send_cb(tok, cb_state=cb_state, i=i):
            t_s = b.op("sync", lambda e, o=tot_snd.ap().rearrange("p (h r) -> p h r", h=4),
                       i=P.totsend_sb.rearrange("p (h r) -> p h r", h=4):
                       e.dma_start(out=o, in_=i),
                       waits=[tok, cb_state.get("prev_cb")], dma=True)
            t_c = b.op("gpsimd", lambda e: e.collective_compute(
                "AllGather", ALU.bypass, replica_groups=PAIR_GROUPS,
                ins=[tot_snd.ap().opt()], outs=[tot_rcv.ap().opt()]),
                waits=[t_s, cb_state.get("prev_tsb")], coll=True)
            t_r = b.op("sync", lambda e, o=P.tot_sb, i=tot_rcv.ap()[0:128, :]:
                       e.dma_start(out=o, in_=i), waits=[t_c], dma=True)
            P.tot_tok = t_r
            cb_state["prev_cb"] = t_c
            cb_state["prev_tsb"] = t_r
            return t_r

        P.send_cb = send_cb
        zffn = emit_inner_attn_common(ctx, P)
        prev_z_tok = zffn
        lv_free = zffn

    ctx.begin_overlay(arena)
    moe_done = emit_moe(ctx, z, xn, z0_sp, x0_sp, None, w1t, w3t, w2t, gt, gb, None,
                        base_tok=prev_z_tok, skip_out=True)
    ctx.end_overlay()

    ctx.begin_overlay(arena)
    _, t_ln1 = emit_ln(ctx, fm_view(z), T, y16_c=fm_view(xn), extra_wait=[moe_done])
    d_xn1 = b.op("gpsimd", lambda e, o=xn1_sp.ap().rearrange("c p t -> p c t"),
                 i=xn.rearrange("p (c t) -> p c t", c=C): e.dma_start(out=o, in_=i),
                 waits=[t_ln1], dma=True)
    zl1 = emit_l1attn(ctx, z, xn, t_ln1)
    ctx.end_overlay()

    d_x1 = fm_relay_out(ctx, z, x1_sp, waits=[zl1])
    vm = {
        "h_a": z.bitcast(BF16),
        "xnh": xn[:, 0:8192],
        "x1t": xn[:, 8192:10240].bitcast(F32),
        "stg": xn[:, 10240:12288].bitcast(F32),
        "wsm0": xn[:, 12288:13312],
        "wsm1": xn[:, 13312:14336],
        "wsm2": xn[:, 14336:15360],
        "wsm3": xn[:, 15360:16384],
    }
    ctx.begin_overlay(arena, view_map=vm)
    last = emit_l1ffn(ctx, x1_sp, xn1_sp, w1tp, w2tp, h1tp, h2tp, w3tp, l1s, out,
                      start_wait=(("sync", "d"), b.marks[("sync", "d")]))
    ctx.end_overlay()
    b.wait("sync", [last, d_xn1])
    b.replay()
    return nc


def kernel(**inputs):
    H = _prep_host(inputs)
    cores = list(range(N_CORES))
    nc = build_fused()
    maps = []
    for c in cores:
        m = dict(xs=H["xs"][c],
                 muse_in=np.full((128, 1), 1.0 if c % 2 else 0.0, np.float32),
                 maskw0_in=H["mask_first"] if c % 2 == 0 else H["mask_std"],
                 w1t=H["w1t"], w3t=H["w3t"], w2t=H["w2t"], gt=H["gt"], gb=H["gb"],
                 w1tp=H["w1tp"], w2tp=H["w2tp"], h1tp=H["h1tp"], h2tp=H["h2tp"],
                 w3tp=H["w3tp"], l1s=H["l1s"])
        for i in range(4):
            m[f"wq{i}"] = H[f"wq{i}"]
            m[f"wk{i}"] = H[f"wk{i}"]
            m[f"wv{i}"] = H[f"wv{i}"]
            m[f"wo{i}"] = H[f"wo{i}"]
            m[f"w1p{i}"] = H[f"w1p{i}"]
            m[f"w2p{i}"] = H[f"w2p{i}"]
            m[f"lv{i}"] = H[f"lv{i}"]
        maps.append(m)
    res = run_bass_kernel_spmd(nc, maps, core_ids=cores)
    outp = np.stack([res.results[c]["out"] for c in cores], 0)
    return np.ascontiguousarray(outp.reshape(4, 4096, 1024))


# version 05
# speedup vs baseline: 1.1596x; 1.1596x over previous
import numpy as np
import ml_dtypes
import concourse.bass as bass
import concourse.mybir as mybir
from concourse.bass_utils import run_bass_kernel_spmd

F32 = mybir.dt.float32
U32 = mybir.dt.uint32
BF16 = mybir.dt.bfloat16
AF = mybir.ActivationFunctionType
ALU = mybir.AluOpType
AX = mybir.AxisListType.X

T = 2048
C = 8
D = 1024
TT = 512
NTT = T // TT
NEG = -1e30
EPS_LN = 1e-5
N_CORES = 8
PAIRS = [[0, 1], [2, 3], [4, 5], [6, 7]]

ENGINES = ("tensor", "vector", "scalar", "gpsimd", "sync")
RACE_DETECT = True


def _mk_nc():
    return bass.Bass("TRN2", target_bir_lowering=False,
                     detect_race_conditions=RACE_DETECT)


class B:

    def __init__(self, nc):
        self.nc = nc
        self.ops = {e: [] for e in ENGINES}
        self.marks = {}
        self.sems = {}
        self._nded = 0

    def _semkey(self, key):
        if key not in self.sems:
            self.sems[key] = self.nc.alloc_semaphore(f"pg_{key[0]}_{key[1]}")
            self.marks[key] = 0
        return key

    def op(self, eng, fn, waits=(), mark=False, dma=False, coll=False):
        waits = [w for w in waits if w is not None]
        rec = dict(fn=fn, waits=waits, markval=None, inc=0, key=None)
        self.ops[eng].append(rec)
        if coll:
            key = self._semkey(("x", self._nded))
            self._nded += 1
            rec["inc"] = 1
            self.marks[key] += 1
            rec["markval"] = self.marks[key]
            rec["key"] = key
            return (key, self.marks[key])
        if mark or dma:
            key = self._semkey((eng, "d" if dma else "c"))
            rec["inc"] = 16 if dma else 1
            self.marks[key] += rec["inc"]
            rec["markval"] = self.marks[key]
            rec["key"] = key
            return (key, self.marks[key])
        return None

    def barrier_token(self, eng, kind="c"):
        key = (eng, kind)
        if key not in self.marks or self.marks[key] == 0:
            return None
        return (key, self.marks[key])

    def wait(self, eng, tokens):
        toks = [t for t in tokens if t is not None]
        if toks:
            self.ops[eng].append(dict(fn=None, waits=toks, markval=None, inc=0, key=None))

    def replay(self):
        with self.nc.Block() as blk:
            for ename in ENGINES:
                if not self.ops[ename]:
                    continue

                def make(ename):
                    def body(eng):
                        for rec in self.ops[ename]:
                            for (wk, wv) in rec["waits"]:
                                if rec["fn"] is not None and wk == (ename, "c"):
                                    continue
                                eng.wait_ge(self.sems[wk], wv)
                            if rec["fn"] is None:
                                continue
                            ins = rec["fn"](eng)
                            if rec["markval"] is not None:
                                ins.then_inc(self.sems[rec["key"]], rec["inc"])
                    return body

                getattr(blk, ename)(make(ename))


class Ctx:
    def __init__(self, nc):
        self.nc = nc
        self.b = B(nc)
        self.pb = [nc.alloc_psum_tensor(f"bank{i}", [128, 512], F32).ap() for i in range(8)]
        self._consts_done = None

    def sb(self, name, shape, dtype):
        if getattr(self, "view_map", None) is not None and name in self.view_map:
            return self.view_map[name]
        if getattr(self, "overlay", None) is not None:
            nbytes = 1
            for d_ in shape[1:]:
                nbytes *= d_
            nbytes *= (4 if dtype == F32 else (2 if dtype == BF16 else 4))
            off = (self.overlay_cur + 63) // 64 * 64
            assert off + nbytes <= self.overlay_end, f"overlay overflow for {name}"
            self.overlay_cur = off + nbytes
            npart = shape[0]
            view = self.overlay[0:npart, off // 2:(off + nbytes) // 2]
            if dtype == F32:
                view = view.bitcast(F32)
            elif dtype != BF16:
                view = view.bitcast(dtype)
            if len(shape) > 2:
                raise RuntimeError("overlay only supports 2-D")
            return view
        return self.nc.alloc_sbuf_tensor(name, shape, dtype).ap()

    def begin_overlay(self, arena_ap, view_map=None, end_bytes=None):
        self.overlay = arena_ap
        self.overlay_cur = 0
        self.overlay_end = end_bytes if end_bytes is not None else arena_ap.shape[1] * 2
        self.view_map = view_map or {}

    def end_overlay(self):
        self.overlay = None
        self.view_map = None

    def emit_consts(self, alloc_lnf32=True, alloc_ln=True):
        b = self.b
        self.ones128_32 = self.sb("ones128_32", [128, 1], F32)
        self.ones128_16 = self.sb("ones128_16", [128, 1], BF16)
        self.ones1_32 = self.sb("ones1_32", [1, 128], F32)
        self.ones1_16 = self.sb("ones1_16", [1, 128], BF16)
        self.ident32 = self.sb("ident32", [128, 128], F32)
        self.ident16 = self.sb("ident16", [128, 128], BF16)
        self.tri16 = self.sb("tri16", [32, 32], BF16)
        self.tri32 = self.sb("tri32", [32, 32], F32)
        self.indsel16 = self.sb("indsel16", [128, 2], BF16)
        b.op("gpsimd", lambda e: e.memset(self.ones128_32, 1.0))
        b.op("gpsimd", lambda e: e.memset(self.ones128_16, 1.0))
        b.op("gpsimd", lambda e: e.memset(self.ones1_32, 1.0))
        b.op("gpsimd", lambda e: e.memset(self.ones1_16, 1.0))
        b.op("gpsimd", lambda e: e.memset(self.ident32, 1.0))
        b.op("gpsimd", lambda e: e.affine_select(self.ident32, self.ident32, [[1, 128]],
                                                 ALU.is_equal, 0.0, base=0, channel_multiplier=-1))
        b.op("gpsimd", lambda e: e.memset(self.ident16, 1.0))
        b.op("gpsimd", lambda e: e.affine_select(self.ident16, self.ident16, [[1, 128]],
                                                 ALU.is_equal, 0.0, base=0, channel_multiplier=-1))
        b.op("gpsimd", lambda e: e.memset(self.tri16, 1.0))
        b.op("gpsimd", lambda e: e.affine_select(self.tri16, self.tri16, [[1, 32]],
                                                 ALU.is_gt, 0.0, base=0, channel_multiplier=-1))
        b.op("gpsimd", lambda e: e.memset(self.tri32, 1.0))
        b.op("gpsimd", lambda e: e.affine_select(self.tri32, self.tri32, [[1, 32]],
                                                 ALU.is_gt, 0.0, base=0, channel_multiplier=-1))
        b.op("gpsimd", lambda e: e.memset(self.indsel16, 1.0))
        b.op("gpsimd", lambda e: e.affine_select(self.indsel16, self.indsel16, [[-64, 2]],
                                                 ALU.is_ge, 0.0, base=0, channel_multiplier=1))
        t = b.op("gpsimd", lambda e: e.affine_select(self.indsel16, self.indsel16, [[64, 2]],
                                                     ALU.is_ge, 0.0, base=63,
                                                     channel_multiplier=-1), mark=True)
        if alloc_ln:
            self.ln_b16 = self.sb("ln_b16", [128, 2 * 512], BF16)
            self.ln_sq16 = self.sb("ln_sq16", [128, 2 * 512], BF16)
            self.ln_small = self.sb("ln_small", [1, 512 * 2], F32)
            if alloc_lnf32:
                self.ln_f32 = self.sb("ln_f32", [128, 512], F32)
        self._consts_done = t
        return t


def mm_group(b, out_ap, pairs, waits=(), mark=True):
    n = len(pairs)
    tok = None
    for i, (lh, rh) in enumerate(pairs):
        def f(e, lh=lh, rh=rh, s=(i == 0), p=(i == n - 1), o=out_ap):
            return e.matmul(o, lh, rh, start=s, stop=p)
        tok = b.op("tensor", f, waits=(waits if i == 0 else ()), mark=(mark and i == n - 1))
    return tok


def emit_inv(ctx, out_ap, in_ap, waits=()):
    b = ctx.b
    t1 = b.op("scalar", lambda e, o=out_ap, i=in_ap: e.activation(o, i, AF.Ln),
              waits=list(waits), mark=True)
    t2 = b.op("vector", lambda e, o=out_ap: e.tensor_scalar(o, o, -1.0, None, ALU.mult),
              waits=[t1], mark=True)
    t3 = b.op("scalar", lambda e, o=out_ap: e.activation(o, o, AF.Exp),
              waits=[t2], mark=True)
    return t3


def emit_ln(ctx, x_c, Ttot, y32_c=None, y16_c=None, g_col=None, b_col=None,
            extra_wait=()):
    b = ctx.b
    nt = (Ttot + TT - 1) // TT
    S = ctx.pb[6][0:1, :]
    Q = ctx.pb[7][0:1, :]
    sm = ctx.ln_small
    y32_tok = y16_tok = None
    prev_small_tok = None
    prev_apply_tok = None
    prev_stats_tok = None
    for it in range(nt):
        t0 = it * TT
        t1 = min(Ttot, t0 + TT)
        w = t1 - t0
        qtoks = {}
        stats_tok = None
        for c in range(C):
            xa = x_c(c, t0, t1)
            sl = c % 2
            dst16 = ctx.ln_b16[:, sl * 512:sl * 512 + w]
            dsq = ctx.ln_sq16[:, sl * 512:sl * 512 + w]
            wts = list(extra_wait) if (c == 0 and it == 0) else []
            wts += [qtoks.get(c - 2)] if c >= 2 else ([prev_stats_tok] if prev_stats_tok else [])
            tg = b.op("gpsimd", lambda e, o=dst16, i=xa: e.tensor_copy(o, i),
                      waits=wts, mark=True)
            ts = b.op("scalar", lambda e, o=dsq, i=xa: e.activation(o, i, AF.Square),
                      waits=wts, mark=True)
            wmm = [tg, ts] + ([prev_small_tok] if (c == 0 and prev_small_tok) else [])
            b.op("tensor", lambda e, o=S[:, 0:w], l=ctx.ones128_16, r=dst16,
                 s=(c == 0), p=(c == C - 1): e.matmul(o, l, r, start=s, stop=p), waits=wmm)
            stats_tok = b.op("tensor", lambda e, o=Q[:, 0:w], l=ctx.ones128_16, r=dsq,
                             s=(c == 0), p=(c == C - 1): e.matmul(o, l, r, start=s, stop=p),
                             mark=True)
            qtoks[c] = stats_tok
        prev_stats_tok = stats_tok
        arow, brow = sm[:, 0:w], sm[:, 512:512 + w]
        mrow = S[:, 0:w]
        tvm = b.op("vector", lambda e, o=mrow: e.tensor_scalar(o, o, 1.0 / 1024, None, ALU.mult),
                   waits=[stats_tok], mark=True)
        tsq = b.op("scalar", lambda e, o=arow, i=mrow: e.activation(o, i, AF.Square),
                   waits=[tvm], mark=True)
        b.op("vector", lambda e, o=Q[:, 0:w], j=arow:
             e.scalar_tensor_tensor(o, o, 1.0 / 1024, j, ALU.mult, ALU.subtract),
             waits=[tsq])
        tv = b.op("vector", lambda e, o=Q[:, 0:w]:
                  e.tensor_scalar(o, o, EPS_LN, None, ALU.add), mark=True)
        tln = b.op("scalar", lambda e, o=arow, i=Q[:, 0:w]:
                   e.activation(o, i, AF.Ln), waits=[tv], mark=True)
        tv2 = b.op("vector", lambda e, o=arow, i=arow:
                   e.tensor_scalar(o, i, -0.5, None, ALU.mult), waits=[tln], mark=True)
        texp = b.op("scalar", lambda e, o=arow, i=arow:
                    e.activation(o, i, AF.Exp), waits=[tv2], mark=True)
        small_tok = b.op("vector", lambda e, o=brow, i=mrow, j=arow:
                         e.scalar_tensor_tensor(o, i, -1.0, j, ALU.mult, ALU.mult),
                         waits=[texp], mark=True)
        prev_small_tok = small_tok
        A_b = ctx.pb[4][:, 0:w]
        B_b = ctx.pb[5][:, 0:w]
        wb = [small_tok] + ([prev_apply_tok] if prev_apply_tok else [])
        b.op("tensor", lambda e, o=A_b, r=arow: e.matmul(o, ctx.ones1_32, r, start=True, stop=True),
             waits=wb)
        bcast_tok = b.op("tensor", lambda e, o=B_b, r=brow:
                         e.matmul(o, ctx.ones1_32, r, start=True, stop=True), mark=True)
        prev_cast16 = None
        for c in range(C):
            xa = x_c(c, t0, t1)
            scratch = y32_c is None
            dst = ctx.ln_f32[:, 0:w] if scratch else y32_c(c, t0, t1)
            wl = ([bcast_tok] if c == 0 else []) + ([prev_cast16] if (scratch and prev_cast16) else [])
            b.op("vector", lambda e, o=dst, i=xa, a=A_b: e.tensor_tensor(o, i, a, ALU.mult),
                 waits=wl)
            t32 = b.op("vector", lambda e, o=dst, bb=B_b: e.tensor_tensor(o, o, bb, ALU.add),
                       mark=True)
            if y16_c is not None:
                d16 = y16_c(c, t0, t1)
                if g_col is not None:
                    y16_tok = b.op("vector", lambda e, o=d16, i=dst, g=g_col(c), bc=b_col(c):
                                   e.tensor_scalar(o, i, g, bc, ALU.mult, ALU.add),
                                   mark=True)
                    t32 = y16_tok
                    prev_cast16 = y16_tok
                else:
                    y16_tok = b.op("gpsimd", lambda e, o=d16, i=dst: e.tensor_copy(o, i),
                                   waits=[t32], mark=True)
                    prev_cast16 = y16_tok
        prev_apply_tok = t32
        y32_tok = t32
    return y32_tok, y16_tok


def emit_tm_to_fm(ctx, src_tm_dram, dst_fm_c, Ttot, banks=(0, 1, 2, 3)):
    b = ctx.b
    if not hasattr(ctx, "tm_stage"):
        ctx.tm_stage = ctx.sb("tm_stage", [128, 2 * 1024], F32)
    ntile = Ttot // 128
    evac_toks = {}
    tr_toks = {}
    last = None
    for i in range(ntile):
        slot = i % 2
        stg = ctx.tm_stage[:, slot * 1024:(slot + 1) * 1024]
        wl = [tr_toks.get(slot), ctx._consts_done] if i < 2 else [tr_toks.get(slot)]
        dtok = b.op("sync", lambda e, o=stg, s=src_tm_dram[i * 128:(i + 1) * 128, :]:
                    e.dma_start(out=o, in_=s), waits=wl, dma=True)
        for c in range(C):
            bk = banks[(i * C + c) % len(banks)]
            pt = ctx.pb[bk][:, 0:128]
            wl2 = [dtok] + ([evac_toks.get(bk)] if evac_toks.get(bk) else [])
            ttok = b.op("tensor", lambda e, o=pt, s=stg[:, c * 128:(c + 1) * 128]:
                        e.transpose(o, s, ctx.ident32), waits=wl2, mark=True)
            last = b.op("vector", lambda e, o=dst_fm_c(c, i * 128, (i + 1) * 128), s=pt:
                        e.tensor_copy(o, s), waits=[ttok], mark=True)
            evac_toks[bk] = last
            tr_toks[slot] = ttok
    return last


def emit_fm_to_tm_store(ctx, src_fm_c, dst_tm_dram, Ttot, waits=(), banks=(0, 1, 2, 3)):
    b = ctx.b
    if not hasattr(ctx, "tm_stage"):
        ctx.tm_stage = ctx.sb("tm_stage", [128, 2 * 1024], F32)
    ntile = Ttot // 128
    evac_toks = {}
    dma_toks = {}
    last = None
    for i in range(ntile):
        slot = i % 2
        stg = ctx.tm_stage[:, slot * 1024:(slot + 1) * 1024]
        for c in range(C):
            bk = banks[(i * C + c) % len(banks)]
            pt = ctx.pb[bk][:, 0:128]
            wl = list(waits) if (i == 0 and c == 0) else []
            if evac_toks.get(bk):
                wl.append(evac_toks[bk])
            ttok = b.op("tensor", lambda e, o=pt, s=src_fm_c(c, i * 128, (i + 1) * 128):
                        e.transpose(o, s, ctx.ident32), waits=wl, mark=True)
            wl2 = [ttok] + ([dma_toks.get(slot)] if dma_toks.get(slot) else [])
            etok = b.op("vector", lambda e, o=stg[:, c * 128:(c + 1) * 128], s=pt:
                        e.tensor_copy(o, s), waits=wl2, mark=True)
            evac_toks[bk] = etok
        last = b.op("sync", lambda e, o=dst_tm_dram[i * 128:(i + 1) * 128, :], s=stg:
                    e.dma_start(out=o, in_=s), waits=[etok], dma=True)
        dma_toks[slot] = last
    return last


def fm_relay_in(ctx, buf_flat, relay_dram, dtype_note="", waits=(), eng="sync", Tr=T):
    b = ctx.b
    o = buf_flat.rearrange("p (c t) -> p c t", c=C)
    i = relay_dram.ap().rearrange("c p t -> p c t")
    return b.op(eng, lambda e: e.dma_start(out=o, in_=i), waits=waits, dma=True)


def fm_relay_out(ctx, buf_flat, relay_dram, waits=(), eng="sync", Tr=T):
    b = ctx.b
    i = buf_flat.rearrange("p (c t) -> p c t", c=C)
    o = relay_dram.ap().rearrange("c p t -> p c t")
    return b.op(eng, lambda e: e.dma_start(out=o, in_=i), waits=waits, dma=True)


def fm_view(buf, Tbuf=T):
    def v(c, t0, t1):
        return buf[:, c * Tbuf + t0:c * Tbuf + t1]
    return v


SC_LOC = float(128.0 ** -0.5)


class NS:
    pass


def emit_inner_attn_common(ctx, P, phase1_only=False, stop_after=None):
    b = ctx.b
    nc = ctx.nc
    lv = P.lv
    bo_c = lambda c: lv[:, c:c + 1]
    l1g_c = lambda c: lv[:, 8 + c:9 + c]
    l1b_c = lambda c: lv[:, 16 + c:17 + c]
    b1_c = lambda j: lv[:, 24 + j:25 + j]
    b2_c = lambda c: lv[:, 56 + c:57 + c]
    l2g_c = lambda c: lv[:, 64 + c:65 + c]
    l2b_c = lambda c: lv[:, 72 + c:73 + c]
    zv = fm_view(P.z)
    xnv = fm_view(P.xn)

    _, xn_tok = emit_ln(ctx, zv, T, y16_c=xnv, g_col=l1g_c, b_col=l1b_c,
                        extra_wait=[P.z_wait, P.lv_tok])
    if stop_after == "ln1":
        return xn_tok

    hxn_tok = None
    if not phase1_only:
        hv = fm_view(P.halo_sb, Tbuf=128)
        hxv = fm_view(P.haloxn, Tbuf=128)
        _, hxn_tok = emit_ln(ctx, hv, 128, y16_c=hxv, g_col=l1g_c, b_col=l1b_c,
                             extra_wait=[P.halo_tok, P.lv_tok, xn_tok])

    wp = getattr(P, "wslot_prev", None)
    wslot_tok = [wp, wp]
    wload_tok = [None, None]
    P.wload = wload_tok

    def load_head_w(h, slot, extra=()):
        wl = [wslot_tok[slot]] + list(extra)
        t1 = b.op("gpsimd", lambda e, o=P.wq_sb[slot], i=P.wq.ap()[h]:
                  e.dma_start(out=o, in_=i), waits=wl, dma=True)
        t2 = b.op("gpsimd", lambda e, o=P.wk_sb[slot], i=P.wk.ap()[h]:
                  e.dma_start(out=o, in_=i), dma=True)
        t3 = b.op("gpsimd", lambda e, o=P.wv_sb[slot], i=P.wv.ap()[h]:
                  e.dma_start(out=o, in_=i), dma=True)
        wload_tok[slot] = t3
        return t3

    load_head_w(4, 0)
    ktm = P.lin_ktm
    vtm = P.lin_vtm
    cg_sb = P.cg_sb
    ksum_sb = P.ksum_sb
    ts_sb = P.totsend_sb
    sc1 = P.sc1
    ktot_toks = []
    for hh in range(4):
        h = 4 + hh
        slot = hh % 2
        if hh < 3:
            load_head_w(h + 1, (hh + 1) % 2)
        wsb_k, wsb_v = P.wk_sb[slot], P.wv_sb[slot]
        wtok = wload_tok[slot]
        last_elu = None
        last_v = None
        bank_tok = {}
        for g4 in range(4):
            bk = (g4 % 2)
            pk = ctx.pb[4 + bk]
            pvv = ctx.pb[6 + bk]
            for k in range(4):
                g = g4 * 4 + k
                wl = [xn_tok, wtok] + \
                     ([bank_tok.get(4 + bk)] if k == 0 else []) + \
                     ([hxn_tok] if (hh == 0 and g == 0) else [])
                mm_group(b, pk[:, k * 128:(k + 1) * 128],
                         [(xnv(kc, g * 128, (g + 1) * 128), wsb_k[:, kc * 128:(kc + 1) * 128])
                          for kc in range(8)], waits=wl, mark=(k == 3))
            for k in range(4):
                g = g4 * 4 + k
                wl2 = [bank_tok.get(6 + bk)] if k == 0 else []
                mm_group(b, pvv[:, k * 128:(k + 1) * 128],
                         [(xnv(kc, g * 128, (g + 1) * 128), wsb_v[:, kc * 128:(kc + 1) * 128])
                          for kc in range(8)], waits=wl2, mark=(k == 3))
            tk_mm = b.barrier_token("tensor")
            tvmin = b.op("vector", lambda e, o=sc1[:, 0:512], i=pk[:, 0:512]:
                         e.tensor_scalar(o, i, 0.0, None, ALU.min),
                         waits=[tk_mm, bank_tok.get("sc1")], mark=True)
            texp = b.op("scalar", lambda e, o=P.psc4[:, 0:512], i=sc1[:, 0:512]:
                        e.activation(o, i, AF.Exp),
                        waits=[tvmin, bank_tok.get("psc4")], mark=True)
            bank_tok["sc1"] = texp
            last_elu = b.op("vector", lambda e,
                            o=ktm[:, hh * 2048 + g4 * 512: hh * 2048 + (g4 + 1) * 512],
                            i=pk[:, 0:512], x2=P.psc4[:, 0:512]:
                            e.scalar_tensor_tensor(o, i, 0.0, x2, ALU.max, ALU.add),
                            waits=[texp], mark=True)
            bank_tok["psc4"] = last_elu
            bank_tok[4 + bk] = last_elu
            last_v = b.op("scalar", lambda e,
                          o=vtm[:, hh * 2048 + g4 * 512: hh * 2048 + (g4 + 1) * 512],
                          i=pvv[:, 0:512]: e.copy(o, i), waits=[tk_mm], mark=True)
            bank_tok[6 + bk] = last_v
        cg_ps = ctx.pb[0]
        ks_ps = ctx.pb[1]
        cg_done = []
        for g in range(4):
            mm_group(b, cg_ps[:, g * 128:(g + 1) * 128],
                     [(ktm[:, hh * 2048 + (g * 4 + j) * 128: hh * 2048 + (g * 4 + j + 1) * 128],
                       vtm[:, hh * 2048 + (g * 4 + j) * 128: hh * 2048 + (g * 4 + j + 1) * 128])
                      for j in range(4)],
                     waits=[last_elu, last_v, P.cg_free], mark=(g == 3))
        for g in range(16):
            tsk = b.op("tensor", lambda e, o=ks_ps[:, 2 * g:2 * g + 2],
                       l=ktm[:, hh * 2048 + g * 128: hh * 2048 + (g + 1) * 128]:
                       e.matmul(o, l, ctx.indsel16, start=True, stop=True),
                       waits=([P.ks_free] if g == 0 else ()), mark=(g == 15))
        tcg = b.barrier_token("tensor")
        ev1 = b.op("vector", lambda e, o=cg_sb[:, hh * 512:(hh + 1) * 512], i=cg_ps[:, 0:512]:
                   e.tensor_copy(o, i), waits=[tcg], mark=True)
        b.op("vector", lambda e, o=ts_sb[:, hh * 129 + 128: hh * 129 + 129], i=ks_ps[:, 0:32]:
             e.reduce_sum(o, i, axis=AX))
        ev2 = b.op("vector", lambda e, o=ksum_sb[:, hh * 32:(hh + 1) * 32], i=ks_ps[:, 0:32]:
                   e.tensor_copy(o, i), mark=True)
        b.op("vector", lambda e, o=ts_sb[:, hh * 129: hh * 129 + 128],
             a=cg_sb[:, hh * 512: hh * 512 + 128], bb=cg_sb[:, hh * 512 + 128: hh * 512 + 256]:
             e.tensor_tensor(o, a, bb, ALU.add))
        b.op("vector", lambda e, o=ts_sb[:, hh * 129: hh * 129 + 128],
             a=cg_sb[:, hh * 512 + 256: hh * 512 + 384]:
             e.tensor_tensor(o, o, a, ALU.add))
        tt = b.op("vector", lambda e, o=ts_sb[:, hh * 129: hh * 129 + 128],
                  a=cg_sb[:, hh * 512 + 384: hh * 512 + 512]:
                  e.tensor_tensor(o, o, a, ALU.add), mark=True)
        ktot_toks.append(tt)
        P.cg_free = ev1
        P.ks_free = ev2
        wslot_tok[slot] = b.barrier_token("tensor")
    P.ksum_tok = P.cgsb_tok = b.barrier_token("vector")
    send_tok = P.send_cb(ktot_toks[-1])
    if phase1_only:
        return send_tok
    if stop_after == "p1":
        return b.barrier_token("vector")
    return emit_inner_attn_phase2(ctx, P, xn_tok, hxn_tok, wslot_tok, load_head_w,
                                  stop_after=stop_after)


def emit_inner_attn_phase2(ctx, P, xn_tok, hxn_tok, wslot_tok, load_head_w, stop_after=None):
    b = ctx.b
    xnv = fm_view(P.xn)
    zv = fm_view(P.z)
    hxv = fm_view(P.haloxn, Tbuf=128)
    lv = P.lv
    bo_c = lambda c: lv[:, c:c + 1]
    b1_c = lambda j: lv[:, 24 + j:25 + j]
    b2_c = lambda c: lv[:, 56 + c:57 + c]
    l2g_c = lambda c: lv[:, 64 + c:65 + c]
    l2b_c = lambda c: lv[:, 72 + c:73 + c]
    bt = {}

    def mm(bank_region, pairs, waits=(), mark=True):
        return mm_group(b, bank_region, pairs, waits=waits, mark=mark)

    ph1_done = [b.barrier_token("vector"), b.barrier_token("scalar")]
    if not hasattr(P, "bt_sc"):
        P.bt_sc = {}
        P.bt_psc = P.bt_scm = P.bt_p16 = P.bt_pT = None
    load_head_w(0, 0)
    for h in range(4):
        slot = h % 2
        if h < 3:
            load_head_w(h + 1, (h + 1) % 2)
        wtok = P.wload[slot]
        qb, kb, vb = P.loc_q, P.loc_k, P.loc_v
        for tt in range(4):
            pq = ctx.pb[0 if tt % 2 == 0 else 1]
            wl = [xn_tok, wtok, bt.get(0 if tt % 2 == 0 else 1)] + \
                 ([P.bt_sc.get(tt % 2)] if tt < 2 else []) + \
                 (ph1_done if (h == 0 and tt == 0) else [])
            mm(pq[:, 0:512], [(P.wq_sb[slot][:, kc * 128:(kc + 1) * 128],
                               xnv(kc, tt * 512, (tt + 1) * 512)) for kc in range(8)], waits=wl)
            tq = b.op("scalar", lambda e, o=qb[:, tt * 512:(tt + 1) * 512], i=pq[:, 0:512]:
                      e.copy(o, i), waits=[b.barrier_token("tensor")], mark=True)
            bt[0 if tt % 2 == 0 else 1] = tq
        for tt in range(4):
            pk = ctx.pb[0 if tt % 2 == 0 else 1]
            wl = [bt.get(0 if tt % 2 == 0 else 1)]
            mm(pk[:, 0:512], [(P.wk_sb[slot][:, kc * 128:(kc + 1) * 128],
                               xnv(kc, tt * 512, (tt + 1) * 512)) for kc in range(8)], waits=wl)
            tk = b.op("scalar", lambda e, o=kb[:, tt * 512:(tt + 1) * 512], i=pk[:, 0:512]:
                      e.copy(o, i), waits=[b.barrier_token("tensor")], mark=True)
            bt[0 if tt % 2 == 0 else 1] = tk
        ph = ctx.pb[0]
        mm(ph[:, 0:128], [(P.wk_sb[slot][:, kc * 128:(kc + 1) * 128],
                           hxv(kc, 0, 128)) for kc in range(8)],
           waits=[hxn_tok, bt.get(0)])
        thk = b.op("scalar", lambda e, o=P.hk, i=ph[:, 0:128]: e.copy(o, i),
                   waits=[b.barrier_token("tensor")], mark=True)
        mm(ph[:, 128:256], [(hxv(kc, 0, 128),
                             P.wv_sb[slot][:, kc * 128:(kc + 1) * 128]) for kc in range(8)],
           waits=[thk])
        thv = b.op("scalar", lambda e, o=P.hv, i=ph[:, 128:256]: e.copy(o, i),
                   waits=[b.barrier_token("tensor")], mark=True)
        bt[0] = thv
        for g4 in range(4):
            pvv = ctx.pb[2 + g4 % 2]
            for k in range(4):
                g = g4 * 4 + k
                wl = [bt.get(2 + g4 % 2)] if k == 0 else []
                mm(pvv[:, k * 128:(k + 1) * 128],
                   [(xnv(kc, g * 128, (g + 1) * 128),
                     P.wv_sb[slot][:, kc * 128:(kc + 1) * 128]) for kc in range(8)],
                   waits=wl, mark=(k == 3))
            tv = b.op("scalar", lambda e, o=vb[:, g4 * 512:(g4 + 1) * 512], i=pvv[:, 0:512]:
                      e.copy(o, i), waits=[b.barrier_token("tensor")], mark=True)
            bt[2 + g4 % 2] = tv
        qdone = b.barrier_token("scalar")
        pav = ctx.pb[6]
        for w4 in range(4):
            par = w4 % 2
            pbA = ctx.pb[4 if par == 0 else 5]
            pbB = ctx.pb[0 if par == 0 else 1]
            ptb = ctx.pb[2 + par].bitcast(BF16)
            sc_tok = None
            for k in range(4):
                w = w4 * 4 + k
                pbX = pbA if k < 2 else pbB
                off = (k % 2) * 256
                prevk = P.hk if w == 0 else kb[:, (w - 1) * 128: w * 128]
                wl = ([qdone, thk, thv] if w4 == 0 else []) + \
                     ([P.bt_sc.get(par)] if k == 0 else [])
                b.op("tensor", lambda e, o=pbX[:, off:off + 128],
                     l=qb[:, w * 128:(w + 1) * 128], r=prevk:
                     e.matmul(o, l, r, start=True, stop=True), waits=wl)
                sc_tok = b.op("tensor", lambda e, o=pbX[:, off + 128:off + 256],
                              l=qb[:, w * 128:(w + 1) * 128], r=kb[:, w * 128:(w + 1) * 128]:
                              e.matmul(o, l, r, start=True, stop=True),
                              mark=(k == 1 or k == 3))
                if k == 1:
                    scA_tok = sc_tok
            mstd = P.mask2a[:, 256:512]
            if w4 == 0:
                t_e1 = b.op("vector", lambda e, o=P.psc4[:, 0:512], i=pbA[:, 0:512],
                            m=P.mask2a:
                            e.scalar_tensor_tensor(o, i, SC_LOC, m, ALU.mult, ALU.add),
                            waits=[scA_tok, P.bt_psc] + list(P.mask_toks), mark=True)
            else:
                for kk in range(2):
                    t_e1 = b.op("vector", lambda e, o=P.psc4[:, kk * 256:(kk + 1) * 256],
                                i=pbA[:, kk * 256:(kk + 1) * 256], m=mstd:
                                e.scalar_tensor_tensor(o, i, SC_LOC, m, ALU.mult, ALU.add),
                                waits=([scA_tok, P.bt_psc] if kk == 0 else []), mark=(kk == 1))
            for kk in range(2):
                t_e2 = b.op("vector", lambda e, o=P.psc4[:, 512 + kk * 256:768 + kk * 256],
                            i=pbB[:, kk * 256:(kk + 1) * 256], m=mstd:
                            e.scalar_tensor_tensor(o, i, SC_LOC, m, ALU.mult, ALU.add),
                            waits=([sc_tok] if kk == 0 else []), mark=(kk == 1))
            P.bt_sc[par] = t_e2
            mx = []
            for k in range(4):
                tm = b.op("vector", lambda e, o=P.scmisc[:, k:k + 1],
                          i=P.psc4[:, k * 256:(k + 1) * 256]:
                          e.reduce_max(o, i, axis=AX, negate=True),
                          waits=([t_e1] if k == 0 else []) + ([P.bt_scm] if k == 0 else []),
                          mark=True)
                mx.append(tm)
            ex = None
            for k in range(4):
                ex = b.op("scalar", lambda e, o=P.p16_4[:, k * 256:(k + 1) * 256],
                          i=P.psc4[:, k * 256:(k + 1) * 256], bi=P.scmisc[:, k:k + 1]:
                          e.activation(o, i, AF.Exp, bias=bi),
                          waits=[mx[k]] + ([P.bt_p16] if k == 0 else []), mark=True)
            P.bt_psc = ex
            sm_ = None
            for k in range(4):
                sm_ = b.op("vector", lambda e, o=P.scmisc[:, 4 + k:5 + k],
                           i=P.p16_4[:, k * 256:(k + 1) * 256]:
                           e.reduce_sum(o, i, axis=AX), waits=[ex] if k == 0 else (), mark=(k == 3))
            tiv = emit_inv(ctx, P.scmisc[:, 8:12], P.scmisc[:, 4:8], waits=[sm_])
            tn = None
            for k in range(4):
                tn = b.op("vector", lambda e, o=P.p16_4[:, k * 256:(k + 1) * 256],
                          r=P.scmisc[:, 8 + k:9 + k]:
                          e.tensor_scalar(o, o, r, None, ALU.mult),
                          waits=[tiv] if k == 0 else (), mark=(k == 3))
            P.bt_scm = tn
            ttr = None
            for k in range(4):
                b.op("tensor", lambda e, o=ptb[:, k * 256:k * 256 + 128],
                     i=P.p16_4[:, k * 256:k * 256 + 128]:
                     e.transpose(o, i, ctx.ident16),
                     waits=[tn, bt.get(2 + par)] if k == 0 else ())
                ttr = b.op("tensor", lambda e, o=ptb[:, k * 256 + 128:k * 256 + 256],
                           i=P.p16_4[:, k * 256 + 128:k * 256 + 256]:
                           e.transpose(o, i, ctx.ident16), mark=(k == 3))
            P.bt_p16 = ttr
            tev = b.op("vector", lambda e, o=P.pT16_4, i=ptb[:, 0:1024]: e.tensor_copy(o, i),
                       waits=[ttr, P.bt_pT], mark=True)
            bt[2 + par] = tev
            avk = None
            for k in range(4):
                w = w4 * 4 + k
                prevv = P.hv if w == 0 else vb[:, (w - 1) * 128: w * 128]
                reg = pav[:, k * 128:(k + 1) * 128]
                wl3 = [tev] + ([bt.get(6)] if k == 0 else [])
                b.op("tensor", lambda e, o=reg, l=prevv, r=P.pT16_4[:, k * 256:k * 256 + 128]:
                     e.matmul(o, l, r, start=True, stop=False), waits=wl3)
                avk = b.op("tensor", lambda e, o=reg, l=vb[:, w * 128:(w + 1) * 128],
                           r=P.pT16_4[:, k * 256 + 128:k * 256 + 256]:
                           e.matmul(o, l, r, start=False, stop=True), mark=(k == 3))
            P.bt_pT = avk
            teo = b.op("scalar", lambda e,
                       o=P.att_out[:, h * 2048 + w4 * 512: h * 2048 + (w4 + 1) * 512],
                       i=pav[:, 0:512]: e.copy(o, i), waits=[avk], mark=True)
            bt[6] = teo
        wslot_tok[slot] = b.barrier_token("tensor")
    bt[4] = bt[0] = P.bt_sc.get(0)
    bt[5] = bt[1] = P.bt_sc.get(1)
    att_loc_done = b.barrier_token("scalar")
    if stop_after == "loc":
        return att_loc_done

    def emit_wo_pass(heads, first_pass, done_tok):
        zlast = None
        for pi in range(0, len(heads), 2):
            pair = heads[pi:pi + 2]
            ltoks = []
            for k, h in enumerate(pair):
                ltoks.append(b.op("gpsimd", lambda e, o=P.wo_sb[k],
                                  i=P.wo.ap()[h * 128:(h + 1) * 128, :]:
                                  e.dma_start(out=o, in_=i),
                                  waits=[P.wo_slot_free[k], done_tok], dma=True))
            for dc in range(C):
                for tt in range(4):
                    bkid = 7 if (dc * 4 + tt) % 2 == 0 else 0
                    pw = ctx.pb[bkid]
                    wl = [done_tok, bt.get(bkid)] + ltoks
                    mm(pw[:, 0:512],
                       [(P.wo_sb[k][:, dc * 128:(dc + 1) * 128],
                         P.att_out[:, (h % 4) * 2048 + tt * 512:(h % 4) * 2048 + (tt + 1) * 512])
                        for k, h in enumerate(pair)], waits=wl)
                    if first_pass and pi == 0:
                        zlast = b.op("vector", lambda e, o=zv(dc, tt * 512, (tt + 1) * 512),
                                     i=pw[:, 0:512], bc=bo_c(dc):
                                     e.scalar_tensor_tensor(o, i, bc, o, ALU.add, ALU.add),
                                     waits=[b.barrier_token("tensor")], mark=True)
                    else:
                        zlast = b.op("vector", lambda e, o=zv(dc, tt * 512, (tt + 1) * 512),
                                     i=pw[:, 0:512]: e.tensor_tensor(o, o, i, ALU.add),
                                     waits=[b.barrier_token("tensor")], mark=True)
                    bt[bkid] = zlast
            P.wo_slot_free[0] = P.wo_slot_free[1] = b.barrier_token("tensor")
        return zlast

    P.wo_slot_free = [None, None]
    wo1_tok = emit_wo_pass([0, 1, 2, 3], True, att_loc_done)
    if stop_after == "wo1":
        return wo1_tok
    wo_free = b.barrier_token("tensor")

    muse = P.muse
    qb = P.loc_q
    qfm = P.loc_v
    for hh in range(4):
        h = 4 + hh
        slot = hh % 2
        load_head_w(h, slot, extra=[wo_free])
        wtok = P.wload[slot]
        tce = b.op("vector", lambda e, o=P.ceff32, i=P.tot_sb[:, hh * 129: hh * 129 + 128], m=muse:
                   e.tensor_scalar(o, i, m, None, ALU.mult),
                   waits=[P.tot_tok, P.muse_tok], mark=True)
        tce16 = b.op("gpsimd", lambda e, o=P.ceff16, i=P.ceff32: e.tensor_copy(o, i),
                     waits=[tce], mark=True)
        tkb = b.op("vector", lambda e, o=P.kb_sb[:, hh:hh + 1],
                   i=P.tot_sb[:, hh * 129 + 128: hh * 129 + 129], m=muse:
                   e.tensor_tensor(o, i, m, ALU.mult), mark=True)
        for g4 in range(4):
            par = g4 % 2
            pq = ctx.pb[par]
            for k in range(4):
                g = g4 * 4 + k
                wl = [xn_tok, wtok] + ([bt.get(par)] if k == 0 else [])
                mm(pq[:, k * 128:(k + 1) * 128],
                   [(xnv(kc, g * 128, (g + 1) * 128),
                     P.wq_sb[slot][:, kc * 128:(kc + 1) * 128]) for kc in range(8)],
                   waits=wl, mark=(k == 3))
            tq = b.op("scalar", lambda e, o=qb[:, g4 * 512:(g4 + 1) * 512], i=pq[:, 0:512]:
                      e.copy(o, i), waits=[b.barrier_token("tensor"), P.bt_scm], mark=True)
            bt[par] = tq
            mx = []
            for k in range(4):
                tm = b.op("vector", lambda e, o=P.scmisc[:, k:k + 1],
                          i=qb[:, (g4 * 4 + k) * 128:(g4 * 4 + k + 1) * 128]:
                          e.reduce_max(o, i, axis=AX, negate=True),
                          waits=[tq] if k == 0 else (), mark=True)
                mx.append(tm)
            ex = None
            for k in range(4):
                ex = b.op("scalar", lambda e, g=g4 * 4 + k,
                          o=qb[:, (g4 * 4 + k) * 128:(g4 * 4 + k + 1) * 128],
                          i=qb[:, (g4 * 4 + k) * 128:(g4 * 4 + k + 1) * 128],
                          bi=P.scmisc[:, k:k + 1]:
                          e.activation(o, i, AF.Exp, bias=bi), waits=[mx[k]], mark=True)
            sm_ = None
            for k in range(4):
                sm_ = b.op("vector", lambda e, o=P.scmisc[:, 4 + k:5 + k],
                           i=qb[:, (g4 * 4 + k) * 128:(g4 * 4 + k + 1) * 128]:
                           e.reduce_sum(o, i, axis=AX), waits=[ex] if k == 0 else (),
                           mark=(k == 3))
            tiv = emit_inv(ctx, P.scmisc[:, 8:12], P.scmisc[:, 4:8], waits=[sm_])
            tn = None
            for k in range(4):
                tn = b.op("vector", lambda e,
                          o=qb[:, (g4 * 4 + k) * 128:(g4 * 4 + k + 1) * 128],
                          r=P.scmisc[:, 8 + k:9 + k]:
                          e.tensor_scalar(o, o, r, None, ALU.mult),
                          waits=[tiv] if k == 0 else (), mark=(k == 3))
            P.bt_scm = tn
            pq2 = ctx.pb[2 + par].bitcast(BF16)
            ttr = None
            for k in range(4):
                ttr = b.op("tensor", lambda e, o=pq2[:, k * 128:(k + 1) * 128],
                           i=qb[:, (g4 * 4 + k) * 128:(g4 * 4 + k + 1) * 128]:
                           e.transpose(o, i, ctx.ident16),
                           waits=[tn, bt.get(2 + par)] if k == 0 else (), mark=(k == 3))
            tev = b.op("vector", lambda e, o=qfm[:, g4 * 512:(g4 + 1) * 512], i=pq2[:, 0:512]:
                       e.tensor_copy(o, i), waits=[ttr], mark=True)
            bt[2 + par] = tev
        pk3 = ctx.pb[3]
        t1 = b.op("tensor", lambda e, o=pk3[0:32, 0:128], i=P.ksum_sb[:, hh * 32:(hh + 1) * 32]:
                  e.transpose(o, i, ctx.ident32), waits=[bt.get(3), P.ksum_tok], mark=True)
        t2 = b.op("vector", lambda e, o=P.ksum_um, i=pk3[0:32, 0:128]: e.tensor_copy(o, i),
                  waits=[t1], mark=True)
        t3 = b.op("tensor", lambda e, o=pk3[:, 128:160], l=P.ksum_um, r=ctx.tri32:
                  e.matmul(o, l, r, start=True, stop=True), waits=[t2], mark=True)
        t4 = b.op("vector", lambda e, o=P.r_sb, i=pk3[:, 128:160], kb2=P.kb_sb[:, hh:hh + 1]:
                  e.tensor_scalar(o, i, kb2, 1e-6, ALU.add, ALU.add), waits=[t3, tkb], mark=True)
        t5 = emit_inv(ctx, P.r_sb, P.r_sb, waits=[t4])
        bt[3] = t5
        for u in range(32):
            tqs = b.op("vector", lambda e, o=qfm[:, u * 64:(u + 1) * 64], r=P.r_sb[:, u:u + 1]:
                       e.tensor_scalar(o, o, r, None, ALU.mult),
                       waits=[t5] if u == 0 else (), mark=(u == 31))
        qs_done = tqs
        crun_tok = tce16
        for g in range(4):
            if g > 0:
                if g == 1:
                    tcr = b.op("vector", lambda e, o=P.crun32, a=P.ceff32,
                               bb=cg_view(P, hh, 0): e.tensor_tensor(o, a, bb, ALU.add),
                               waits=[P.cgsb_tok], mark=True)
                else:
                    tcr = b.op("vector", lambda e, o=P.crun32, bb=cg_view(P, hh, g - 1):
                               e.tensor_tensor(o, o, bb, ALU.add), mark=True)
                crun_tok = b.op("gpsimd", lambda e, o=P.crun16, i=P.crun32: e.tensor_copy(o, i),
                                waits=[tcr], mark=True)
            lhs0 = P.ceff16 if g == 0 else P.crun16
            pt5 = ctx.pb[5].bitcast(BF16)
            tbk = None
            for j in range(4):
                tbk = b.op("tensor", lambda e, o=pt5[:, j * 128:(j + 1) * 128],
                           i=P.lin_ktm[:, hh * 2048 + (g * 4 + j) * 128: hh * 2048 + (g * 4 + j + 1) * 128]:
                           e.transpose(o, i, ctx.ident16),
                           waits=[bt.get(5)] if j == 0 else (), mark=(j == 3))
            tbe = b.op("vector", lambda e, o=P.bk16, i=pt5[:, 0:512]: e.tensor_copy(o, i),
                       waits=[tbk], mark=True)
            bt[5] = tbe
            pO = ctx.pb[6]
            tO0 = b.op("tensor", lambda e, o=pO[:, 0:512], l=lhs0, r=qfm[:, g * 512:(g + 1) * 512]:
                       e.matmul(o, l, r, start=True, stop=False),
                       waits=[qs_done, crun_tok, bt.get(6)], mark=True)
            for j in range(4):
                pA = ctx.pb[4]
                tA = b.op("tensor", lambda e, o=pA[:, 0:512], l=P.bk16[:, j * 128:(j + 1) * 128],
                          r=qfm[:, g * 512:(g + 1) * 512]:
                          e.matmul(o, l, r, start=True, stop=True),
                          waits=[tbe, bt.get(4)], mark=True)
                tm0 = b.op("gpsimd", lambda e, o=P.A16: e.memset(o, 0.0),
                           waits=[P.A16_free], mark=True)
                c0 = (2 * j + 1) * 64
                tc1 = b.op("scalar", lambda e, o=P.A16[0:64, c0:512], i=pA[0:64, c0:512]:
                           e.copy(o, i), waits=[tA, tm0], mark=True)
                tc2 = tc1
                if j < 3:
                    c1 = (2 * j + 2) * 64
                    tc2 = b.op("scalar", lambda e, o=P.A16[64:128, c1:512], i=pA[64:128, c1:512]:
                               e.copy(o, i), mark=True)
                bt[4] = tc2
                tOj = b.op("tensor", lambda e, o=pO[:, 0:512],
                           l=P.lin_vtm[:, hh * 2048 + (g * 4 + j) * 128: hh * 2048 + (g * 4 + j + 1) * 128],
                           r=P.A16, p=(j == 3):
                           e.matmul(o, l, r, start=False, stop=p),
                           waits=[tc2], mark=True)
                P.A16_free = tOj
            teo = b.op("scalar", lambda e, o=P.att_out[:, hh * 2048 + g * 512: hh * 2048 + (g + 1) * 512],
                       i=pO[:, 0:512]: e.copy(o, i), waits=[tOj], mark=True)
            bt[6] = teo
        wslot_tok[slot] = b.barrier_token("tensor")
    att_lin_done = b.barrier_token("scalar")
    if stop_after == "lin":
        return att_lin_done
    zadd2 = emit_wo_pass([4, 5, 6, 7], False, att_lin_done)
    lin_q_done = b.barrier_token("tensor")
    if stop_after == "wo2":
        return zadd2

    _, xn2_tok = emit_ln(ctx, zv, T, y16_c=xnv, g_col=l2g_c, b_col=l2b_c,
                         extra_wait=[zadd2, lin_q_done])
    ffl_tok = [None, None]

    def load_ffn_half(half, slot, extra=()):
        wl = [P.ff_free[slot]] + list(extra)
        b.op("gpsimd", lambda e, o=P.w1h_sb[slot].rearrange("p (kc h2) -> p kc h2", kc=8),
             i=P.w1p.ap()[:, :, half * 2048:(half + 1) * 2048].rearrange("kc kp h -> kp kc h"):
             e.dma_start(out=o, in_=i), waits=wl, dma=True)
        t = b.op("gpsimd", lambda e, o=P.w2h_sb[slot].rearrange("p (kc d2) -> p kc d2", kc=16),
                 i=P.w2p.ap()[:, half * 16:(half + 1) * 16, :]:
                 e.dma_start(out=o, in_=i), dma=True)
        ffl_tok[slot] = t
        return t

    load_ffn_half(0, 0, extra=[zadd2, lin_q_done])
    zffn = None
    for half in range(2):
        slot = half
        wtok = ffl_tok[slot]
        for tt in range(4):
            for j in range(16):
                pj = ctx.pb[j % 2]
                wl = [xn2_tok, wtok, bt.get(j % 2)]
                mm(pj[:, 0:512], [(P.w1h_sb[slot][:, kc * 2048 + j * 128: kc * 2048 + (j + 1) * 128],
                                   xnv(kc, tt * 512, (tt + 1) * 512)) for kc in range(8)],
                   waits=wl)
                tg = b.op("scalar", lambda e, o=P.h1_sb[:, j * 512:(j + 1) * 512], i=pj[:, 0:512],
                          bi=b1_c(half * 16 + j): e.activation(o, i, AF.Gelu, bias=bi),
                          waits=[b.barrier_token("tensor")], mark=True)
                bt[j % 2] = tg
            h1_done = b.barrier_token("scalar")
            for dcq in range(2):
                for dc4 in range(4):
                    dc = dcq * 4 + dc4
                    pw = ctx.pb[2 + dc4]
                    wl = [h1_done, bt.get(2 + dc4)]
                    mm(pw[:, 0:512],
                       [(P.w2h_sb[slot][:, j * 1024 + dc * 128: j * 1024 + (dc + 1) * 128],
                         P.h1_sb[:, j * 512:(j + 1) * 512]) for j in range(16)],
                       waits=wl)
                    if half == 0:
                        zffn = b.op("vector", lambda e, o=zv(dc, tt * 512, (tt + 1) * 512),
                                    i=pw[:, 0:512], bc=b2_c(dc):
                                    e.scalar_tensor_tensor(o, i, bc, o, ALU.add, ALU.add),
                                    waits=[b.barrier_token("tensor")], mark=True)
                    else:
                        zffn = b.op("vector", lambda e, o=zv(dc, tt * 512, (tt + 1) * 512),
                                    i=pw[:, 0:512]: e.tensor_tensor(o, o, i, ALU.add),
                                    waits=[b.barrier_token("tensor")], mark=True)
                    bt[2 + dc4] = zffn
        P.ff_free[slot] = b.barrier_token("tensor")
        if half == 0:
            load_ffn_half(1, 1, extra=[P.ff_free[0], b.barrier_token("scalar")])
    P.wslot_prev = b.barrier_token("tensor")
    return zffn


def cg_view(P, hh, g):
    return P.cg_sb[:, hh * 512 + g * 128: hh * 512 + (g + 1) * 128]


ARENA_COLS = 50880


def alloc_inner_P(arena):
    def bv(off, cols, dt):
        if dt == BF16:
            return arena[:, off // 2: off // 2 + cols]
        return arena[:, off // 2: off // 2 + 2 * cols].bitcast(F32)
    P = NS()
    P.lin_ktm = bv(0, 8192, BF16)
    P.lin_vtm = bv(16384, 8192, BF16)
    P.att_out = bv(32768, 8192, BF16)
    P.cg_sb = bv(49152, 2048, F32)
    P.loc_q = bv(57344, 2048, BF16)
    P.loc_k = bv(61440, 2048, BF16)
    P.loc_v = bv(65536, 2048, BF16)
    P.wq_sb = [bv(69632, 1024, BF16), bv(71680, 1024, BF16)]
    P.wk_sb = [bv(73728, 1024, BF16), bv(75776, 1024, BF16)]
    P.wv_sb = [bv(77824, 1024, BF16), bv(79872, 1024, BF16)]
    P.wo_sb = [bv(81920, 1024, BF16), bv(83968, 1024, BF16)]
    P.halo_sb = bv(86016, 1024, F32)
    P.haloxn = bv(90112, 1024, BF16)
    P.tot_sb = bv(92160, 516, F32)
    P.totsend_sb = bv(94272, 516, F32)
    P.ksum_sb = bv(96384, 128, F32)
    SCR = 96896
    P.sc1 = bv(SCR, 512, F32)
    P.hk = bv(SCR + 2048, 128, BF16)
    P.hv = bv(SCR + 2304, 128, BF16)
    P.bk16 = bv(SCR, 512, BF16)
    P.A16 = bv(SCR + 1024, 512, BF16)
    P.psc4 = bv(81920, 1024, F32)
    P.p16_4 = bv(SCR, 1024, BF16)
    P.pT16_4 = bv(86016, 1024, BF16)
    base2 = SCR + 2560
    P.scmisc = bv(base2, 16, F32)
    P.ceff32 = bv(base2 + 64, 128, F32)
    P.crun32 = bv(base2 + 576, 128, F32)
    P.ceff16 = bv(base2 + 1088, 128, BF16)
    P.crun16 = bv(base2 + 1344, 128, BF16)
    P.kb_sb = bv(base2 + 1600, 4, F32)
    P.ksum_um = bv(base2 + 1664, 128, F32)[0:32, :]
    P.r_sb = bv(base2 + 2176, 32, F32)
    P.w1h_sb = [bv(0, 16384, BF16)] * 2
    P.w2h_sb = [bv(32768, 16384, BF16)] * 2
    P.h1_sb = bv(65536, 8192, BF16)
    P.A16_free = None
    P.ff_free = [None, None]
    P.cg_free = None
    P.ks_free = None
    return P


def inner_dram_tensors(nc, with_exchange_in=True, lite=False):
    d = NS()
    d.wq = nc.dram_tensor("wq", [8, 128, 1024], F32, kind="ExternalInput")
    d.wk = nc.dram_tensor("wk", [8, 128, 1024], F32, kind="ExternalInput")
    d.wv = nc.dram_tensor("wv", [8, 128, 1024], F32, kind="ExternalInput")
    if not lite:
        d.wo = nc.dram_tensor("wo", [1024, 1024], F32, kind="ExternalInput")
        d.w1p = nc.dram_tensor("w1p", [8, 128, 4096], F32, kind="ExternalInput")
        d.w2p = nc.dram_tensor("w2p", [128, 32, 1024], F32, kind="ExternalInput")
    else:
        d.wo = d.w1p = d.w2p = None
    d.lvd = nc.dram_tensor("lvd", [128, 80], F32, kind="ExternalInput")
    if with_exchange_in:
        d.halo_in = nc.dram_tensor("halo_in", [8, 128, 128], F32, kind="ExternalInput")
        d.tot_in = nc.dram_tensor("tot_in", [4, 128, 129], F32, kind="ExternalInput")
        d.muse_in = nc.dram_tensor("muse_in", [128, 1], F32, kind="ExternalInput")
        d.maskw0_in = nc.dram_tensor("maskw0_in", [128, 256], F32, kind="ExternalInput")
    return d


def emit_inner_common_setup(ctx, P, d):
    b = ctx.b
    P.lv = ctx.sb("lv", [128, 80], F32)
    P.lv_tok = b.op("sync", lambda e: e.dma_start(out=P.lv, in_=d.lvd.ap()), dma=True)
    P.wq, P.wk, P.wv, P.wo, P.w1p, P.w2p = d.wq, d.wk, d.wv, d.wo, d.w1p, d.w2p
    P.maskstd = ctx.sb("mask_std", [128, 256], F32)
    b.op("gpsimd", lambda e: e.memset(P.maskstd, 0.0))
    t1 = b.op("gpsimd", lambda e: e.affine_select(P.maskstd, P.maskstd, [[-1, 256]],
                                                  ALU.is_ge, NEG, base=128,
                                                  channel_multiplier=1), mark=True)
    P.maskw0 = ctx.sb("maskw0", [128, 256], F32)
    t2 = b.op("sync", lambda e: e.dma_start(out=P.maskw0, in_=d.maskw0_in.ap()), dma=True)
    P.mask_toks = [t1, t2]
    P.muse = ctx.sb("muse", [128, 1], F32)
    P.muse_tok = b.op("sync", lambda e: e.dma_start(out=P.muse, in_=d.muse_in.ap()), dma=True)
    P.halo_tok = b.op("sync", lambda e, o=P.halo_sb.rearrange("p (c t) -> p c t", c=8),
                      i=d.halo_in.ap().rearrange("c p t -> p c t"):
                      e.dma_start(out=o, in_=i), dma=True)
    P.tot_tok = b.op("sync", lambda e, o=P.tot_sb.rearrange("p (h r) -> p h r", h=4),
                     i=d.tot_in.ap().rearrange("h d r -> d h r"):
                     e.dma_start(out=o, in_=i), dma=True)


def build_pre():
    nc = _mk_nc()
    xs = nc.dram_tensor("xs", [T, D], F32, kind="ExternalInput")
    z_rel = nc.dram_tensor("z", [C, 128, T], F32, kind="ExternalOutput")
    x0_rel = nc.dram_tensor("x0", [C, 128, T], F32, kind="ExternalOutput")
    ctx = Ctx(nc)
    ctx.emit_consts()
    x0 = ctx.sb("x0f", [128, C * T], F32)
    z = ctx.sb("zf", [128, C * T], F32)
    tload = emit_tm_to_fm(ctx, xs.ap(), fm_view(x0), T)
    t32, _ = emit_ln(ctx, fm_view(x0), T, y32_c=fm_view(z), extra_wait=[tload])
    d1 = fm_relay_out(ctx, x0, x0_rel, waits=[tload])
    d2 = fm_relay_out(ctx, z, z_rel, waits=[t32])
    ctx.b.wait("sync", [d1, d2])
    ctx.b.replay()
    return nc


def build_minner(stop_after=None, dumps=()):
    nc = _mk_nc()
    z_in = nc.dram_tensor("z_in", [C, 128, T], F32, kind="ExternalInput")
    z_out = nc.dram_tensor("z_out", [C, 128, T], F32, kind="ExternalOutput")
    d = inner_dram_tensors(nc)
    ctx = Ctx(nc)
    ctx.emit_consts(alloc_lnf32=False)
    z = ctx.sb("z", [128, C * T], F32)
    xn = ctx.sb("xn", [128, C * T], BF16)
    arena = ctx.sb("arena", [128, ARENA_COLS], BF16)
    P = alloc_inner_P(arena)
    ctx.ln_f32 = P.sc1
    P.z, P.xn = z, xn
    emit_inner_common_setup(ctx, P, d)
    P.z_wait = fm_relay_in(ctx, z, z_in, waits=[ctx._consts_done])
    P.send_cb = lambda tok: None
    zffn = emit_inner_attn_common(ctx, P, stop_after=stop_after)
    dd = fm_relay_out(ctx, z, z_out, waits=[zffn])
    waitlist = [dd]
    for (nm, src_attr, c0, cols) in dumps:
        dt_out = nc.dram_tensor(nm, [128, cols], F32, kind="ExternalOutput")
        if src_attr == "arena32":
            src = arena[:, c0:c0 + 2 * cols].bitcast(F32)
        else:
            src = dict(z=z, xn=xn, arena16=arena)[src_attr][:, c0:c0 + cols]
        if src.dtype == F32:
            dtk = ctx.b.op("sync", lambda e, o=dt_out.ap(), i=src: e.dma_start(out=o, in_=i),
                           waits=[zffn], dma=True)
        else:
            dtk = ctx.b.op("gpsimd", lambda e, o=dt_out.ap(), i=src: e.dma_start(out=o, in_=i),
                           waits=[zffn], dma=True)
        waitlist.append(dtk)
        ctx.b.wait("gpsimd" if src.dtype != F32 else "sync", [dtk])
    ctx.b.wait("sync", [dd])
    ctx.b.replay()
    return nc


def build_finner():
    nc = _mk_nc()
    z_in = nc.dram_tensor("z_in", [C, 128, T], F32, kind="ExternalInput")
    tot_out = nc.dram_tensor("tot_out", [4, 128, 129], F32, kind="ExternalOutput")
    d = inner_dram_tensors(nc, with_exchange_in=False, lite=True)
    ctx = Ctx(nc)
    ctx.emit_consts(alloc_lnf32=False)
    z = ctx.sb("z", [128, C * T], F32)
    xn = ctx.sb("xn", [128, C * T], BF16)
    arena = ctx.sb("arena", [128, ARENA_COLS], BF16)
    P = alloc_inner_P(arena)
    ctx.ln_f32 = P.sc1
    P.z, P.xn = z, xn
    b = ctx.b
    P.lv = ctx.sb("lv", [128, 80], F32)
    P.lv_tok = b.op("sync", lambda e: e.dma_start(out=P.lv, in_=d.lvd.ap()), dma=True)
    P.wq, P.wk, P.wv, P.wo, P.w1p, P.w2p = d.wq, d.wk, d.wv, d.wo, d.w1p, d.w2p
    P.z_wait = fm_relay_in(ctx, z, z_in, waits=[ctx._consts_done])

    def send_cb(tok):
        dd = b.op("sync", lambda e, o=tot_out.ap().rearrange("h d r -> d h r"),
                  i=P.totsend_sb.rearrange("p (h r) -> p h r", h=4):
                  e.dma_start(out=o, in_=i), waits=[tok], dma=True)
        b.wait("sync", [dd])
        return dd
    P.send_cb = send_cb
    emit_inner_attn_common(ctx, P, phase1_only=True)
    ctx.b.replay()
    return nc
    nc = _mk_nc()
    xs = nc.dram_tensor("xs", [T, D], F32, kind="ExternalInput")
    z_rel = nc.dram_tensor("z", [C, 128, T], F32, kind="ExternalOutput")
    x0_rel = nc.dram_tensor("x0", [C, 128, T], F32, kind="ExternalOutput")
    ctx = Ctx(nc)
    ctx.emit_consts()
    x0 = ctx.sb("x0f", [128, C * T], F32)
    z = ctx.sb("zf", [128, C * T], F32)
    tload = emit_tm_to_fm(ctx, xs.ap(), fm_view(x0), T)
    t32, _ = emit_ln(ctx, fm_view(x0), T, y32_c=fm_view(z), extra_wait=[tload])
    d1 = fm_relay_out(ctx, x0, x0_rel, waits=[tload])
    d2 = fm_relay_out(ctx, z, z_rel, waits=[t32])
    ctx.b.wait("sync", [d1, d2])
    ctx.b.replay()
    return nc





MOE_SUPERS = [4, 4, 4, 4, 4, 2]


def build_moe(stop_after=None, dbg=False):
    nc = _mk_nc()
    z0_rel = nc.dram_tensor("z0_rel", [C, 128, T], F32, kind="ExternalInput")
    x0_rel = nc.dram_tensor("x0_rel", [C, 128, T], F32, kind="ExternalInput")
    z4_rel = nc.dram_tensor("z4_rel", [C, 128, T], F32, kind="ExternalInput")
    w1t = nc.dram_tensor("w1t", [9, 128, 22, 1024], F32, kind="ExternalInput")
    w3t = nc.dram_tensor("w3t", [9, 128, 22, 1024], F32, kind="ExternalInput")
    w2t = nc.dram_tensor("w2t", [9, 128, 22, 1024], F32, kind="ExternalInput")
    gt = nc.dram_tensor("gt", [128, 72], F32, kind="ExternalInput")
    gb = nc.dram_tensor("gb", [9, 1], F32, kind="ExternalInput")
    x1_rel = nc.dram_tensor("x1_rel", [C, 128, T], F32, kind="ExternalOutput")
    ctx = Ctx(nc)
    ctx.emit_consts(alloc_ln=False)
    b = ctx.b
    base = ctx.sb("base", [128, C * T], F32)
    xn0b = ctx.sb("xn0b", [128, C * T], BF16)
    dbg_out = None
    if dbg:
        dbg_out = (nc.dram_tensor("d_scores", [9, T], F32, kind="ExternalOutput"),
                   nc.dram_tensor("d_cw", [9, T], F32, kind="ExternalOutput"),
                   nc.dram_tensor("d_stm", [128, 9], F32, kind="ExternalOutput"),
                   nc.dram_tensor("d_mx8", [128, 8], F32, kind="ExternalOutput"),
                   nc.dram_tensor("d_idxf", [128, 2], F32, kind="ExternalOutput"),
                   nc.dram_tensor("d_mis2", [128, 8], F32, kind="ExternalOutput"),
                   nc.dram_tensor("d_e1b", [128, 9], F32, kind="ExternalOutput"),
                   nc.dram_tensor("d_cwt", [128, 9], F32, kind="ExternalOutput"))
    emit_moe(ctx, base, xn0b, z0_rel, x0_rel, z4_rel, w1t, w3t, w2t, gt, gb, x1_rel,
             stop_after=stop_after, dbg_out=dbg_out)
    ctx.b.replay()
    return nc


def emit_moe(ctx, base, xn0b, z0_rel, x0_rel, z4_rel, w1t, w3t, w2t, gt, gb, x1_rel,
             fused_in=None, stop_after=None, dbg_out=None, base_tok=None, skip_out=False):
    nc = ctx.nc
    b = ctx.b
    bt = {}
    bv_ = fm_view(base)
    xv = fm_view(xn0b)
    if base_tok is None:
        t_base = fm_relay_in(ctx, base, z4_rel, waits=[ctx._consts_done])
    else:
        t_base = base_tok
    t_xn = b.op("gpsimd", lambda e, o=xn0b.rearrange("p (c t) -> p c t", c=C),
                i=z0_rel.ap().rearrange("c p t -> p c t"): e.dma_start(out=o, in_=i),
                waits=[ctx._consts_done, base_tok], dma=True)
    x0tmp = ctx.sb("x0tmp", [128, T], F32)
    add_tok = None
    for c in range(C):
        td = b.op("sync", lambda e, o=x0tmp, i=x0_rel.ap()[c]: e.dma_start(out=o, in_=i),
                  waits=[t_base, add_tok], dma=True)
        add_tok = b.op("vector", lambda e, o=bv_(c, 0, T), i=x0tmp:
                       e.tensor_tensor(o, o, i, ALU.add), waits=[td], mark=True)
    gtsb = ctx.sb("gtsb", [128, 72], F32)
    gbsb = ctx.sb("gbsb", [9, 1], F32)
    tg1 = b.op("sync", lambda e, o=gtsb, i=gt.ap():
               e.dma_start(out=o, in_=i), dma=True)
    tg2 = b.op("sync", lambda e, o=gbsb, i=gb.ap(): e.dma_start(out=o, in_=i), dma=True)
    gf32 = ctx.sb("gf32", [128, 2 * 512], F32)
    scores = ctx.sb("scores", [9, T], F32)
    gld = {}
    for tt in range(4):
        for kc in range(C):
            sl_ = (tt * C + kc) % 2
            gsl = gf32[:, sl_ * 512:(sl_ + 1) * 512]
            tld = b.op("sync", lambda e, o=gsl, i=z0_rel.ap()[kc][:, tt * 512:(tt + 1) * 512]:
                       e.dma_start(out=o, in_=i), waits=[gld.get(sl_)], dma=True)
            tmm = b.op("tensor", lambda e, o=ctx.pb[6][0:9, 0:512], l=gtsb[:, kc * 9:(kc + 1) * 9],
                       r=gsl, st=(kc == 0), sp=(kc == C - 1):
                       e.matmul(o, l, r, start=st, stop=sp),
                       waits=[tld, tg1] + ([bt.get(6)] if kc == 0 else []), mark=True)
            gld[sl_] = tmm
        tsg = b.op("scalar", lambda e, o=scores[:, tt * 512:(tt + 1) * 512],
                   i=ctx.pb[6][0:9, 0:512], bi=gbsb:
                   e.activation(o, i, AF.Sigmoid, bias=bi), waits=[tmm, tg2], mark=True)
        bt[6] = tsg
    if stop_after == "gate":
        dd = fm_relay_out(ctx, base, x1_rel, waits=[b.barrier_token("scalar")])
        b.wait("sync", [dd])
        return dd
    cw16 = ctx.sb("cw16", [9, T], BF16)
    stm = ctx.sb("stm", [128, 9], F32)
    e1b = ctx.sb("e1b", [128, 9], F32)
    s2b = ctx.sb("s2b", [128, 9], F32)
    cwt = ctx.sb("cwt", [128, 9], F32)
    mis2 = ctx.sb("mis2", [128, 8], F32)
    mx8 = ctx.sb("mx8", [128, 8], F32)
    idx2 = ctx.sb("idx2", [128, 8], U32)
    ctx.iota9 = ctx.sb("iota9", [128, 9], F32)
    idxf = ctx.sb("idxf", [128, 2], F32)
    b.op("gpsimd", lambda e: e.iota(ctx.iota9, [[1, 9]], channel_multiplier=0, allow_small_or_imprecise_dtypes=True))
    prev_cw_tr = None
    prev_tiv = None
    for t16 in range(16):
        ttr = b.op("tensor", lambda e, o=ctx.pb[6][:, 0:9],
                   i=scores[:, t16 * 128:(t16 + 1) * 128], idn=ctx.ident32[0:9, 0:9]:
                   e.transpose(o, i, idn), waits=[bt.get(6)], mark=True)
        tev = b.op("scalar", lambda e, o=stm, i=ctx.pb[6][:, 0:9]: e.copy(o, i),
                   waits=[ttr, prev_cw_tr], mark=True)
        bt[6] = tev
        trm = b.op("vector", lambda e, o=mis2[:, 0:1], i=stm: e.reduce_max(o, i, axis=AX),
                   waits=[tev], mark=True)
        te1 = b.op("gpsimd", lambda e, o=e1b, i=stm, m=mis2[:, 0:1]:
                   e.tensor_scalar(o, i, m, None, ALU.is_equal), waits=[trm], mark=True)
        ts2 = b.op("vector", lambda e, o=s2b, i=e1b, j=stm:
                   e.scalar_tensor_tensor(o, i, NEG, j, ALU.mult, ALU.add),
                   waits=[te1], mark=True)
        tbb = b.op("scalar", lambda e: e.nop(), waits=[ts2], mark=True)
        trm2 = b.op("vector", lambda e, o=mis2[:, 1:2], i=s2b: e.reduce_max(o, i, axis=AX),
                    waits=[tbb], mark=True)
        te2 = b.op("gpsimd", lambda e, o=s2b, i=s2b, m=mis2[:, 1:2]:
                   e.tensor_scalar(o, i, m, None, ALU.is_equal), waits=[trm2], mark=True)
        tdn = b.op("gpsimd", lambda e, o=mis2[:, 2:3], a=mis2[:, 0:1], c2=mis2[:, 1:2]:
                   e.tensor_tensor(o, a, c2, ALU.add), waits=[prev_tiv], mark=True)
        trd = b.op("gpsimd", lambda e, o=mis2[:, 2:3]:
                   e.tensor_scalar(o, o, 1e-20, None, ALU.add), mark=True)
        tiv = emit_inv(ctx, mis2[:, 3:4], mis2[:, 2:3], waits=[trd])
        prev_tiv = tiv
        tw1 = b.op("vector", lambda e, o=mis2[:, 4:5], a=mis2[:, 0:1], r=mis2[:, 3:4]:
                   e.tensor_tensor(o, a, r, ALU.mult), waits=[tiv], mark=True)
        tw2 = b.op("vector", lambda e, o=mis2[:, 5:6], a=mis2[:, 1:2], r=mis2[:, 3:4]:
                   e.tensor_tensor(o, a, r, ALU.mult), mark=True)
        tc1 = b.op("gpsimd", lambda e, o=cwt, i=e1b, w1c=mis2[:, 4:5]:
                   e.tensor_scalar(o, i, w1c, None, ALU.mult), waits=[tw2, te2], mark=True)
        tcw = b.op("vector", lambda e, o=cwt, i=s2b, w2c=mis2[:, 5:6]:
                   e.scalar_tensor_tensor(o, i, w2c, o, ALU.mult, ALU.add),
                   waits=[tc1], mark=True)
        ttr2 = b.op("tensor", lambda e, o=ctx.pb[7][0:9, 0:128],
                    i=cwt, idn=ctx.ident32:
                    e.transpose(o, i, idn), waits=[tcw, bt.get(7)], mark=True)
        prev_cw_tr = ttr2
        tcv = b.op("vector", lambda e, o=cw16[0:9, t16 * 128:(t16 + 1) * 128],
                   i=ctx.pb[7][0:9, 0:128]: e.tensor_copy(o, i), waits=[ttr2], mark=True)
        bt[7] = tcv
    cw_done = b.barrier_token("vector")
    if dbg_out is not None:
        d1 = b.op("sync", lambda e, o=dbg_out[0].ap(), i=scores: e.dma_start(out=o, in_=i),
                  waits=[cw_done], dma=True)
        d2 = b.op("gpsimd", lambda e, o=dbg_out[1].ap(), i=cw16: e.dma_start(out=o, in_=i),
                  waits=[cw_done], dma=True)
        for dd_, src_ in [(2, stm), (3, mx8), (5, mis2), (6, e1b), (7, cwt)]:
            d1 = b.op("sync", lambda e, o=dbg_out[dd_].ap(), i=src_: e.dma_start(out=o, in_=i),
                      waits=[cw_done], dma=True)
        d3 = b.op("sync", lambda e, o=dbg_out[4].ap(), i=idxf: e.dma_start(out=o, in_=i),
                  waits=[cw_done], dma=True)
        b.wait("sync", [d1, d2, d3])
    HCOF = [0, 4, 8, 12, 16, 20]
    w1sb = [ctx.sb(f"w1sb{s}", [128, 4 * 1024], BF16) for s in range(2)]
    w3sb = [ctx.sb(f"w3sb{s}", [128, 4 * 1024], BF16) for s in range(2)]
    w2sb = [ctx.sb(f"w2sb{s}", [128, 4 * 1024], BF16) for s in range(2)]
    CWb = ctx.sb("CWb", [128, T], BF16)
    cwrow = ctx.sb("cwrow", [1, T], BF16)
    tb1 = ctx.sb("tb1", [128, 2 * 512], BF16)
    tb2 = ctx.sb("tb2", [128, 4 * 512], BF16)
    slot_free = [None, None]
    wload = [None, None]

    def load_super(e_, s_, slot):
        hc0 = HCOF[s_]
        hcn = MOE_SUPERS[s_]
        wl = [slot_free[slot]]
        b.op("gpsimd", lambda e, o=w1sb[slot][:, 0:hcn * 1024].rearrange(
            "p (hj x) -> p hj x", hj=hcn),
            i=w1t.ap()[e_][:, hc0:hc0 + hcn, :]:
            e.dma_start(out=o, in_=i), waits=wl, dma=True)
        b.op("gpsimd", lambda e, o=w3sb[slot][:, 0:hcn * 1024].rearrange(
            "p (hj x) -> p hj x", hj=hcn),
            i=w3t.ap()[e_][:, hc0:hc0 + hcn, :]:
            e.dma_start(out=o, in_=i), dma=True)
        t = b.op("gpsimd", lambda e, o=w2sb[slot][:, 0:hcn * 1024].rearrange(
            "p (hj x) -> p hj x", hj=hcn),
            i=w2t.ap()[e_][:, hc0:hc0 + hcn, :]:
            e.dma_start(out=o, in_=i), dma=True)
        wload[slot] = t
        return t

    NS_ = len(MOE_SUPERS)
    load_super(0, 0, 0)
    prev_cwb = None
    prev_phaseB = None
    sidx = 0
    prev_bc = None
    for e_ in range(9):
        trow = b.op("gpsimd", lambda e, o=cwrow, i=cw16[e_:e_ + 1, :]:
                    e.dma_start(out=o, in_=i), waits=[cw_done, prev_bc], dma=True)
        cwb_toks = []
        for tt4 in range(4):
            tbc = b.op("tensor", lambda e, o=ctx.pb[7][:, 0:512], l=ctx.ones1_16,
                       r=cwrow[0:1, tt4 * 512:(tt4 + 1) * 512]:
                       e.matmul(o, l, r, start=True, stop=True),
                       waits=[trow, bt.get(7)], mark=True)
            tce = b.op("scalar", lambda e, o=CWb[:, tt4 * 512:(tt4 + 1) * 512],
                       i=ctx.pb[7][:, 0:512]: e.copy(o, i), waits=[tbc, prev_cwb], mark=True)
            bt[7] = tce
            cwb_toks.append(tce)
        cwb_tok = cwb_toks[-1]
        prev_bc = cwb_tok
        for s_ in range(NS_):
            slot = sidx % 2
            if sidx + 1 < 9 * NS_:
                ne, ns2 = (e_, s_ + 1) if s_ + 1 < NS_ else (e_ + 1, 0)
                load_super(ne, ns2, (sidx + 1) % 2)
            wtok = wload[slot]
            hcn = MOE_SUPERS[s_]
            for tt in range(4):
                hstore_tok = None
                for hj in range(hcn):
                    bka = 4 + hj % 2
                    bkb = 6 + hj % 2
                    ph1 = ctx.pb[bka]
                    ph3 = ctx.pb[bkb]
                    mm_group(b, ph1[:, 0:512],
                             [(w1sb[slot][:, (hj * 8 + kc) * 128:(hj * 8 + kc + 1) * 128],
                               xv(kc, tt * 512, (tt + 1) * 512)) for kc in range(C)],
                             waits=[t_xn, wtok, bt.get(bka)])
                    mm_group(b, ph3[:, 0:512],
                             [(w3sb[slot][:, (hj * 8 + kc) * 128:(hj * 8 + kc + 1) * 128],
                               xv(kc, tt * 512, (tt + 1) * 512)) for kc in range(C)],
                             waits=[bt.get(bkb)])
                    tmm = b.barrier_token("tensor")
                    tbs = tb1[:, (hj % 2) * 512:(hj % 2 + 1) * 512]
                    ts1 = b.op("scalar", lambda e, o=tbs, i=ph1[:, 0:512]:
                               e.activation(o, i, AF.Silu),
                               waits=[tmm, bt.get(("tb1", hj % 2))], mark=True)
                    bt[bka] = ts1
                    th = b.op("vector", lambda e, o=tb2[:, hj * 512:(hj + 1) * 512],
                              i=ph3[:, 0:512], j=tbs:
                              e.scalar_tensor_tensor(o, i, 1.0, j, ALU.mult, ALU.mult),
                              waits=[ts1] + ([prev_phaseB] if hj == 0 else []), mark=True)
                    bt[bkb] = th
                    bt[("tb1", hj % 2)] = th
                    hstore_tok = b.op("vector", lambda e, o=tb2[:, hj * 512:(hj + 1) * 512],
                                      c2=CWb[:, tt * 512:(tt + 1) * 512]:
                                      e.tensor_tensor(o, o, c2, ALU.mult),
                                      waits=[cwb_tok], mark=True)
                for dc in range(C):
                    bk_o = dc % 4
                    reg = ctx.pb[bk_o][:, 0:512]
                    tout = None
                    for hj in range(hcn):
                        tout = b.op("tensor", lambda e, o=reg,
                                    l=w2sb[slot][:, hj * 1024 + dc * 128: hj * 1024 + (dc + 1) * 128],
                                    r=tb2[:, hj * 512:(hj + 1) * 512],
                                    st=(hj == 0), sp=(hj == hcn - 1):
                                    e.matmul(o, l, r, start=st, stop=sp),
                                    waits=([hstore_tok, bt.get(bk_o)] if hj == 0 else []),
                                    mark=(hj == hcn - 1))
                    tza = b.op("vector", lambda e, o=bv_(dc, tt * 512, (tt + 1) * 512),
                               i=reg: e.tensor_tensor(o, o, i, ALU.add),
                               waits=[tout], mark=True)
                    bt[bk_o] = tza
                prev_phaseB = b.barrier_token("tensor")
            slot_free[slot] = b.barrier_token("tensor")
            sidx += 1
        prev_cwb = b.barrier_token("tensor")
    if skip_out:
        return b.barrier_token("vector")
    dd = fm_relay_out(ctx, base, x1_rel, waits=[b.barrier_token("vector")])
    b.wait("sync", [dd])
    return dd


def build_l1attn():
    nc = _mk_nc()
    x1_rel = nc.dram_tensor("x1_rel", [C, 128, T], F32, kind="ExternalInput")
    x1b_rel = nc.dram_tensor("x1b_rel", [C, 128, T], F32, kind="ExternalOutput")
    xn1_rel = nc.dram_tensor("xn1_rel", [C, 128, T], BF16, kind="ExternalOutput")
    ctx = Ctx(nc)
    ctx.emit_consts()
    b = ctx.b
    xb = ctx.sb("xb", [128, C * T], F32)
    xn1 = ctx.sb("xn1", [128, C * T], BF16)
    t_x = fm_relay_in(ctx, xb, x1_rel, waits=[ctx._consts_done])
    _, t_ln = emit_ln(ctx, fm_view(xb), T, y16_c=fm_view(xn1), extra_wait=[t_x])
    d_xn = b.op("gpsimd", lambda e, o=xn1_rel.ap().rearrange("c p t -> p c t"),
                i=xn1.rearrange("p (c t) -> p c t", c=C): e.dma_start(out=o, in_=i),
                waits=[t_ln], dma=True)
    zlast = emit_l1attn(ctx, xb, xn1, t_ln)
    dd = fm_relay_out(ctx, xb, x1b_rel, waits=[zlast])
    b.wait("sync", [dd])
    b.wait("gpsimd", [d_xn])
    ctx.b.replay()
    return nc


def emit_l1attn(ctx, xb, xn1, t_ln):
    b = ctx.b
    bt = {}
    xnv = fm_view(xn1)
    Xr = ctx.sb("Xr", [128, C * 512], BF16)
    Xtm = ctx.sb("Xtm", [128, 4 * 1024], BF16)
    probs = ctx.sb("probs", [128, 4 * 512], BF16)
    pT = ctx.sb("pT", [128, 4 * 512], BF16)
    psc2 = ctx.sb("psc2", [128, 512], F32)
    scm = ctx.sb("scm", [128, 4], F32)
    zlast = None
    prev_gather = None
    for r in range(4):
        tg = None
        for c in range(C):
            tg = b.op("vector", lambda e, o=Xr[:, c * 512:(c + 1) * 512],
                      i=xn1[:, c * T + r: c * T + T: 4]: e.tensor_copy(o, i),
                      waits=([t_ln, prev_gather] if c == 0 else []), mark=(c == C - 1))
        prev_texp = None
        for qt in range(4):
            pq = ctx.pb[0 if qt % 2 == 0 else 1]
            bk = 0 if qt % 2 == 0 else 1
            mm_group(b, pq[:, 0:512],
                     [(Xr[:, kc * 512 + qt * 128: kc * 512 + (qt + 1) * 128],
                       Xr[:, kc * 512:(kc + 1) * 512]) for kc in range(C)],
                     waits=[tg, bt.get(bk)])
            tmm = b.barrier_token("tensor")
            tst = b.op("vector", lambda e, o=psc2, i=pq[:, 0:512]:
                       e.tensor_scalar(o, i, 0.03125, None, ALU.mult),
                       waits=[tmm, prev_texp], mark=True)
            bt[bk] = tst
            tmx = b.op("vector", lambda e, o=scm[:, 0:1], i=psc2:
                       e.reduce_max(o, i, axis=AX, negate=True), mark=True)
            texp = b.op("scalar", lambda e, o=probs[:, qt * 512:(qt + 1) * 512], i=psc2,
                        bi=scm[:, 0:1]: e.activation(o, i, AF.Exp, bias=bi),
                        waits=[tmx], mark=True)
            trs = b.op("vector", lambda e, o=scm[:, 1:2], i=probs[:, qt * 512:(qt + 1) * 512]:
                       e.reduce_sum(o, i, axis=AX), waits=[texp], mark=True)
            tiv = emit_inv(ctx, scm[:, 2:3], scm[:, 1:2], waits=[trs])
            prev_texp = b.op("vector", lambda e, o=probs[:, qt * 512:(qt + 1) * 512],
                             rc=scm[:, 2:3]: e.tensor_scalar(o, o, rc, None, ALU.mult),
                             waits=[tiv], mark=True)
        pnorm_done = prev_texp
        pt2 = ctx.pb[2].bitcast(BF16)
        for qt in range(4):
            for kt in range(4):
                ttr = b.op("tensor", lambda e, o=pt2[:, kt * 128:(kt + 1) * 128],
                           i=probs[:, qt * 512 + kt * 128: qt * 512 + (kt + 1) * 128]:
                           e.transpose(o, i, ctx.ident16),
                           waits=[pnorm_done, bt.get(2)] if kt == 0 else (), mark=(kt == 3))
            tpv = b.op("scalar", lambda e, o=pT[:, qt * 512:(qt + 1) * 512], i=pt2[:, 0:512]:
                       e.copy(o, i), waits=[ttr], mark=True)
            bt[2] = tpv
        pT_done = tpv
        pt3 = ctx.pb[3].bitcast(BF16)
        for kt in range(4):
            for c in range(C):
                ttx = b.op("tensor", lambda e, o=pt3[:, c * 128:(c + 1) * 128],
                           i=Xr[:, c * 512 + kt * 128: c * 512 + (kt + 1) * 128]:
                           e.transpose(o, i, ctx.ident16),
                           waits=[bt.get(3)] if c == 0 else (), mark=(c == C - 1))
            txv = b.op("scalar", lambda e, o=Xtm[:, kt * 1024:(kt + 1) * 1024], i=pt3[:, 0:1024]:
                       e.copy(o, i), waits=[ttx], mark=True)
            bt[3] = txv
        Xtm_done = txv
        for dc in range(C):
            bko = 4 + dc % 2
            pO = ctx.pb[bko]
            tmo = None
            for qt in range(4):
                for kt in range(4):
                    tmo = b.op("tensor", lambda e, o=pO[:, qt * 128:(qt + 1) * 128],
                               l=Xtm[:, kt * 1024 + dc * 128: kt * 1024 + (dc + 1) * 128],
                               r=pT[:, qt * 512 + kt * 128: qt * 512 + (kt + 1) * 128],
                               st=(kt == 0), sp=(kt == 3): e.matmul(o, l, r, start=st, stop=sp),
                               waits=([pT_done, Xtm_done, bt.get(bko)] if (qt == 0 and kt == 0) else ()),
                               mark=(qt == 3 and kt == 3))
            zlast = b.op("vector", lambda e, o=xb[:, dc * T + r: dc * T + T: 4], i=pO[:, 0:512]:
                         e.scalar_tensor_tensor(o, i, 0.25, o, ALU.mult, ALU.add),
                         waits=[tmo], mark=True)
            bt[bko] = zlast
        prev_gather = zlast
    return zlast


def build_l1ffn():
    nc = _mk_nc()
    x1b_rel = nc.dram_tensor("x1b_rel", [C, 128, T], F32, kind="ExternalInput")
    xn1_rel = nc.dram_tensor("xn1_rel", [C, 128, T], BF16, kind="ExternalInput")
    w1tp = nc.dram_tensor("w1tp", [32, 128, 1024], F32, kind="ExternalInput")
    w2tp = nc.dram_tensor("w2tp", [32, 128, 1024], F32, kind="ExternalInput")
    h1tp = nc.dram_tensor("h1tp", [32, 128, 4096], F32, kind="ExternalInput")
    h2tp = nc.dram_tensor("h2tp", [32, 128, 4096], F32, kind="ExternalInput")
    w3tp = nc.dram_tensor("w3tp", [8, 128, 4096], F32, kind="ExternalInput")
    l1s = nc.dram_tensor("l1s", [128, 136], F32, kind="ExternalInput")
    out = nc.dram_tensor("out", [T, D], F32, kind="ExternalOutput")
    ctx = Ctx(nc)
    ctx.emit_consts(alloc_ln=False)
    b = ctx.b
    emit_l1ffn(ctx, x1b_rel, xn1_rel, w1tp, w2tp, h1tp, h2tp, w3tp, l1s, out)
    ctx.b.replay()
    return nc


def emit_l1ffn(ctx, x1b_rel, xn1_rel, w1tp, w2tp, h1tp, h2tp, w3tp, l1s, out,
               start_wait=None):
    b = ctx.b
    bt = {}
    h_a = ctx.sb("h_a", [128, 32 * 1024], BF16)
    h_b = ctx.sb("h_b", [128, 32 * 1024], BF16)
    xnh = ctx.sb("xnh", [128, C * 1024], BF16)
    lsb = ctx.sb("lsb", [128, 136], F32)
    wsm = [ctx.sb(f"wsm{i}", [128, 8 * 128], BF16) for i in range(4)]
    wbg = [ctx.sb(f"wbg{i}", [128, 32 * 128], BF16) for i in range(2)]
    res_fm = ctx.sb("res_fm", [128, C * 512], F32)
    x1t = ctx.sb("x1t", [128, 2 * 512], F32)
    stg = ctx.sb("stg", [128, 1024], F32)
    tb1 = ctx.sb("l1tb1", [128, 512], BF16)
    tl = b.op("sync", lambda e: e.dma_start(out=lsb, in_=l1s.ap()), waits=[start_wait], dma=True)
    s1c = lambda j: lsb[:, j:j + 1]
    s2c = lambda j: lsb[:, 32 + j:33 + j]
    sh1c = lambda j: lsb[:, 64 + j:65 + j]
    sh2c = lambda j: lsb[:, 96 + j:97 + j]
    s3c = lambda dc: lsb[:, 128 + dc:129 + dc]
    wsm_free = [start_wait] * 4
    wbg_free = [start_wait] * 2
    wload = {}

    def load_small(which, j, slot):
        src = w1tp if which == 0 else w2tp
        t = b.op("gpsimd", lambda e, o=wsm[slot], i=src.ap()[j]:
                 e.dma_start(out=o, in_=i), waits=[wsm_free[slot]], dma=True)
        wload[("s", slot)] = t

    def load_big(src, j, slot):
        t = b.op("gpsimd", lambda e, o=wbg[slot], i=src.ap()[j]:
                 e.dma_start(out=o, in_=i), waits=[wbg_free[slot]], dma=True)
        wload[("b", slot)] = t

    last_store = None
    for th in range(2):
        txn = b.op("gpsimd", lambda e, o=xnh.rearrange("p (c t) -> p c t", c=C),
                   i=xn1_rel.ap()[:, :, th * 1024:(th + 1) * 1024].rearrange("c p t -> p c t"):
                   e.dma_start(out=o, in_=i), waits=[bt.get("xnh"), start_wait], dma=True)
        load_small(0, 0, 0)
        load_small(1, 0, 1)
        prev_silu = None
        tha = None
        for j in range(32):
            sl1 = 0 if j % 2 == 0 else 2
            sl2 = 1 if j % 2 == 0 else 3
            if j < 31:
                load_small(0, j + 1, 2 if j % 2 == 0 else 0)
                load_small(1, j + 1, 3 if j % 2 == 0 else 1)
            t1tok = wload[("s", sl1)]
            t2tok = wload[("s", sl2)]
            for tt2 in range(2):
                p0 = ctx.pb[tt2]
                p1 = ctx.pb[2 + tt2]
                mm_group(b, p0[:, 0:512],
                         [(wsm[sl1][:, kc * 128:(kc + 1) * 128],
                           xnh[:, kc * 1024 + tt2 * 512: kc * 1024 + (tt2 + 1) * 512])
                          for kc in range(C)],
                         waits=[txn, t1tok, tl, bt.get(tt2)])
                mm_group(b, p1[:, 0:512],
                         [(wsm[sl2][:, kc * 128:(kc + 1) * 128],
                           xnh[:, kc * 1024 + tt2 * 512: kc * 1024 + (tt2 + 1) * 512])
                          for kc in range(C)],
                         waits=[t2tok, bt.get(2 + tt2)])
                tmm = b.barrier_token("tensor")
                tsl = b.op("scalar", lambda e, o=tb1, i=p0[:, 0:512], sc=s1c(j):
                           e.activation(o, i, AF.Silu, scale=sc),
                           waits=[tmm, prev_silu], mark=True)
                bt[tt2] = tsl
                tha = b.op("vector", lambda e,
                           o=h_a[:, j * 1024 + tt2 * 512: j * 1024 + (tt2 + 1) * 512],
                           i=p1[:, 0:512], sc=s2c(j), j2=tb1:
                           e.scalar_tensor_tensor(o, i, sc, j2, ALU.mult, ALU.mult),
                           waits=[tsl], mark=True)
                bt[2 + tt2] = tha
                prev_silu = tha
            wsm_free[sl1] = b.barrier_token("tensor")
            wsm_free[sl2] = wsm_free[sl1]
        bt["xnh"] = b.barrier_token("tensor")
        s1_done = tha
        for stage in range(2):
            src = h1tp if stage == 0 else h2tp
            scf = sh1c if stage == 0 else sh2c
            hin = h_a if stage == 0 else h_b
            hout = h_b if stage == 0 else h_a
            sc_ctl = s1_done if stage == 0 else None
            load_big(src, 0, 0)
            prev_ev = None
            for j in range(32):
                slot = j % 2
                if j < 31:
                    load_big(src, j + 1, (j + 1) % 2)
                wtok = wload[("b", slot)]
                for tt2 in range(2):
                    pj = ctx.pb[4 + tt2]
                    mm_group(b, pj[:, 0:512],
                             [(wbg[slot][:, kc * 128:(kc + 1) * 128],
                               hin[:, kc * 1024 + tt2 * 512: kc * 1024 + (tt2 + 1) * 512])
                              for kc in range(32)],
                             waits=[wtok, sc_ctl, bt.get(4 + tt2)])
                    tmm = b.barrier_token("tensor")
                    prev_ev = b.op("scalar", lambda e,
                                   o=hout[:, j * 1024 + tt2 * 512: j * 1024 + (tt2 + 1) * 512],
                                   i=pj[:, 0:512], sc=scf(j):
                                   e.activation(o, i, AF.Gelu, scale=sc),
                                   waits=[tmm], mark=True)
                    bt[4 + tt2] = prev_ev
                wbg_free[slot] = b.barrier_token("tensor")
        s3_done = prev_ev
        for tt2 in range(2):
            tres = None
            for dc in range(C):
                slot = dc % 2
                load_big(w3tp, dc, slot)
                wtok = wload[("b", slot)]
                tx1 = b.op("sync", lambda e, o=x1t[:, slot * 512:(slot + 1) * 512],
                           i=x1b_rel.ap()[dc][:, th * 1024 + tt2 * 512: th * 1024 + (tt2 + 1) * 512]:
                           e.dma_start(out=o, in_=i), waits=[bt.get(("x1", slot))], dma=True)
                pj = ctx.pb[6 + dc % 2]
                mm_group(b, pj[:, 0:512],
                         [(wbg[slot][:, kc * 128:(kc + 1) * 128],
                           h_a[:, kc * 1024 + tt2 * 512: kc * 1024 + (tt2 + 1) * 512])
                          for kc in range(32)],
                         waits=[wtok, s3_done, bt.get(6 + dc % 2)])
                tmm = b.barrier_token("tensor")
                tres = b.op("vector", lambda e, o=res_fm[:, dc * 512:(dc + 1) * 512],
                            i=pj[:, 0:512], sc=s3c(dc), x1a=x1t[:, slot * 512:(slot + 1) * 512]:
                            e.scalar_tensor_tensor(o, i, sc, x1a, ALU.mult, ALU.add),
                            waits=[tmm, tx1], mark=True)
                bt[6 + dc % 2] = tres
                bt[("x1", slot)] = tres
                wbg_free[slot] = b.barrier_token("tensor")
            res_done = tres
            for it in range(4):
                tok0 = th * 1024 + tt2 * 512 + it * 128
                tev = None
                for c in range(C):
                    bk = c % 2
                    ttr = b.op("tensor", lambda e, o=ctx.pb[bk][:, 0:128],
                               i=res_fm[:, c * 512 + it * 128: c * 512 + (it + 1) * 128]:
                               e.transpose(o, i, ctx.ident32),
                               waits=[res_done, bt.get(("tr", bk))], mark=True)
                    tev = b.op("scalar", lambda e, o=stg[:, c * 128:(c + 1) * 128],
                               i=ctx.pb[bk][:, 0:128]: e.copy(o, i),
                               waits=[ttr, last_store if c < 2 else None], mark=True)
                    bt[("tr", bk)] = tev
                last_store = b.op("sync", lambda e, o=out.ap()[tok0:tok0 + 128, :], i=stg:
                                  e.dma_start(out=o, in_=i), waits=[tev], dma=True)
    b.wait("sync", [last_store])
    return last_store


def _vcol(v, n):
    return np.ascontiguousarray(np.asarray(v, np.float32).reshape(n, 128).T)


def _prep_host(inputs):
    H = {}
    x = np.ascontiguousarray(np.asarray(inputs["L0_attn_x"], np.float32))
    H["xs"] = x.reshape(8, 2048, 1024)
    def hkp(w):
        return np.ascontiguousarray(
            np.asarray(w, np.float32).reshape(8, 128, 8, 128)
            .transpose(2, 1, 0, 3).reshape(8, 128, 1024))

    for i in range(4):
        H[f"wq{i}"] = hkp(inputs["L0_attn_Wq"][i])
        H[f"wk{i}"] = hkp(inputs["L0_attn_Wk"][i])
        H[f"wv{i}"] = hkp(inputs["L0_attn_Wv"][i])
        H[f"wo{i}"] = np.ascontiguousarray(inputs["L0_attn_Wo"][i])
        H[f"w1p{i}"] = np.ascontiguousarray(inputs["L0_attn_W1"][i].reshape(8, 128, 4096))
        H[f"w2p{i}"] = np.ascontiguousarray(
            np.asarray(inputs["L0_attn_W2"][i], np.float32)
            .reshape(32, 128, 1024).transpose(1, 0, 2))
        H[f"lv{i}"] = np.concatenate(
            [_vcol(inputs["L0_attn_bo"][i], 8), _vcol(inputs["L0_attn_ln1_g"][i], 8),
             _vcol(inputs["L0_attn_ln1_b"][i], 8), _vcol(inputs["L0_attn_b1"][i], 32),
             _vcol(inputs["L0_attn_b2"][i], 8), _vcol(inputs["L0_attn_ln2_g"][i], 8),
             _vcol(inputs["L0_attn_ln2_b"][i], 8)], axis=1).astype(np.float32)
    ew1 = np.concatenate([inputs["L0_ffn_w1"], inputs["L0_ffn_new_w1"]], 0)
    ew2 = np.concatenate([inputs["L0_ffn_w2"], inputs["L0_ffn_new_w2"]], 0)
    ew3 = np.concatenate([inputs["L0_ffn_w3"], inputs["L0_ffn_new_w3"]], 0)
    H["w1t"] = np.ascontiguousarray(
        np.asarray(ew1, np.float32).transpose(0, 2, 1).reshape(9, 8, 128, 22, 128)
        .transpose(0, 2, 3, 1, 4).reshape(9, 128, 22, 1024))
    H["w3t"] = np.ascontiguousarray(
        np.asarray(ew3, np.float32).transpose(0, 2, 1).reshape(9, 8, 128, 22, 128)
        .transpose(0, 2, 3, 1, 4).reshape(9, 128, 22, 1024))
    H["w2t"] = np.ascontiguousarray(
        np.asarray(ew2, np.float32).transpose(0, 2, 1).reshape(9, 22, 128, 1024)
        .transpose(0, 2, 1, 3))
    gate = np.concatenate([inputs["L0_ffn_gate_weight"], inputs["L0_ffn_new_gate_weight"]], 0)
    H["gt"] = np.ascontiguousarray(
        np.asarray(gate, np.float32).T.reshape(8, 128, 9).transpose(1, 0, 2).reshape(128, 72))
    gb = np.zeros((9, 1), np.float32)
    gb[8, 0] = np.asarray(inputs["L0_ffn_gate_bias"], np.float32)[0]
    H["gb"] = gb

    def tp(w, nj):
        wt = np.asarray(w, np.float32).T
        nin = wt.shape[0]
        return np.ascontiguousarray(
            wt.reshape(nin // 128, 128, nj, 128).transpose(2, 1, 0, 3)
            .reshape(nj, 128, nin))

    H["w1tp"] = tp(inputs["L1_ffn_w1_w"], 32)
    H["w2tp"] = tp(inputs["L1_ffn_w2_w"], 32)
    H["h1tp"] = tp(inputs["L1_ffn_h1_w"], 32)
    H["h2tp"] = tp(inputs["L1_ffn_h2_w"], 32)
    H["w3tp"] = tp(inputs["L1_ffn_w3_w"], 8)
    H["l1s"] = np.concatenate(
        [_vcol(inputs["L1_ffn_w1_s"], 32), _vcol(inputs["L1_ffn_w2_s"], 32),
         _vcol(inputs["L1_ffn_h1_s"], 32), _vcol(inputs["L1_ffn_h2_s"], 32),
         _vcol(inputs["L1_ffn_w3_s"], 8)], axis=1).astype(np.float32)
    ii = np.arange(128)[:, None]
    jj = np.arange(256)[None, :]
    mask_std = np.where(jj - 128 <= ii, 0.0, NEG).astype(np.float32)
    mask_first = mask_std.copy()
    mask_first[:, 0:128] = NEG
    H["mask_std"] = mask_std
    H["mask_first"] = mask_first
    H["mask_std2"] = np.ascontiguousarray(np.hstack([mask_std, mask_std]))
    H["mask_first2"] = np.ascontiguousarray(np.hstack([mask_first, mask_std]))
    return H


def kernel_multilaunch(**inputs):
    H = _prep_host(inputs)
    cores = list(range(N_CORES))

    ncP = build_pre()
    res = run_bass_kernel_spmd(ncP, [dict(xs=H["xs"][c]) for c in cores], core_ids=cores)
    z = [res.results[c]["z"] for c in cores]
    x0 = [res.results[c]["x0"] for c in cores]
    z0 = z

    ncF = build_finner()
    ncM = build_minner()
    for i in range(4):
        baseF = dict(wq=H[f"wq{i}"], wk=H[f"wk{i}"], wv=H[f"wv{i}"], lvd=H[f"lv{i}"])
        resF = run_bass_kernel_spmd(
            ncF, [dict(z_in=z[c], **baseF) for c in cores], core_ids=cores)
        tots = [resF.results[c]["tot_out"] for c in cores]
        baseM = dict(wq=H[f"wq{i}"], wk=H[f"wk{i}"], wv=H[f"wv{i}"], wo=H[f"wo{i}"],
                     w1p=H[f"w1p{i}"], w2p=H[f"w2p{i}"], lvd=H[f"lv{i}"])
        maps = []
        for c in cores:
            partner = c - 1 if c % 2 == 1 else c
            maps.append(dict(
                z_in=z[c],
                halo_in=np.ascontiguousarray(z[partner][:, :, 1920:2048]),
                tot_in=tots[partner],
                muse_in=np.full((128, 1), 1.0 if c % 2 else 0.0, np.float32),
                maskw0_in=H["mask_first"] if c % 2 == 0 else H["mask_std"],
                **baseM))
        resM = run_bass_kernel_spmd(ncM, maps, core_ids=cores)
        z = [resM.results[c]["z_out"] for c in cores]

    ncMOE = build_moe()
    maps = [dict(z0_rel=z0[c], x0_rel=x0[c], z4_rel=z[c], w1t=H["w1t"], w3t=H["w3t"],
                 w2t=H["w2t"], gt=H["gt"], gb=H["gb"]) for c in cores]
    resMOE = run_bass_kernel_spmd(ncMOE, maps, core_ids=cores)
    x1 = [resMOE.results[c]["x1_rel"] for c in cores]

    ncA = build_l1attn()
    resA = run_bass_kernel_spmd(ncA, [dict(x1_rel=x1[c]) for c in cores], core_ids=cores)
    x1b = [resA.results[c]["x1b_rel"] for c in cores]
    xn1 = [resA.results[c]["xn1_rel"] for c in cores]

    ncFF = build_l1ffn()
    maps = [dict(x1b_rel=x1b[c], xn1_rel=xn1[c], w1tp=H["w1tp"], w2tp=H["w2tp"],
                 h1tp=H["h1tp"], h2tp=H["h2tp"], w3tp=H["w3tp"], l1s=H["l1s"])
            for c in cores]
    resFF = run_bass_kernel_spmd(ncFF, maps, core_ids=cores)
    out = np.stack([resFF.results[c]["out"] for c in cores], 0)
    return np.ascontiguousarray(out.reshape(4, 4096, 1024))


PAIR_GROUPS = [[0, 1], [2, 3], [4, 5], [6, 7]]


def build_fused():
    nc = _mk_nc()
    xs = nc.dram_tensor("xs", [T, D], F32, kind="ExternalInput")
    layers = []
    for i in range(4):
        d = NS()
        d.wq = nc.dram_tensor(f"wq{i}", [8, 128, 1024], F32, kind="ExternalInput")
        d.wk = nc.dram_tensor(f"wk{i}", [8, 128, 1024], F32, kind="ExternalInput")
        d.wv = nc.dram_tensor(f"wv{i}", [8, 128, 1024], F32, kind="ExternalInput")
        d.wo = nc.dram_tensor(f"wo{i}", [1024, 1024], F32, kind="ExternalInput")
        d.w1p = nc.dram_tensor(f"w1p{i}", [8, 128, 4096], F32, kind="ExternalInput")
        d.w2p = nc.dram_tensor(f"w2p{i}", [128, 32, 1024], F32, kind="ExternalInput")
        d.lvd = nc.dram_tensor(f"lv{i}", [128, 80], F32, kind="ExternalInput")
        layers.append(d)
    w1t = nc.dram_tensor("w1t", [9, 128, 22, 1024], F32, kind="ExternalInput")
    w3t = nc.dram_tensor("w3t", [9, 128, 22, 1024], F32, kind="ExternalInput")
    w2t = nc.dram_tensor("w2t", [9, 128, 22, 1024], F32, kind="ExternalInput")
    gt = nc.dram_tensor("gt", [128, 72], F32, kind="ExternalInput")
    gb = nc.dram_tensor("gb", [9, 1], F32, kind="ExternalInput")
    w1tp = nc.dram_tensor("w1tp", [32, 128, 1024], F32, kind="ExternalInput")
    w2tp = nc.dram_tensor("w2tp", [32, 128, 1024], F32, kind="ExternalInput")
    h1tp = nc.dram_tensor("h1tp", [32, 128, 4096], F32, kind="ExternalInput")
    h2tp = nc.dram_tensor("h2tp", [32, 128, 4096], F32, kind="ExternalInput")
    w3tp = nc.dram_tensor("w3tp", [8, 128, 4096], F32, kind="ExternalInput")
    l1s = nc.dram_tensor("l1s", [128, 136], F32, kind="ExternalInput")
    muse_in = nc.dram_tensor("muse_in", [128, 1], F32, kind="ExternalInput")
    maskw0_in = nc.dram_tensor("maskw0_in", [128, 512], F32, kind="ExternalInput")
    out = nc.dram_tensor("out", [T, D], F32, kind="ExternalOutput")
    x0_sp = nc.dram_tensor("x0_sp", [C, 128, T], F32, kind="Internal")
    z0_sp = nc.dram_tensor("z0_sp", [C, 128, T], F32, kind="Internal")
    x1_sp = nc.dram_tensor("x1_sp", [C, 128, T], F32, kind="Internal")
    xn1_sp = nc.dram_tensor("xn1_sp", [C, 128, T], BF16, kind="Internal")
    halo_snd = nc.dram_tensor("halo_snd", [128, 1024], F32, kind="Internal")
    halo_rcv = nc.dram_tensor("halo_rcv", [256, 1024], F32, kind="Internal")
    tot_snd = nc.dram_tensor("tot_snd", [128, 516], F32, kind="Internal")
    tot_rcv = nc.dram_tensor("tot_rcv", [256, 516], F32, kind="Internal")

    ctx = Ctx(nc)
    ctx.emit_consts(alloc_lnf32=False)
    b = ctx.b
    z = ctx.sb("z", [128, C * T], F32)
    xn = ctx.sb("xn", [128, C * T], BF16)
    arena = ctx.sb("arena", [128, ARENA_COLS], BF16)
    P = alloc_inner_P(arena)
    P.z, P.xn = z, xn
    ctx.ln_f32 = P.sc1

    P.mask2a = ctx.sb("mask2a", [128, 512], F32)
    t1 = b.op("sync", lambda e: e.dma_start(out=P.mask2a, in_=maskw0_in.ap()), dma=True)
    P.mask_toks = [t1]
    P.muse = ctx.sb("muse", [128, 1], F32)
    P.muse_tok = b.op("sync", lambda e: e.dma_start(out=P.muse, in_=muse_in.ap()), dma=True)
    P.lv = ctx.sb("lv", [128, 80], F32)

    ctx.begin_overlay(arena)
    x0buf = ctx.sb("x0f", [128, C * T], F32)
    ctx.tm_stage = ctx.sb("tm_stage", [128, 2048], F32)
    ctx.end_overlay()
    tload = emit_tm_to_fm(ctx, xs.ap(), fm_view(x0buf), T)
    t32, _ = emit_ln(ctx, fm_view(x0buf), T, y32_c=fm_view(z), extra_wait=[tload])
    d_x0 = fm_relay_out(ctx, x0buf, x0_sp, waits=[tload])
    d_z0 = fm_relay_out(ctx, z, z0_sp, waits=[t32])

    prev_z_tok = t32
    prev_collA = None
    prev_hsb = None
    lv_free = None
    for i in range(4):
        d = layers[i]
        P.wq, P.wk, P.wv, P.wo, P.w1p, P.w2p = d.wq, d.wk, d.wv, d.wo, d.w1p, d.w2p
        P.lv_tok = b.op("sync", lambda e, dd=d: e.dma_start(out=P.lv, in_=dd.lvd.ap()),
                        waits=[lv_free], dma=True)
        extra_a = [d_x0, d_z0] if i == 0 else []
        tsnd = b.op("sync", lambda e,
                    o=halo_snd.ap().rearrange("p (c t) -> p c t", c=C),
                    i=z.rearrange("p (c t) -> p c t", c=C)[:, :, 1920:2048]:
                    e.dma_start(out=o, in_=i),
                    waits=[prev_z_tok, prev_collA] + extra_a, dma=True)
        tca = b.op("gpsimd", lambda e: e.collective_compute(
            "AllGather", ALU.bypass, replica_groups=PAIR_GROUPS,
            ins=[halo_snd.ap().opt()], outs=[halo_rcv.ap().opt()]),
            waits=[tsnd, prev_hsb], coll=True)
        prev_collA = tca
        thsb = b.op("sync", lambda e, o=P.halo_sb, i=halo_rcv.ap()[0:128, :]:
                    e.dma_start(out=o, in_=i), waits=[tca], dma=True)
        prev_hsb = thsb
        P.halo_tok = thsb
        P.z_wait = prev_z_tok

        cb_state = {}

        def send_cb(tok, cb_state=cb_state, i=i):
            t_s = b.op("sync", lambda e, o=tot_snd.ap().rearrange("p (h r) -> p h r", h=4),
                       i=P.totsend_sb.rearrange("p (h r) -> p h r", h=4):
                       e.dma_start(out=o, in_=i),
                       waits=[tok, cb_state.get("prev_cb")], dma=True)
            t_c = b.op("gpsimd", lambda e: e.collective_compute(
                "AllGather", ALU.bypass, replica_groups=PAIR_GROUPS,
                ins=[tot_snd.ap().opt()], outs=[tot_rcv.ap().opt()]),
                waits=[t_s, cb_state.get("prev_tsb")], coll=True)
            t_r = b.op("sync", lambda e, o=P.tot_sb, i=tot_rcv.ap()[0:128, :]:
                       e.dma_start(out=o, in_=i), waits=[t_c], dma=True)
            P.tot_tok = t_r
            cb_state["prev_cb"] = t_c
            cb_state["prev_tsb"] = t_r
            return t_r

        P.send_cb = send_cb
        zffn = emit_inner_attn_common(ctx, P)
        prev_z_tok = zffn
        lv_free = zffn

    ctx.begin_overlay(arena)
    moe_done = emit_moe(ctx, z, xn, z0_sp, x0_sp, None, w1t, w3t, w2t, gt, gb, None,
                        base_tok=prev_z_tok, skip_out=True)
    ctx.end_overlay()

    ctx.begin_overlay(arena)
    _, t_ln1 = emit_ln(ctx, fm_view(z), T, y16_c=fm_view(xn), extra_wait=[moe_done])
    d_xn1 = b.op("gpsimd", lambda e, o=xn1_sp.ap().rearrange("c p t -> p c t"),
                 i=xn.rearrange("p (c t) -> p c t", c=C): e.dma_start(out=o, in_=i),
                 waits=[t_ln1], dma=True)
    zl1 = emit_l1attn(ctx, z, xn, t_ln1)
    ctx.end_overlay()

    d_x1 = fm_relay_out(ctx, z, x1_sp, waits=[zl1])
    vm = {
        "h_a": z.bitcast(BF16),
        "xnh": xn[:, 0:8192],
        "x1t": xn[:, 8192:10240].bitcast(F32),
        "stg": xn[:, 10240:12288].bitcast(F32),
        "wsm0": xn[:, 12288:13312],
        "wsm1": xn[:, 13312:14336],
        "wsm2": xn[:, 14336:15360],
        "wsm3": xn[:, 15360:16384],
    }
    ctx.begin_overlay(arena, view_map=vm)
    last = emit_l1ffn(ctx, x1_sp, xn1_sp, w1tp, w2tp, h1tp, h2tp, w3tp, l1s, out,
                      start_wait=(("sync", "d"), b.marks[("sync", "d")]))
    ctx.end_overlay()
    b.wait("sync", [last, d_xn1])
    b.replay()
    return nc


def kernel(**inputs):
    H = _prep_host(inputs)
    cores = list(range(N_CORES))
    nc = build_fused()
    maps = []
    for c in cores:
        m = dict(xs=H["xs"][c],
                 muse_in=np.full((128, 1), 1.0 if c % 2 else 0.0, np.float32),
                 maskw0_in=H["mask_first2"] if c % 2 == 0 else H["mask_std2"],
                 w1t=H["w1t"], w3t=H["w3t"], w2t=H["w2t"], gt=H["gt"], gb=H["gb"],
                 w1tp=H["w1tp"], w2tp=H["w2tp"], h1tp=H["h1tp"], h2tp=H["h2tp"],
                 w3tp=H["w3tp"], l1s=H["l1s"])
        for i in range(4):
            m[f"wq{i}"] = H[f"wq{i}"]
            m[f"wk{i}"] = H[f"wk{i}"]
            m[f"wv{i}"] = H[f"wv{i}"]
            m[f"wo{i}"] = H[f"wo{i}"]
            m[f"w1p{i}"] = H[f"w1p{i}"]
            m[f"w2p{i}"] = H[f"w2p{i}"]
            m[f"lv{i}"] = H[f"lv{i}"]
        maps.append(m)
    res = run_bass_kernel_spmd(nc, maps, core_ids=cores)
    outp = np.stack([res.results[c]["out"] for c in cores], 0)
    return np.ascontiguousarray(outp.reshape(4, 4096, 1024))
